# version 03
# baseline (speedup 1.0000x reference)
import math
import numpy as np
from contextlib import ExitStack

import concourse.bass as bass
import concourse.mybir as mybir
from concourse.bass_utils import run_bass_kernel_spmd
from concourse.bass_core_rust import IndirectOffsetOnAxis

F32 = mybir.dt.float32
I32 = mybir.dt.int32
AF = mybir.ActivationFunctionType
OP = mybir.AluOpType
AX = mybir.AxisListType

B, T, D = 4, 2048, 2048
TC = 1024
NT = TC // 128
NDC = D // 128
FFN = 8192
SEL = 1024
SELC = SEL // 2
LN2 = 0.6931471805599453

SUBLAYERS = ["A", "B", "A2", "C", "D", "C2"]
PAIR_GROUPS = [[0, 1], [2, 3], [4, 5], [6, 7]]


class Ctr:
    def __init__(self, handle):
        self.h = handle
        self.v = 0

    def bump(self, n=1):
        self.v += n
        return self.v


class Prog:
    ENGINES = ("sync", "tensor", "vector", "scalar", "gpsimd")

    def __init__(self, nc):
        self.nc = nc
        self.q = {e: [] for e in self.ENGINES}

    def add(self, eng, fn):
        self.q[eng].append(fn)

    def wait(self, eng, ctr, val=None):
        v = ctr.v if val is None else val
        h = ctr.h
        self.add(eng, lambda e, h=h, v=v: e.wait_ge(h, v))

    def run(self, block):
        for eng in self.ENGINES:
            items = self.q[eng]
            if not items:
                continue
            def body(e, items=items):
                for fn in items:
                    fn(e)
            getattr(block, eng)(body)


class Stream:
    def __init__(self, prog, ctr):
        self.prog = prog
        self.ctr = ctr
        self.gate = None
        self.dirty = False

    def group(self, eng, pairs, indirect=None):
        if self.gate is not None:
            self.prog.wait(eng, self.gate[0], self.gate[1])
        elif self.dirty:
            self.prog.wait(eng, self.ctr, self.ctr.v)
        self.gate = None
        h = self.ctr.h
        for out_ap, in_ap in pairs:
            def fn(e, o=out_ap, i=in_ap, h=h):
                e.dma_start(out=o, in_=i).then_inc(h, 16)
            self.prog.add(eng, fn)
            self.ctr.v += 16
        if indirect:
            for out_ap, out_off, in_ap, in_off, cop in indirect:
                def fn(e, o=out_ap, oo=out_off, i=in_ap, io=in_off, c=cop, h=h):
                    e.indirect_dma_start(o, oo, i, io, compute_op=c).then_inc(h, 16)
                self.prog.add(eng, fn)
                self.ctr.v += 16
        self.dirty = True
        return self.ctr.v

    def consumed(self, ctr, val):
        self.gate = (ctr, val)
        self.dirty = False


class Rot:
    def __init__(self, prog, ctrs):
        self.prog = prog
        self.ctrs = ctrs
        self.n = len(ctrs)
        self.hist = {}
        self.i = 0

    def load(self, eng, pairs):
        slot = self.i % self.n
        c = self.ctrs[slot]
        g = self.hist.get(slot)
        if g is not None:
            self.prog.wait(eng, g[0], g[1])
        elif c.v > 0:
            self.prog.wait(eng, c, c.v)
        for out_ap, in_ap in pairs:
            def fn(e, o=out_ap, i=in_ap, h=c.h):
                e.dma_start(out=o, in_=i).then_inc(h, 16)
            self.prog.add(eng, fn)
            c.v += 16
        self.i += 1
        return (c, c.v)

    def consumed(self, ctr, val, back=0):
        slot = (self.i - 1 - back) % self.n
        self.hist[slot] = (ctr, val)

    def reset(self, ctr, val):
        for sl in range(self.n):
            self.hist[sl] = (ctr, val)


def pub(prog, s, eng):
    key = {"vector": "svec", "scalar": "ssc"}[eng]
    c = s[key]
    prog.add(eng, lambda e, h=c.h: e.drain().then_inc(h, 1))
    return c.bump()


def tensor_group(prog, s, fns):
    assert fns
    n = len(fns)
    for i, fn in enumerate(fns):
        if i == n - 1:
            def wrap(e, fn=fn, h=s["smm"].h):
                fn(e).then_inc(h, 1)
            prog.add("tensor", wrap)
        else:
            prog.add("tensor", fn)
    return s["smm"].bump()


def build_nc(sublayers=None):
    if sublayers is None:
        sublayers = SUBLAYERS
    nc = bass.Bass("TRN2", target_bir_lowering=False, debug=False)
    st = ExitStack()

    din = {}
    def I(name, shape, dt=F32):
        din[name] = nc.dram_tensor(name, list(shape), dt, kind="ExternalInput").ap()
        return din[name]

    I("x_in", (TC, D))
    I("ident", (128, 128))
    if "C" in sublayers or "C2" in sublayers:
        I("c_weight", (D, D))
        I("c_bias_rep", (128, D))
    if "A" in sublayers or "A2" in sublayers:
        I("a_wq", (D, D)); I("a_wk", (D, D)); I("a_wv", (D, D)); I("a_wo", (D, D))
        I("a_w1", (D, FFN)); I("a_w3", (D, FFN)); I("a_w2", (FFN, D))
        I("a_g1_rep", (128, D)); I("a_g2_rep", (128, D)); I("a_wrt_rep", (128, D))
        I("a_mask_my", (SELC, SEL))
        I("a_ropeA", (128, 512)); I("a_ropeB", (128, 512))
        I("a_iota", (128, T))
        I("a_idxcol", (128, 16))
        I("a_rcol", (128, 4))
        I("a_ones1", (1, 128))
        I("a_halfflag", (128, 1))
    if "B" in sublayers:
        I("b_Wq_my", (D, 1024))
        I("b_Wk_my", (D, 256)); I("b_Wv_my", (D, 256))
        I("b_Wo", (D, D))
        I("b_ropeA", (128, NQT_G * 64)) if False else None
        I("b_ropeA", (128, 1024))
        I("b_ropeB", (128, 1024))
        I("b_causal", (128, 128))
    if "D" in sublayers:
        I("d_Wq", (D, D)); I("d_Wk", (D, D)); I("d_Wv", (D, D)); I("d_Wo", (D, D))
        I("d_Wg1", (D, 128)); I("d_Wg2", (128, D))
        I("d_normw_rep", (128, D))
        I("d_ld_rep", (128, 16))
        I("d_diffTm", (128, 128))
        I("d_maskT", (128, 128))
        I("d_kdc", (128, 1))
        I("d_qdr", (128, 128))
        I("d_flag", (128, 1))
        I("d_srow", (128, 1), I32)

    y = nc.dram_tensor("y", [TC, D], F32, kind="ExternalOutput").ap()

    dint = {}
    def Dram(name, shape, dt=F32, **kw):
        dint[name] = nc.dram_tensor(name, list(shape), dt, kind="Internal", **kw).ap()
        return dint[name]

    if "C" in sublayers or "C2" in sublayers:
        Dram("wqdT", (D, D))
    if "B" in sublayers or "A" in sublayers or "A2" in sublayers:
        Dram("lnmy", (TC, D))
        Dram("ln_ag", (2 * TC, D), addr_space="Local")
    if "A" in sublayers or "A2" in sublayers:
        Dram("lgmy", (128, 8))
        Dram("lg_ag", (256, 8), addr_space="Local")
        Dram("kmy", (128, 16 * 512))
        Dram("k_ag", (256, 16 * 512), addr_space="Local")
        Dram("vmy", (SELC, D))
        Dram("v_ag", (SEL, D), addr_space="Local")
        Dram("xwmy", (SELC, D))
        Dram("xw_pad", (SEL + 1, D))
    if "D" in sublayers:
        Dram("kvpark", (32 * 128, TC))
        Dram("s_my", (128, 16 * 128))
        Dram("s_ag", (256, 16 * 128), addr_space="Local")

    sb = {}
    def S(name, shape, dt=F32):
        sb[name] = st.enter_context(nc.sbuf_tensor(name, list(shape), dt))
        return sb[name]

    x_sb = S("x", (128, NT * D))
    g0 = S("g0", (128, 8192))
    g1 = S("g1", (128, 8192))
    g2 = S("g2", (128, 8192))
    g3 = S("g3", (128, 4096))
    g4 = S("g4", (128, 4096))
    wbuf = S("wbuf", (128, 2048))
    sm = S("sm", (128, 1024), F32)
    smi = S("smi", (128, 16), I32)
    ident_sb = S("ident_sb", (128, 128))

    ps0 = st.enter_context(nc.psum_tensor("ps0", [128, 2048], F32))
    ps1 = st.enter_context(nc.psum_tensor("ps1", [128, 2048], F32))

    sems = {}
    for name in ("smm", "svec", "ssc", "sout", "stW", "stG", "stX"):
        sems[name] = Ctr(st.enter_context(nc.semaphore(name)))
    rsems = [Ctr(st.enter_context(nc.semaphore(f"rot{i}"))) for i in range(8)]

    prog = Prog(nc)
    stW = Stream(prog, sems["stW"])
    stG = Stream(prog, sems["stG"])
    stX = Stream(prog, sems["stX"])

    coll_sems = []
    def new_coll_sem():
        c = Ctr(st.enter_context(nc.semaphore(f"coll{len(coll_sems)}")))
        coll_sems.append(c)
        return c

    rotA = Rot(prog, rsems[0:4])
    rotB = Rot(prog, rsems[4:6])
    rotC = Rot(prog, rsems[6:8])
    ctx = dict(nc=nc, prog=prog, din=din, dint=dint, sb=sb, y=y,
               rotA=rotA, rotB=rotB, rotC=rotC,
               x_sb=x_sb, g0=g0, g1=g1, g2=g2, g3=g3, g4=g4, wbuf=wbuf, sm=sm,
               smi=smi,
               ident=ident_sb, ps0=ps0, ps1=ps1, sems=sems,
               stW=stW, stG=stG, stX=stX, new_coll_sem=new_coll_sem)

    emit_prelude(ctx)
    first_c = True
    for sl in sublayers:
        emit_barrier(ctx)
        if sl in ("C", "C2"):
            emit_C(ctx, first=first_c)
            first_c = False
        elif sl in ("A", "A2"):
            emit_A(ctx)
        elif sl == "B":
            emit_B(ctx)
        elif sl == "D":
            emit_D(ctx)
    emit_barrier(ctx)
    emit_finale(ctx)

    with nc.Block() as block:
        prog.run(block)
    st.close()
    return nc



def emit_barrier(ctx):
    prog, s = ctx["prog"], ctx["sems"]
    stW, stG = ctx["stW"], ctx["stG"]
    for eng in ("sync", "tensor", "vector", "scalar", "gpsimd"):
        for c in (s["smm"], s["svec"], s["ssc"], stW.ctr, stG.ctr):
            if c.v > 0:
                prog.wait(eng, c, c.v)


def emit_prelude(ctx):
    prog, din, s = ctx["prog"], ctx["din"], ctx["sems"]
    stX = ctx["stX"]
    lv = stX.group("sync", [
        (ctx["x_sb"][:].rearrange("p (t f) -> p t f", t=NT),
         din["x_in"].rearrange("(t p) f -> p t f", p=128)),
        (ctx["ident"][:], din["ident"]),
    ])
    ctx["xs_ready"] = lv
    for eng in ("tensor", "vector", "scalar", "gpsimd"):
        prog.wait(eng, stX.ctr, lv)


def emit_finale(ctx):
    prog, s, y = ctx["prog"], ctx["sems"], ctx["y"]
    prog.wait("gpsimd", s["svec"])
    prog.wait("gpsimd", s["ssc"])
    prog.wait("gpsimd", s["smm"])
    h = s["sout"].h
    def fn(e, h=h):
        e.dma_start(out=y.rearrange("(t p) f -> p t f", p=128),
                    in_=ctx["x_sb"][:].rearrange("p (t f) -> p t f", t=NT)).then_inc(h, 16)
    prog.add("gpsimd", fn)
    s["sout"].bump(16)
    prog.wait("gpsimd", s["sout"])


def ln_tile(ctx, src_ap, dst_ap, sq_ap, col4, eps=1e-5, gain_rep=None, rms=False,
            save_musig=None):
    prog, s = ctx["prog"], ctx["sems"]
    c_sum = col4[:, 0:1]; c_ssq = col4[:, 1:2]; c_rstd = col4[:, 2:3]; c_b = col4[:, 3:4]
    inv_n = 1.0 / D

    prog.wait("scalar", s["svec"], s["svec"].v)
    def f_sc(e, src=src_ap, sq=sq_ap, cs=c_ssq):
        e.activation(sq, src, AF.Square, accum_out=cs)
    prog.add("scalar", f_sc)
    sc_done = pub(prog, s, "scalar")

    if not rms:
        prog.add("vector", lambda e, src=src_ap, c=c_sum: e.reduce_sum(c, src, axis=AX.X))
        prog.add("vector", lambda e: e.drain())
    prog.wait("vector", s["ssc"], sc_done)
    if rms:
        def f_v1(e, ci=c_ssq, co=c_rstd):
            e.tensor_scalar(co, ci, inv_n, eps, OP.mult, OP.add)
        prog.add("vector", f_v1)
    else:
        def f_v1(e, cs=c_sum, co=c_rstd, cb=c_b):
            e.tensor_scalar_mul(cb, cs, inv_n)
            e.drain()
            e.tensor_scalar(co, cb, cb, None, OP.mult, OP.bypass)
        prog.add("vector", f_v1)
        def f_v2(e, cq=c_ssq, co=c_rstd):
            e.drain()
            e.scalar_tensor_tensor(co, cq, inv_n, co, OP.mult, OP.subtract)
            e.drain()
            e.tensor_scalar_add(co, co, eps)
        prog.add("vector", f_v2)
    v_done = pub(prog, s, "vector")

    prog.wait("scalar", s["svec"], v_done)
    def f_sc2(e, ci=c_rstd):
        e.activation(ci, ci, AF.Ln)
        e.drain()
        e.activation(ci, ci, AF.Exp, scale=-0.5)
    prog.add("scalar", f_sc2)
    if save_musig is not None:
        mu_ap, rs_ap = save_musig
        def f_sc2b(e, mu=mu_ap, rs=rs_ap, cb=c_b, cr=c_rstd):
            e.drain()
            e.copy(mu, cb)
            e.copy(rs, cr)
        prog.add("scalar", f_sc2b)
    sc2_done = pub(prog, s, "scalar")

    prog.wait("vector", s["ssc"], sc2_done)
    if rms:
        if gain_rep is not None:
            def f_v3(e, src=src_ap, dst=dst_ap, cr=c_rstd, g=gain_rep):
                e.scalar_tensor_tensor(dst, src, cr, g, OP.mult, OP.mult)
            prog.add("vector", f_v3)
        else:
            def f_v3(e, src=src_ap, dst=dst_ap, cr=c_rstd):
                e.tensor_scalar_mul(dst, src, cr)
            prog.add("vector", f_v3)
    else:
        def f_v3(e, src=src_ap, dst=dst_ap, cr=c_rstd, cb=c_b):
            e.tensor_scalar(cb, cb, cr, -1.0, OP.mult, OP.mult)
            e.drain()
            e.tensor_scalar(dst, src, cr, cb, OP.mult, OP.add)
        prog.add("vector", f_v3)
    return pub(prog, s, "vector")


def emit_quant_dequant(ctx, src_ap, dst_ap, scr):
    prog, s = ctx["prog"], ctx["sems"]
    NB = D // 32
    f0 = scr["f0"]; b0 = scr["b0"]; b1 = scr["b1"]; b2 = scr["b2"]
    m2 = scr["m2"]; m4 = scr["m4"]; vbuf = scr["v"]
    src3 = src_ap.rearrange("p (nb k) -> p nb k", k=32)
    dst3 = dst_ap.rearrange("p (nb k) -> p nb k", k=32)
    f03 = f0.rearrange("p (nb k) -> p nb k", k=32)

    prog.add("vector", lambda e, i=src3, o=b0: e.reduce_max(o, i, axis=AX.X, apply_absolute_value=True))
    def f_exp(e, a=b0, ee=b2, m=b1):
        e.drain()
        e.tensor_scalar_max(a, a, 2.0**-126)
        e.drain()
        e.tensor_single_scalar(m, a, 1.0, OP.is_lt)
        e.drain()
        e.tensor_scalar(ee, m, -126.0, None, OP.mult, OP.bypass)
        e.tensor_scalar(m, m, 2.0**126 - 1.0, 1.0, OP.mult, OP.add)
        e.drain()
        e.tensor_mul(a, a, m)
        e.drain()
        for k in (64, 32, 16, 8, 4, 2, 1):
            e.tensor_single_scalar(m, a, float(2.0**k), OP.is_ge)
            e.drain()
            e.scalar_tensor_tensor(ee, m, float(k), ee, OP.mult, OP.add)
            kk = min(k, 16)
            e.tensor_scalar(m, m, float(2.0**-kk) - 1.0, 1.0, OP.mult, OP.add)
            e.drain()
            for _ in range(k // kk):
                e.tensor_mul(a, a, m)
                e.drain()
        e.tensor_scalar_add(ee, ee, -2.0)
    prog.add("vector", f_exp)
    v2 = pub(prog, s, "vector")

    prog.wait("scalar", s["svec"], v2)
    def f_s2(e, i=b2, o0=b0, o1=b1):
        e.activation(o0, i, AF.Exp, scale=LN2)
        e.activation(o1, i, AF.Exp, scale=-LN2)
    prog.add("scalar", f_s2)
    sc2 = pub(prog, s, "scalar")
    prog.wait("vector", s["ssc"], sc2)

    def f_q(e, i=src3, o=f03, rs=b1, q=f0, aq=vbuf):
        e.tensor_tensor(o, i, rs.unsqueeze(2).broadcast_to([128, NB, 32]), OP.mult)
        e.drain()
        e.tensor_scalar_mul(aq, q, -1.0)
        e.drain()
        e.tensor_max(aq, aq, q)
        e.drain()
    prog.add("vector", f_q)
    def f_lad(e, q=f0, aq=vbuf, a05=m2, a1=m4, o=dst_ap):
        e.tensor_single_scalar(a05, aq, 0.25, OP.is_gt)
        e.drain()
        e.scalar_tensor_tensor(a05, aq, 0.75, a05, OP.is_ge, OP.add)
        e.drain()
        e.scalar_tensor_tensor(a05, aq, 1.25, a05, OP.is_gt, OP.add)
        e.drain()
        e.scalar_tensor_tensor(a05, aq, 1.75, a05, OP.is_ge, OP.add)
        e.drain()
        e.tensor_single_scalar(a1, aq, 2.5, OP.is_gt)
        e.drain()
        e.scalar_tensor_tensor(a1, aq, 3.5, a1, OP.is_ge, OP.add)
        e.drain()
        e.scalar_tensor_tensor(a1, a05, 0.5, a1, OP.mult, OP.add)
        e.drain()
        e.tensor_single_scalar(a05, aq, 5.0, OP.is_gt)
        e.drain()
        e.scalar_tensor_tensor(a1, a05, 2.0, a1, OP.mult, OP.add)
        e.drain()
        e.tensor_single_scalar(a05, q, 0.0, OP.is_ge)
        e.drain()
        e.scalar_tensor_tensor(a05, a05, 2.0, a1, OP.mult, OP.mult)
        e.drain()
        e.tensor_sub(o, a05, a1)
        e.drain()
    prog.add("vector", f_lad)
    def f_out(e, o=dst3, sc=b0):
        e.tensor_tensor(o, o, sc.unsqueeze(2).broadcast_to([128, NB, 32]), OP.mult)
    prog.add("vector", f_out)
    return pub(prog, s, "vector")



def emit_C(ctx, first):
    prog, s = ctx["prog"], ctx["sems"]
    nc, din, dint = ctx["nc"], ctx["din"], ctx["dint"]
    x_sb, g0, g1, g2, g3, g4, wbuf, sm = (ctx[k] for k in
        ("x_sb", "g0", "g1", "g2", "g3", "g4", "wbuf", "sm"))
    ps0, ps1 = ctx["ps0"], ctx["ps1"]
    stW, stG = ctx["stW"], ctx["stG"]
    NB = D // 32

    scr = dict(f0=g2[:, 0:D], m2=g2[:, D:2*D], m4=g3[:, 0:D], v=g3[:, D:2*D],
               b0=sm[:, 0:NB], b1=sm[:, NB:2*NB], b2=sm[:, 2*NB:3*NB])

    if first:
        for nck in range(NDC):
            lv = stW.group("sync", [(wbuf[:, 0:D], din["c_weight"][nck*128:(nck+1)*128, :])])
            prog.wait("vector", stW.ctr, lv)
            if nck >= 1:
                prog.wait("vector", s["smm"], s["smm"].v)
                prog.wait("vector", stG.ctr, stG.ctr.v)
            qv = emit_quant_dequant(ctx, wbuf[:, 0:D], g0[:, 0:D], scr)
            stW.consumed(s["svec"], qv)
            prog.wait("tensor", s["svec"], qv)
            for half in range(2):
                fns = []
                for kc in range(half*8, half*8+8):
                    def f_t(e, src=g0[:, kc*128:(kc+1)*128],
                            d=ps1[:, (kc % 8)*128:(kc % 8)*128+128], idn=ctx["ident"]):
                        return e.transpose(d, src, idn[:])
                    fns.append(f_t)
                mmv = tensor_group(prog, s, fns)
                prog.wait("vector", s["smm"], mmv)
                def f_cp(e, d=g1[:, half*1024:half*1024+1024], ss=ps1[:, 0:1024]):
                    e.tensor_copy(d, ss)
                prog.add("vector", f_cp)
                cv = pub(prog, s, "vector")
                prog.wait("tensor", s["svec"], cv)
            prog.wait("gpsimd", s["svec"], s["svec"].v)
            stG.group("gpsimd", [(
                dint["wqdT"].rearrange("(kc p) n -> p kc n", p=128)[:, :, nck*128:(nck+1)*128],
                g1[:, 0:NDC*128].rearrange("p (kc n) -> p kc n", kc=NDC))])

    afm = lambda kc: (g0 if kc < 8 else g1)[:, (kc % 8)*1024:(kc % 8)*1024+1024]
    for t in range(NT):
        xt = x_sb[:, t*D:(t+1)*D]
        if first and t == 0:
            prog.wait("vector", stG.ctr, stG.ctr.v)
        lnv = ln_tile(ctx, xt, g4[:, 0:D], g4[:, D:2*D], sm[:, 3*NB:3*NB+4])
        qv = emit_quant_dequant(ctx, g4[:, 0:D], g4[:, D:2*D], scr)
        prog.wait("tensor", s["svec"], qv)
        for half in range(2):
            fns = []
            for kc in range(half*8, half*8+8):
                def f_t(e, src=g4[:, D + kc*128: D + (kc+1)*128],
                        d=ps1[:, (kc % 8)*128:(kc % 8)*128+128], idn=ctx["ident"]):
                    return e.transpose(d, src, idn[:])
                fns.append(f_t)
            mmv = tensor_group(prog, s, fns)
            prog.wait("vector", s["smm"], mmv)
            for kc in range(half*8, half*8+8):
                def f_cp(e, d=afm(kc)[:, t*128:(t+1)*128],
                         ss=ps1[:, (kc % 8)*128:(kc % 8)*128+128]):
                    e.tensor_copy(d, ss)
                prog.add("vector", f_cp)
            cv = pub(prog, s, "vector")
            prog.wait("tensor", s["svec"], cv)
    afm_ready = s["svec"].v

    prog.wait("sync", s["svec"], afm_ready)
    bv = stW.group("sync", [(g3[:, 0:D], din["c_bias_rep"])])
    prog.wait("vector", stW.ctr, bv)
    prog.add("vector", lambda e, o=g4[:, 0:D], i=g3[:, 0:D]: e.tensor_scalar_mul(o, i, 0.5))
    hb_ready = pub(prog, s, "vector")
    stW.consumed(s["svec"], hb_ready)

    for j in range(4):
        wv = stW.group("sync", [(
            g2[:].rearrange("p (kc n) -> p kc n", kc=NDC),
            dint["wqdT"].rearrange("(kc p) n -> p kc n", p=128)[:, :, j*512:(j+1)*512])])
        prog.wait("tensor", stW.ctr, wv)
        if j == 0:
            prog.wait("tensor", s["svec"], afm_ready)
        fns = []
        for t in range(NT):
            pst = (ps0 if t < 4 else ps1)[:, (t % 4)*512:(t % 4)*512+512]
            for kc in range(NDC):
                def f_mm(e, d=pst, l=afm(kc)[:, t*128:(t+1)*128],
                         r=g2[:, kc*512:(kc+1)*512], st_=(kc == 0), sp=(kc == NDC-1)):
                    return e.matmul(d, l, r, start=st_, stop=sp)
                fns.append(f_mm)
        mmv = tensor_group(prog, s, fns)
        stW.consumed(s["smm"], mmv)
        prog.wait("vector", s["smm"], mmv)
        if j == 0:
            prog.wait("vector", s["svec"], hb_ready)
        for t in range(NT):
            pst = (ps0 if t < 4 else ps1)[:, (t % 4)*512:(t % 4)*512+512]
            def f_ev(e, ps=pst, hb=g4[:, j*512:(j+1)*512],
                     xt=x_sb[:, t*D+j*512:t*D+(j+1)*512],
                     tmp=g3[:, D+512*(t % 2):D+512*(t % 2)+512]):
                e.scalar_tensor_tensor(tmp, ps, 0.5, hb, OP.mult, OP.add)
                e.drain()
                e.tensor_add(xt, xt, tmp)
                e.drain()
            prog.add("vector", f_ev)
        ev = pub(prog, s, "vector")
        prog.wait("tensor", s["svec"], ev)


def emit_A(ctx):
    prog, s = ctx["prog"], ctx["sems"]
    nc, din, dint = ctx["nc"], ctx["din"], ctx["dint"]
    x_sb, g0, g1, g2, g3, g4, wbuf, sm, smi = (ctx[k] for k in
        ("x_sb", "g0", "g1", "g2", "g3", "g4", "wbuf", "sm", "smi"))
    ps0, ps1 = ctx["ps0"], ctx["ps1"]
    stW, stG = ctx["stW"], ctx["stG"]
    rotA = ctx["rotA"]
    NRT = 4
    NH = 16

    col4 = sm[:, 0:4]
    lgcols = sm[:, 16:32]
    cnt_c = sm[:, 32:48]; pos_c = sm[:, 48:64]
    mask_c = sm[:, 64:80]; tmp_c = sm[:, 80:96]
    rw_c = sm[:, 96:100]
    b2col = sm[:, 100:101]
    nmaxc = sm[:, 104:105]; rsumc = sm[:, 105:106]
    one1 = sm[:, 112:113]
    val_m = sm[:, 113:114]; val_d = sm[:, 114:115]
    selcols = smi[:, 0:4]
    srccols = smi[:, 8:16]

    cl = stW.group("sync", [
        (g3[:, 0:2048], din["a_iota"]),
        (sm[:, 128:144], din["a_idxcol"]),
        (sm[:, 144:148], din["a_rcol"]),
        (sm[:, 148:149], din["a_halfflag"]),
        (sm[0:1, 152:280], din["a_ones1"]),
    ])
    idxcol = sm[:, 128:144]; rcol = sm[:, 144:148]
    hf_col = sm[:, 148:149]
    ones1 = sm[0:1, 152:280]
    prog.wait("vector", stW.ctr, cl)
    prog.wait("tensor", stW.ctr, cl)
    gl = stW.group("sync", [(g2[:, 0:D], din["a_wrt_rep"])])
    wrt = g2[:, 0:D]
    prog.wait("vector", stW.ctr, gl)

    prog.wait("vector", s["smm"], s["smm"].v)
    prog.wait("vector", s["ssc"], s["ssc"].v)
    prog.wait("gpsimd", s["ssc"], s["ssc"].v)
    for t in range(NT):
        xt = x_sb[:, t*D:(t+1)*D]
        if t > 0:
            prog.wait("scalar", stG.ctr, stG.ctr.v)
            prog.wait("vector", stG.ctr, stG.ctr.v)
        lnv = ln_tile(ctx, xt, g4[:, 0:D], g4[:, D:2*D], col4)
        prog.wait("gpsimd", s["svec"], lnv)
        stG.group("gpsimd", [(dint["lnmy"][t*128:(t+1)*128, :], g4[:, 0:D])])
        def f_lg(e, h=g4[:, 0:D], w=wrt, scr=g4[:, D:2*D]):
            e.tensor_mul(scr, h, w)
            e.drain()
        prog.add("vector", f_lg)
        prog.add("vector", lambda e, scr=g4[:, D:2*D], lg=tmp_c[:, t:t+1]:
                 e.reduce_sum(lg, scr, axis=AX.X))
        def f_xu(e, xt=xt, h=g4[:, 0:D]):
            e.drain()
            e.scalar_tensor_tensor(xt, h, 0.5, xt, OP.mult, OP.add)
        prog.add("vector", f_xu)
        lgv = pub(prog, s, "vector")
    prog.wait("gpsimd", s["svec"], s["svec"].v)
    lgd = stG.group("gpsimd", [(dint["lgmy"], tmp_c[:, 0:8])])
    prog.wait("gpsimd", stG.ctr, lgd)
    csem = ctx["new_coll_sem"]()
    def f_coll(e, h=csem.h):
        e.collective_compute("AllGather", OP.bypass, replica_groups=PAIR_GROUPS,
                             ins=[dint["lgmy"]], outs=[dint["lg_ag"]]).then_inc(h, 1)
    prog.add("gpsimd", f_coll)
    csem.bump()
    prog.wait("sync", csem)
    lgl = stW.group("sync", [(lgcols[:, 0:8], dint["lg_ag"][0:128, :]),
                             (lgcols[:, 8:16], dint["lg_ag"][128:256, :])])
    prog.wait("vector", stW.ctr, lgl)
    prog.wait("scalar", stW.ctr, lgl)
    prog.wait("vector", stG.ctr, stG.ctr.v)
    prog.add("vector", lambda e: (e.memset(g4[:, 0:128], 1.0), e.drain()))
    onetile = g4[:, 0:128]

    def col_bcast(cols_ap, dst_base_fn, n=16):
        for c in range(n):
            def f_cr(e, o=g4[:, 128:256], one=onetile, cc=cols_ap[:, c:c+1]):
                e.tensor_scalar(o, one, cc, None, OP.mult, OP.bypass)
                e.drain()
            prog.add("vector", f_cr)
            crv = pub(prog, s, "vector")
            prog.wait("tensor", s["svec"], crv)
            cv = transpose_block(ctx, [g4[:, 128:256]], [dst_base_fn(c)], ps1[:, 0:128])
            prog.wait("tensor", s["svec"], cv)
        return cv

    col_bcast(lgcols, lambda c: g3[:, 2048 + c*128: 2048 + (c+1)*128])
    lg_bc = g3[:, 2048:4096]; iota_bc = g3[:, 0:2048]

    scr_rk = g0[:, 0:2048]
    for c in range(16):
        def f_rk(e, bc=lg_bc, lc=lgcols[:, c:c+1], scr=scr_rk, cnt=cnt_c[:, c:c+1],
                 it=iota_bc, ic=idxcol[:, c:c+1], eqc=tmp_c[:, c:c+1]):
            e.tensor_scalar(scr, bc, lc, 0.0, OP.is_gt, OP.add, accum_out=cnt)
            e.drain()
            e.tensor_scalar(scr, bc, lc, None, OP.is_equal, OP.bypass)
            e.drain()
            e.scalar_tensor_tensor(scr, it, ic, scr, OP.is_lt, OP.mult, accum_out=eqc)
            e.drain()
            e.tensor_add(cnt, cnt, eqc)
            e.drain()
        prog.add("vector", f_rk)
    def f_mk(e, cnt=cnt_c, mk=mask_c):
        e.tensor_single_scalar(mk, cnt, 1024.0, OP.is_lt)
        e.drain()
    prog.add("vector", f_mk)
    rkv = pub(prog, s, "vector")
    mask_bc = g4[:, 2048:4096]
    col_bcast(cnt_c, lambda c: mask_bc[:, c*128:(c+1)*128])
    def f_mb(e, d=mask_bc):
        e.tensor_single_scalar(d, d, 1024.0, OP.is_lt)
        e.drain()
    prog.add("vector", f_mb)
    mbv = pub(prog, s, "vector")
    for c in range(16):
        def f_ps(e, it=iota_bc, ic=idxcol[:, c:c+1], mb=mask_bc, scr=scr_rk,
                 pc=pos_c[:, c:c+1]):
            e.scalar_tensor_tensor(scr, it, ic, mb, OP.is_lt, OP.mult, accum_out=pc)
            e.drain()
        prog.add("vector", f_ps)
    def f_pp(e, pc=pos_c, mk=mask_c, t1=tmp_c, cn=cnt_c):
        e.scalar_tensor_tensor(t1, mk, -4096.0, pc, OP.mult, OP.add)
        e.drain()
        e.tensor_scalar_add(t1, t1, 4096.0)
        e.drain()
        e.tensor_scalar(cn, t1, 1024.0, None, OP.min, OP.bypass)
        e.drain()
    prog.add("vector", f_pp)
    pp_bc = g4[:, 2048:4096]
    col_bcast(tmp_c, lambda c: pp_bc[:, c*128:(c+1)*128])
    def f_src(e, cn=cnt_c, hfv=hf_col, dst=tmp_c[:, 0:8], dif=tmp_c[:, 8:16]):
        e.tensor_sub(dif, cn[:, 8:16], cn[:, 0:8])
        e.drain()
        e.scalar_tensor_tensor(dst, dif, hfv, cn[:, 0:8], OP.mult, OP.add)
        e.drain()
    prog.add("vector", f_src)
    def f_srci(e, dst=srccols, srcf=tmp_c[:, 0:8]):
        e.tensor_copy(dst, srcf)
        e.drain()
    prog.add("vector", f_srci)
    for rc in range(NRT):
        def f_sel(e, bc=pp_bc, rcc=rcol[:, rc:rc+1], scr=scr_rk, it=iota_bc,
                  out=tmp_c[:, 8+rc:8+rc+1]):
            e.tensor_scalar(scr, bc, rcc, None, OP.is_equal, OP.bypass)
            e.drain()
            e.tensor_mul(scr, scr, it)
            e.drain()
            e.reduce_sum(out, scr, axis=AX.X)
            e.drain()
        prog.add("vector", f_sel)
    def f_seli(e, dst=selcols, srcf=tmp_c[:, 8:12]):
        e.tensor_copy(dst, srcf)
        e.drain()
    prog.add("vector", f_seli)
    selv = pub(prog, s, "vector")

    def f_rw1(e, lg=lgcols, nm=val_m, em=tmp_c[:, 0:16] if False else None):
        e.reduce_max(nm, lg, axis=AX.X, negate=True)
        e.drain()
    prog.add("vector", f_rw1)
    r1v = pub(prog, s, "vector")
    prog.wait("tensor", s["svec"], r1v)
    cv = transpose_block(ctx, [val_m], [g4[0:1, 128:256]], ps1[0:1, 0:128])
    prog.wait("vector", s["svec"], cv)
    def f_rw2(e, row=g4[0:1, 128:256], nm=val_m[0:1, :]):
        e.reduce_max(nm, row, axis=AX.X)
        pass
    prog.add("vector", lambda e, row=g4[0:1, 128:256], nm=val_m[0:1, :]:
             e.tensor_reduce(nm, row, axis=AX.X, op=OP.min))
    r2v = pub(prog, s, "vector")
    prog.wait("tensor", s["svec"], r2v)
    mmv = tensor_group(prog, s, [lambda e, d=ps1[:, 0:1], l=ones1, r=val_m[0:1, :]:
                                 e.matmul(d, l, r, start=True, stop=True)])
    prog.wait("vector", s["smm"], mmv)
    prog.add("vector", lambda e: (e.tensor_copy(val_m, ps1[:, 0:1]), e.drain()))
    r3v = pub(prog, s, "vector")
    prog.wait("scalar", s["svec"], r3v)
    def f_rw3(e, lg=lgcols, nm=val_m, ec=scr_rk[:, 0:16]):
        e.activation(ec, lg, AF.Exp, bias=nm)
    prog.add("scalar", f_rw3)
    r4v = pub(prog, s, "scalar")
    prog.wait("vector", s["ssc"], r4v)
    def f_rw4(e, ec=scr_rk[:, 0:16], mk=mask_c, sd=val_d):
        e.tensor_mul(ec, ec, mk)
        e.drain()
        e.reduce_sum(sd, ec, axis=AX.X)
        e.drain()
    prog.add("vector", f_rw4)
    r5v = pub(prog, s, "vector")
    prog.wait("tensor", s["svec"], r5v)
    cv = transpose_block(ctx, [val_d], [g4[0:1, 128:256]], ps1[0:1, 0:128])
    prog.wait("vector", s["svec"], cv)
    prog.add("vector", lambda e, row=g4[0:1, 128:256], sd=val_d[0:1, :]:
             (e.reduce_sum(sd, row, axis=AX.X), e.drain()))
    r6v = pub(prog, s, "vector")
    prog.wait("scalar", s["svec"], r6v)
    def f_rw5(e, sd=val_d[0:1, :]):
        e.activation(sd, sd, AF.Ln)
    prog.add("scalar", f_rw5)
    r7v = pub(prog, s, "scalar")
    prog.wait("vector", s["ssc"], r7v)
    def f_rw6(e, sd=val_d[0:1, :], nm=val_m[0:1, :], b2=b2col[0:1, :]):
        e.tensor_sub(b2, nm, sd)
        e.drain()
        e.tensor_scalar_add(b2, b2, float(np.log(0.5)))
        e.drain()
    prog.add("vector", f_rw6)
    r8v = pub(prog, s, "vector")
    prog.wait("tensor", s["svec"], r8v)
    mmv = tensor_group(prog, s, [lambda e, d=ps1[:, 0:1], l=ones1, r=b2col[0:1, :]:
                                 e.matmul(d, l, r, start=True, stop=True)])
    prog.wait("vector", s["smm"], mmv)
    prog.add("vector", lambda e: (e.tensor_copy(b2col, ps1[:, 0:1]), e.drain()))
    b2v = pub(prog, s, "vector")
    prog.wait("tensor", s["svec"], b2v)

    prog.wait("gpsimd", stG.ctr, stG.ctr.v)
    csem2 = ctx["new_coll_sem"]()
    def f_coll2(e, h=csem2.h):
        e.collective_compute("AllGather", OP.bypass, replica_groups=PAIR_GROUPS,
                             ins=[dint["lnmy"]], outs=[dint["ln_ag"]]).then_inc(h, 1)
    prog.add("gpsimd", f_coll2)
    csem2.bump()
    prog.wait("gpsimd", csem2)
    prog.wait("gpsimd", s["svec"], selv)
    fgv = stG.group("gpsimd", pairs=[], indirect=[
        (g1[:, rc*2048:(rc+1)*2048], None, dint["ln_ag"],
         IndirectOffsetOnAxis(selcols[:, rc:rc+1], 0), OP.bypass) for rc in range(NRT)])
    prog.wait("vector", stG.ctr, fgv)
    prog.wait("scalar", stG.ctr, fgv)
    f_tm = g1

    for rc in range(NRT):
        def f_ls(e, f=f_tm[:, rc*2048:(rc+1)*2048], w=wrt, scr=g4[:, 0:2048],
                 out=rw_c[:, rc:rc+1]):
            e.tensor_mul(scr, f, w)
            e.drain()
            e.reduce_sum(out, scr, axis=AX.X)
            e.drain()
        prog.add("vector", f_ls)
    lsv = pub(prog, s, "vector")
    prog.wait("scalar", s["svec"], lsv)
    def f_rwf(e, rw=rw_c, b2=b2col):
        e.activation(rw, rw, AF.Exp, bias=b2)
    prog.add("scalar", f_rwf)
    rwf = pub(prog, s, "scalar")

    prog.wait("sync", s["svec"], s["svec"].v)
    g1l = stW.group("sync", [(g2[:, 0:D], din["a_g1_rep"])])
    g1rep = g2[:, 0:D]
    prog.wait("vector", stW.ctr, g1l)
    for rc in range(NRT):
        lnv = ln_tile(ctx, f_tm[:, rc*2048:(rc+1)*2048], g0[:, rc*2048:(rc+1)*2048],
                      g4[:, 0:D], col4, eps=1e-6, gain_rep=g1rep, rms=True)
    h1v = lnv
    prog.wait("tensor", s["svec"], h1v)
    for rc in range(NRT):
        for half in range(2):
            srcs = [g0[:, rc*2048 + kc*128: rc*2048 + (kc+1)*128] for kc in range(half*8, half*8+8)]
            dsts = [g2[:, (half*8+i)*512 + rc*128: (half*8+i)*512 + (rc+1)*128] for i in range(8)]
            cv = transpose_block(ctx, srcs, dsts, ps1[:, 0:1024])
            prog.wait("tensor", s["svec"], cv)
    h1fm = lambda dc: g2[:, dc*512:(dc+1)*512]

    prog.wait("sync", s["svec"], s["svec"].v)
    ra_l = stW.group("sync", [(g3[:, 2048:2560], din["a_ropeA"]),
                              (g3[:, 2560:3072], din["a_ropeB"])])
    ropeA = g3[:, 2048:2560]; ropeB = g3[:, 2560:3072]
    prog.wait("vector", stW.ctr, ra_l)

    def rope_evac_inner(ps_ap, dst_ap, rt):
        t_all = g3[:, 3072:3584]; t_cr = g3[:, 3584:4096]
        def f_r(e, ps=ps_ap, dst=dst_ap, ta=t_all, tc=t_cr,
                tA=ropeA[:, rt*128:(rt+1)*128], tB=ropeB[:, rt*128:(rt+1)*128]):
            v3 = lambda ap: ap.rearrange("p (h f) -> p h f", f=128)
            e.tensor_tensor(v3(ta), v3(ps), tA.unsqueeze(1).broadcast_to([128, 4, 128]), OP.mult)
            e.tensor_tensor(v3(tc), v3(ps), tB.unsqueeze(1).broadcast_to([128, 4, 128]), OP.mult)
            e.drain()
            ta3 = v3(ta); tc3 = v3(tc); d3 = v3(dst)
            e.tensor_tensor(d3[:, :, 0:64], ta3[:, :, 0:64], ta3[:, :, 64:128], OP.subtract)
            e.tensor_tensor(d3[:, :, 64:128], tc3[:, :, 0:64], tc3[:, :, 64:128], OP.add)
            e.drain()
        prog.add("vector", f_r)

    for fg in range(4):
        for dc in range(NDC):
            sl = rotA.i % 4
            wslot = wbuf[:, sl*512:sl*512+512]
            wc, wv = rotA.load("sync", [(wslot, din["a_wk"][dc*128:(dc+1)*128, fg*512:(fg+1)*512])])
            prog.wait("tensor", wc, wv)
            fns = []
            for rt in range(NRT):
                def f_mm(e, d=ps0[:, rt*512:(rt+1)*512], l=h1fm(dc)[:, rt*128:(rt+1)*128],
                         r=wslot, st_=(dc == 0), sp=(dc == NDC-1)):
                    return e.matmul(d, l, r, start=st_, stop=sp)
                fns.append(f_mm)
            mmv = tensor_group(prog, s, fns)
            rotA.consumed(s["smm"], mmv)
        prog.wait("vector", s["smm"], mmv)
        for rt in range(NRT):
            rope_evac_inner(ps0[:, rt*512:(rt+1)*512], g4[:, rt*512:(rt+1)*512], rt)
        kev = pub(prog, s, "vector")
        prog.wait("tensor", s["svec"], kev)
        for rt in range(NRT):
            srcs = [g4[:, rt*512 + i*128: rt*512 + (i+1)*128] for i in range(4)]
            dsts = [g4[:, 2048 + i*512 + rt*128: 2048 + i*512 + (rt+1)*128] for i in range(4)]
            cv = transpose_block(ctx, srcs, dsts, ps1[:, 0:512])
            prog.wait("tensor", s["svec"], cv)
        prog.wait("gpsimd", s["svec"], cv)
        stG.group("gpsimd", [(dint["kmy"][:, (fg*4)*512:(fg*4+4)*512],
                              g4[:, 2048:4096])])
        prog.wait("vector", stG.ctr, stG.ctr.v)
        for dc in range(NDC):
            sl = rotA.i % 4
            wslot = wbuf[:, sl*512:sl*512+512]
            wc, wv = rotA.load("sync", [(wslot, din["a_wv"][dc*128:(dc+1)*128, fg*512:(fg+1)*512])])
            prog.wait("tensor", wc, wv)
            fns = []
            for rt in range(NRT):
                def f_mm(e, d=ps0[:, rt*512:(rt+1)*512], l=h1fm(dc)[:, rt*128:(rt+1)*128],
                         r=wslot, st_=(dc == 0), sp=(dc == NDC-1)):
                    return e.matmul(d, l, r, start=st_, stop=sp)
                fns.append(f_mm)
            mmv = tensor_group(prog, s, fns)
            rotA.consumed(s["smm"], mmv)
        prog.wait("vector", s["smm"], mmv)
        for rt in range(NRT):
            def f_cp(e, d=g4[:, rt*512:(rt+1)*512], ss=ps0[:, rt*512:(rt+1)*512]):
                e.tensor_copy(d, ss)
            prog.add("vector", f_cp)
        vev = pub(prog, s, "vector")
        prog.wait("tensor", s["svec"], vev)
        prog.wait("gpsimd", s["svec"], vev)
        stG.group("gpsimd", [(dint["vmy"].rearrange("(rt p) f -> p rt f", p=128)[:, :, fg*512:(fg+1)*512],
                              g4[:, 0:2048].rearrange("p (rt f) -> p rt f", rt=NRT))])
        prog.wait("vector", stG.ctr, stG.ctr.v)

    prog.wait("vector", s["smm"], s["smm"].v)
    for fg in range(4):
        for dc in range(NDC):
            sl = rotA.i % 4
            wslot = wbuf[:, sl*512:sl*512+512]
            wc, wv = rotA.load("sync", [(wslot, din["a_wq"][dc*128:(dc+1)*128, fg*512:(fg+1)*512])])
            prog.wait("tensor", wc, wv)
            fns = []
            for rt in range(NRT):
                def f_mm(e, d=ps0[:, rt*512:(rt+1)*512], l=h1fm(dc)[:, rt*128:(rt+1)*128],
                         r=wslot, st_=(dc == 0), sp=(dc == NDC-1)):
                    return e.matmul(d, l, r, start=st_, stop=sp)
                fns.append(f_mm)
            mmv = tensor_group(prog, s, fns)
            rotA.consumed(s["smm"], mmv)
        prog.wait("vector", s["smm"], mmv)
        for rt in range(NRT):
            rope_evac_inner(ps0[:, rt*512:(rt+1)*512],
                            g0[:, rt*2048 + fg*512: rt*2048 + (fg+1)*512], rt)
        qev = pub(prog, s, "vector")
        prog.wait("tensor", s["svec"], qev)
    for rc in range(NRT):
        for half in range(2):
            srcs = [g0[:, rc*2048 + kc*128: rc*2048 + (kc+1)*128] for kc in range(half*8, half*8+8)]
            dsts = [g2[:, (half*8+i)*512 + rc*128: (half*8+i)*512 + (rc+1)*128] for i in range(8)]
            cv = transpose_block(ctx, srcs, dsts, ps1[:, 0:1024])
            prog.wait("tensor", s["svec"], cv)
    qfm = lambda hh: g2[:, hh*512:(hh+1)*512]

    prog.wait("gpsimd", stG.ctr, stG.ctr.v)
    csem3 = ctx["new_coll_sem"]()
    def f_coll3(e, h=csem3.h):
        e.collective_compute("AllGather", OP.bypass, replica_groups=PAIR_GROUPS,
                             ins=[dint["kmy"]], outs=[dint["k_ag"]]).then_inc(h, 1)
        e.collective_compute("AllGather", OP.bypass, replica_groups=PAIR_GROUPS,
                             ins=[dint["vmy"]], outs=[dint["v_ag"]]).then_inc(h, 1)
    prog.add("gpsimd", f_coll3)
    csem3.bump(2)
    prog.wait("sync", csem3)

    scale = 1.0 / math.sqrt(128.0)
    att = g3[:, 0:4096]
    kvh = g3[:, 2048:4096] if False else None
    for hh in range(NH):
        prog.wait("sync", s["smm"], s["smm"].v)
        kl = stW.group("sync", [
            (g0[:, 0:512], dint["k_ag"][0:128, hh*512:(hh+1)*512]),
            (g0[:, 512:1024], dint["k_ag"][128:256, hh*512:(hh+1)*512]),
            (g0[:, 1024:2048],
             dint["v_ag"].rearrange("(kc p) f -> p kc f", p=128)[:, :, hh*128:(hh+1)*128]),
        ])
        kfmh = g0[:, 0:1024]; vtmh = g0[:, 1024:2048]
        prog.wait("tensor", stW.ctr, kl)
        if hh == 0:
            prog.wait("tensor", s["svec"], s["svec"].v)
        if hh == 0:
            rotA.reset(s["svec"], s["svec"].v)
        nmaxq = lambda qt: sm[:, 256 + qt*2: 256 + qt*2 + 1]
        rsumq = lambda qt: sm[:, 256 + qt*2 + 1: 256 + qt*2 + 2]
        exvs = {}; nmvs = {}
        for qt in range(NRT):
            mc = rotA.ctrs[qt]
            g = rotA.hist.get(qt)
            if g is not None:
                prog.wait("sync", g[0], g[1])
            elif mc.v > 0:
                prog.wait("sync", mc, mc.v)
            def f_ld(e, o=g4[:, qt*1024:(qt+1)*1024],
                     i=din["a_mask_my"][qt*128:(qt+1)*128, :], h2=mc.h):
                e.dma_start(out=o, in_=i).then_inc(h2, 16)
            prog.add("sync", f_ld)
            mc.v += 16
            rotA.i += 1
            pbank = ps0[:, (qt % 2)*1024:(qt % 2)*1024+1024]
            if qt >= 2:
                prog.wait("tensor", s["svec"], exvs[qt-2])
            fns = []
            for kg in range(2):
                def f_mm(e, d=pbank[:, kg*512:(kg+1)*512], l=qfm(hh)[:, qt*128:(qt+1)*128],
                         r=kfmh[:, kg*512:(kg+1)*512]):
                    return e.matmul(d, l, r, start=True, stop=True)
                fns.append(f_mm)
            mmv = tensor_group(prog, s, fns)
            prog.wait("vector", s["smm"], mmv)
            prog.wait("vector", mc, mc.v)
            def f_ev(e, sc=g4[:, qt*1024:(qt+1)*1024], ps=pbank, nm=nmaxq(qt)):
                e.scalar_tensor_tensor(sc, ps, scale, sc, OP.mult, OP.add)
                e.drain()
                e.reduce_max(nm, sc, axis=AX.X, negate=True)
            prog.add("vector", f_ev)
            exvs[qt] = pub(prog, s, "vector")
            prog.wait("scalar", s["svec"], exvs[qt])
            def f_ex(e, sc=g4[:, qt*1024:(qt+1)*1024], nm=nmaxq(qt), rs=rsumq(qt)):
                e.activation(sc, sc, AF.Exp, bias=nm, accum_out=rs)
            prog.add("scalar", f_ex)
            exs = pub(prog, s, "scalar")
            prog.wait("vector", s["ssc"], exs)
            def f_nm(e, sc=g4[:, qt*1024:(qt+1)*1024], rs=rsumq(qt)):
                e.reciprocal(rs, rs)
                e.drain()
                e.tensor_scalar(sc, sc, rs, None, OP.mult, OP.bypass)
                e.drain()
            prog.add("vector", f_nm)
            nmvs[qt] = pub(prog, s, "vector")
        for qt in range(NRT):
            prog.wait("tensor", s["svec"], nmvs[qt])
            srcs = [g4[:, qt*1024 + kc*128: qt*1024 + (kc+1)*128] for kc in range(8)]
            dsts = [att[:, kc*512 + qt*128: kc*512 + (qt+1)*128] for kc in range(8)]
            cv = transpose_block(ctx, srcs, dsts, ps1[:, 0:1024])
            prog.wait("tensor", s["svec"], cv)
            rotA.hist[qt] = (s["svec"], cv)
        fns = []
        for kc in range(8):
            def f_mm(e, d=ps1[:, 1024:1536], l=vtmh[:, kc*128:(kc+1)*128],
                     r=att[:, kc*512:(kc+1)*512], st_=(kc == 0), sp=(kc == 7)):
                return e.matmul(d, l, r, start=st_, stop=sp)
            fns.append(f_mm)
        mmc = tensor_group(prog, s, fns)
        prog.wait("vector", s["smm"], mmc)
        def f_cx(e, d=qfm(hh), ss=ps1[:, 1024:1536]):
            e.tensor_copy(d, ss)
            e.drain()
        prog.add("vector", f_cx)
        cxv = pub(prog, s, "vector")
        prog.wait("tensor", s["svec"], cxv)
        stW.consumed(s["svec"], cxv)

    ctxs = qfm
    g2l = stW.group("sync", [(g4[:, 0:D], din["a_g2_rep"])])
    g2rep = g4[:, 0:D]
    prog.wait("vector", stW.ctr, g2l)
    rotC = ctx["rotC"]
    rotC.reset(s["smm"], s["smm"].v)
    for j in range(4):
        for hc in range(NH):
            sl = rotC.i % 2
            ww_c, ww_v = rotC.load("sync", [(wbuf[:, sl*512:sl*512+512],
                                     din["a_wo"][hc*128:(hc+1)*128, j*512:(j+1)*512])])
            prog.wait("tensor", ww_c, ww_v)
            fns = []
            for rt in range(NRT):
                pst = ps0[:, rt*512:(rt+1)*512]
                def f_mm(e, d=pst, l=ctxs(hc)[:, rt*128:(rt+1)*128],
                         r=wbuf[:, sl*512:sl*512+512], st_=(hc == 0), sp=(hc == NH-1)):
                    return e.matmul(d, l, r, start=st_, stop=sp)
                fns.append(f_mm)
            mmo = tensor_group(prog, s, fns)
            rotC.consumed(s["smm"], mmo)
        prog.wait("vector", s["smm"], mmo)
        for rt in range(NRT):
            def f_x2(e, f=f_tm[:, rt*2048 + j*512: rt*2048 + (j+1)*512],
                     ps=ps0[:, rt*512:(rt+1)*512]):
                e.tensor_add(f, f, ps)
                e.drain()
            prog.add("vector", f_x2)
        x2v = pub(prog, s, "vector")
        prog.wait("tensor", s["svec"], x2v)
    x2 = f_tm

    for rc in range(NRT):
        lnv = ln_tile(ctx, x2[:, rc*2048:(rc+1)*2048], g0[:, rc*2048:(rc+1)*2048],
                      g4[:, D:2*D], col4, eps=1e-6, gain_rep=g2rep, rms=True)
    prog.wait("tensor", s["svec"], lnv)
    for rc in range(NRT):
        for half in range(2):
            srcs = [g0[:, rc*2048 + kc*128: rc*2048 + (kc+1)*128] for kc in range(half*8, half*8+8)]
            dsts = [g2[:, (half*8+i)*512 + rc*128: (half*8+i)*512 + (rc+1)*128] for i in range(8)]
            cv = transpose_block(ctx, srcs, dsts, ps1[:, 0:1024])
            prog.wait("tensor", s["svec"], cv)
    h2fm = lambda dc: g2[:, dc*512:(dc+1)*512]

    rotB = ctx["rotB"]
    rotB.reset(s["smm"], s["smm"].v)
    rotC.reset(s["smm"], s["smm"].v)
    g3A = g3[:, 0:2048]; g3B = g3[:, 2048:4096]
    tv_hist = {}
    for gg in range(8):
        for fi in range(8):
            ffc = gg*8 + fi
            p = fi % 2
            w1c, w1v = rotB.load("sync", [(g3A,
                din["a_w1"].rearrange("(dc p) f -> p dc f", p=128)[:, :, ffc*128:(ffc+1)*128])])
            prog.wait("tensor", w1c, w1v)
            if (gg, fi) != (0, 0):
                pv = tv_hist.get((gg*8 + fi - 2))
                if pv is not None:
                    prog.wait("tensor", s["svec"], pv)
            fns = []
            for dc in range(NDC):
                def f_mm(e, d=ps0[:, p*1024:p*1024+512], l=g3A[:, dc*128:(dc+1)*128],
                         r=h2fm(dc), st_=(dc == 0), sp=(dc == NDC-1)):
                    return e.matmul(d, l, r, start=st_, stop=sp)
                fns.append(f_mm)
            mm1 = tensor_group(prog, s, fns)
            rotB.consumed(s["smm"], mm1, back=0)
            w3c, w3v = rotB.load("sync", [(g3B,
                din["a_w3"].rearrange("(dc p) f -> p dc f", p=128)[:, :, ffc*128:(ffc+1)*128])])
            prog.wait("tensor", w3c, w3v)
            fns = []
            for dc in range(NDC):
                def f_mm(e, d=ps0[:, p*1024+512:p*1024+1024], l=g3B[:, dc*128:(dc+1)*128],
                         r=h2fm(dc), st_=(dc == 0), sp=(dc == NDC-1)):
                    return e.matmul(d, l, r, start=st_, stop=sp)
                fns.append(f_mm)
            mm3 = tensor_group(prog, s, fns)
            rotB.consumed(s["smm"], mm3, back=0)
            prog.wait("scalar", s["smm"], mm3)
            def f_t1(e, d=g0[:, p*512:(p+1)*512], ss=ps0[:, p*1024:p*1024+512]):
                e.activation(d, ss, AF.Silu)
            prog.add("scalar", f_t1)
            t1v = pub(prog, s, "scalar")
            prog.wait("vector", s["ssc"], t1v)
            def f_t(e, d=g4[:, fi*512:(fi+1)*512], t1=g0[:, p*512:(p+1)*512],
                    ps3=ps0[:, p*1024+512:p*1024+1024]):
                e.tensor_mul(d, t1, ps3)
                e.drain()
            prog.add("vector", f_t)
            tv_hist[ffc] = pub(prog, s, "vector")
        prog.wait("tensor", s["svec"], tv_hist[gg*8 + 7])
        for j in range(4):
            for fi in range(8):
                ffc = gg*8 + fi
                w2c, w2v = rotC.load("sync", [(wbuf[:, (fi % 2)*512:(fi % 2)*512+512],
                    din["a_w2"][ffc*128:(ffc+1)*128, j*512:(j+1)*512])])
                prog.wait("tensor", w2c, w2v)
                fns = []
                for rt in range(NRT):
                    def f_mm(e, d=ps1[:, rt*512:(rt+1)*512],
                             l=g4[:, fi*512 + rt*128: fi*512 + (rt+1)*128],
                             r=wbuf[:, (fi % 2)*512:(fi % 2)*512+512],
                             st_=(fi == 0), sp=(fi == 7)):
                        return e.matmul(d, l, r, start=st_, stop=sp)
                    fns.append(f_mm)
                mm2 = tensor_group(prog, s, fns)
                rotC.consumed(s["smm"], mm2, back=0)
            prog.wait("vector", s["smm"], mm2)
            for rt in range(NRT):
                def f_o2(e, x=x2[:, rt*2048 + j*512: rt*2048 + (j+1)*512],
                         ps=ps1[:, rt*512:(rt+1)*512]):
                    e.tensor_add(x, x, ps)
                    e.drain()
                prog.add("vector", f_o2)
            o2v = pub(prog, s, "vector")
            prog.wait("tensor", s["svec"], o2v)

    prog.wait("vector", s["ssc"], rwf)
    for rt in range(NRT):
        def f_xw(e, x=x2[:, rt*2048:(rt+1)*2048], rw=rw_c[:, rt:rt+1],
                 d=g0[:, rt*2048:(rt+1)*2048]):
            e.tensor_scalar(d, x, rw, None, OP.mult, OP.bypass)
            e.drain()
        prog.add("vector", f_xw)
    xwv = pub(prog, s, "vector")
    prog.wait("gpsimd", s["svec"], xwv)
    xwd = stG.group("gpsimd", [(dint["xwmy"].rearrange("(rt p) f -> p rt f", p=128),
                                g0[:].rearrange("p (rt f) -> p rt f", rt=NRT))])
    prog.add("vector", lambda e: (e.memset(g4[0:1, 0:D], 0.0), e.drain()))
    zv = pub(prog, s, "vector")
    prog.wait("gpsimd", s["svec"], zv)
    zd = stG.group("gpsimd", [(dint["xw_pad"][SEL:SEL+1, :], g4[0:1, 0:D])])
    prog.wait("gpsimd", stG.ctr, stG.ctr.v)
    csem4 = ctx["new_coll_sem"]()
    def f_coll4(e, h=csem4.h):
        e.collective_compute("AllGather", OP.bypass, replica_groups=PAIR_GROUPS,
                             ins=[dint["xwmy"]], outs=[dint["xw_pad"][0:SEL, :]]).then_inc(h, 1)
    prog.add("gpsimd", f_coll4)
    csem4.bump()
    prog.wait("gpsimd", csem4)
    for t in range(NT):
        gb = stG.group("gpsimd", pairs=[], indirect=[
            (g4[:, (t % 2)*2048:(t % 2)*2048+2048], None, dint["xw_pad"],
             IndirectOffsetOnAxis(srccols[:, t:t+1], 0), OP.bypass)])
        prog.wait("vector", stG.ctr, gb)
        def f_bg(e, x=x_sb[:, t*D:(t+1)*D], gsl=g4[:, (t % 2)*2048:(t % 2)*2048+2048]):
            e.tensor_add(x, x, gsl)
            e.drain()
        prog.add("vector", f_bg)
        bgv = pub(prog, s, "vector")
        stG.consumed(s["svec"], bgv)


def emit_B(ctx):
    prog, s = ctx["prog"], ctx["sems"]
    nc, din, dint = ctx["nc"], ctx["din"], ctx["dint"]
    x_sb, g0, g1, g2, g3, g4, wbuf, sm, smi = (ctx[k] for k in
        ("x_sb", "g0", "g1", "g2", "g3", "g4", "wbuf", "sm", "smi"))
    ps0, ps1 = ctx["ps0"], ctx["ps1"]
    stW, stG = ctx["stW"], ctx["stG"]
    rotA, rotB = ctx["rotA"], ctx["rotB"]
    NQT = T // 128

    col4 = sm[:, 0:4]
    mu_c = sm[:, 8:16]; rstd_c = sm[:, 16:24]; sg_c = sm[:, 24:32]
    causal = sm[:, 128:256]
    nmax = lambda head: sm[:, 256 + head*2: 256 + head*2 + 1]
    rsum = lambda head: sm[:, 256 + head*2 + 1: 256 + head*2 + 2]
    ropetab = lambda sl: g0[:, sl]
    q_slot = lambda hp: x_sb[:, hp*2048:(hp+1)*2048]
    lnfm = lambda dc: g2[:, dc*512:(dc+1)*512]
    kfm = g3
    vtm = g4

    cl = stW.group("sync", [(causal, din["b_causal"])])
    prog.wait("vector", stW.ctr, cl)

    prog.wait("vector", s["smm"], s["smm"].v)
    prog.wait("vector", s["ssc"], s["ssc"].v)
    prog.wait("gpsimd", s["ssc"], s["ssc"].v)
    for t in range(NT):
        xt = x_sb[:, t*D:(t+1)*D]
        lnv = ln_tile(ctx, xt, g4[:, 0:D], g4[:, D:2*D], col4,
                      save_musig=(mu_c[:, t:t+1], rstd_c[:, t:t+1]))
        prog.wait("gpsimd", s["svec"], lnv)
        stG.group("gpsimd", [(dint["lnmy"][t*128:(t+1)*128, :], g4[:, 0:D])])
        if t < NT - 1:
            prog.wait("scalar", stG.ctr, stG.ctr.v)
            prog.wait("vector", stG.ctr, stG.ctr.v)
    prog.wait("vector", s["ssc"], s["ssc"].v)
    def f_sig(e, r=rstd_c[:, 0:NT], o=sg_c[:, 0:NT]):
        e.reciprocal(o, r)
    prog.add("vector", f_sig)
    lnall = pub(prog, s, "vector")
    csem = ctx["new_coll_sem"]()
    prog.wait("gpsimd", stG.ctr, stG.ctr.v)
    def f_coll(e, h=csem.h):
        e.collective_compute("AllGather", OP.bypass, replica_groups=PAIR_GROUPS,
                             ins=[dint["lnmy"]], outs=[dint["ln_ag"]]).then_inc(h, 1)
    prog.add("gpsimd", f_coll)
    csem.bump()
    prog.wait("sync", csem)

    prog.wait("vector", stG.ctr, stG.ctr.v)
    rt = stW.group("sync", [(g0[:, 0:1024], din["b_ropeA"]),
                            (g0[:, 1024:2048], din["b_ropeB"])])
    prog.wait("vector", stW.ctr, rt)
    qtm = g0[:, 2048:4096]

    def rope_evac(ps_ap, dst_ap, gt, nh):
        t_all = g0[:, 4096:4608]; t_cr = g0[:, 4608:5120]
        def f_r(e, ps=ps_ap, dst=dst_ap, ta=t_all[:, 0:nh*64], tc=t_cr[:, 0:nh*64],
                tA=g0[:, gt*64:(gt+1)*64], tB=g0[:, 1024+gt*64:1024+(gt+1)*64], nh=nh):
            v3 = lambda ap: ap.rearrange("p (h f) -> p h f", f=64)
            e.tensor_tensor(v3(ta), v3(ps), tA.unsqueeze(1).broadcast_to([128, nh, 64]), OP.mult)
            e.tensor_tensor(v3(tc), v3(ps), tB.unsqueeze(1).broadcast_to([128, nh, 64]), OP.mult)
            e.drain()
            ta3 = v3(ta); tc3 = v3(tc); d3 = v3(dst)
            e.tensor_tensor(d3[:, :, 0:32], ta3[:, :, 0:32], ta3[:, :, 32:64], OP.subtract)
            e.tensor_tensor(d3[:, :, 32:64], tc3[:, :, 0:32], tc3[:, :, 32:64], OP.add)
            e.drain()
        prog.add("vector", f_r)

    for gi in range(4):
        for tl in range(4):
            lv = stW.group("sync", [(g1[:, 0:2048],
                                     dint["ln_ag"][gi*512 + tl*128: gi*512 + (tl+1)*128, :])])
            prog.wait("tensor", stW.ctr, lv)
            if gi == 0 and tl == 0:
                prog.wait("tensor", s["svec"], lnall)
            for half in range(2):
                srcs = [g1[:, kc*128:(kc+1)*128] for kc in range(half*8, half*8+8)]
                dsts = [lnfm(kc)[:, tl*128:(tl+1)*128] for kc in range(half*8, half*8+8)]
                cv = transpose_block(ctx, srcs, dsts, ps1[:, 0:1024])
                prog.wait("tensor", s["svec"], cv)
            stW.consumed(s["svec"], cv)
        for fg in range(2):
            for dc in range(NDC):
                sl = rotA.i % 4
                wslot = wbuf[:, sl*512:sl*512+512]
                wc, wv = rotA.load("sync", [(wslot,
                    din["b_Wq_my"][dc*128:(dc+1)*128, fg*512:(fg+1)*512])])
                prog.wait("tensor", wc, wv)
                fns = []
                for tl in range(4):
                    def f_mm(e, d=ps0[:, tl*512:(tl+1)*512], l=lnfm(dc)[:, tl*128:(tl+1)*128],
                             r=wslot, st_=(dc == 0), sp=(dc == NDC-1)):
                        return e.matmul(d, l, r, start=st_, stop=sp)
                    fns.append(f_mm)
                mmv = tensor_group(prog, s, fns)
                rotA.consumed(s["smm"], mmv)
            prog.wait("vector", s["smm"], mmv)
            for tl in range(4):
                rope_evac(ps0[:, tl*512:(tl+1)*512], qtm[:, tl*512:(tl+1)*512],
                          gi*4 + tl, 8)
            qv = pub(prog, s, "vector")
            prog.wait("tensor", s["svec"], qv)
            for tl in range(4):
                srcs = [qtm[:, tl*512 + i*128: tl*512 + (i+1)*128] for i in range(4)]
                dsts = [q_slot(fg*4 + i)[:, (gi*4+tl)*128:(gi*4+tl+1)*128] for i in range(4)]
                cv = transpose_block(ctx, srcs, dsts, ps1[:, 0:512])
                prog.wait("tensor", s["svec"], cv)
        for dc in range(NDC):
            sl = rotA.i % 4
            wslot = wbuf[:, sl*512:sl*512+256]
            wc, wv = rotA.load("sync", [(wslot, din["b_Wk_my"][dc*128:(dc+1)*128, :])])
            prog.wait("tensor", wc, wv)
            fns = []
            for tl in range(4):
                def f_mm(e, d=ps0[:, tl*512:tl*512+256], l=lnfm(dc)[:, tl*128:(tl+1)*128],
                         r=wslot, st_=(dc == 0), sp=(dc == NDC-1)):
                    return e.matmul(d, l, r, start=st_, stop=sp)
                fns.append(f_mm)
            mmv = tensor_group(prog, s, fns)
            rotA.consumed(s["smm"], mmv)
        prog.wait("vector", s["smm"], mmv)
        for tl in range(4):
            rope_evac(ps0[:, tl*512:tl*512+256], qtm[:, tl*256:(tl+1)*256], gi*4 + tl, 4)
        kv_ = pub(prog, s, "vector")
        prog.wait("tensor", s["svec"], kv_)
        for tl in range(4):
            srcs = [qtm[:, tl*256 + i*128: tl*256 + (i+1)*128] for i in range(2)]
            dsts = [kfm[:, i*2048 + (gi*4+tl)*128: i*2048 + (gi*4+tl+1)*128] for i in range(2)]
            cv = transpose_block(ctx, srcs, dsts, ps1[:, 0:256])
            prog.wait("tensor", s["svec"], cv)
        for fh in range(2):
            prog.wait("sync", s["smm"], s["smm"].v)
            wv = stW.group("sync", [(wbuf[:, 0:2048],
                din["b_Wv_my"].rearrange("(dc p) f -> p dc f", p=128)[:, :, fh*128:(fh+1)*128])])
            prog.wait("tensor", stW.ctr, wv)
            for tl in range(4):
                pv = ps1[:, (tl % 2)*1024: (tl % 2)*1024 + 128]
                fns = []
                for dc in range(NDC):
                    def f_mm(e, d=pv, l=lnfm(dc)[:, tl*128:(tl+1)*128],
                             r=wbuf[:, dc*128:(dc+1)*128],
                             st_=(dc == 0), sp=(dc == NDC-1)):
                        return e.matmul(d, l, r, start=st_, stop=sp)
                    fns.append(f_mm)
                mmv = tensor_group(prog, s, fns)
                prog.wait("vector", s["smm"], mmv)
                def f_cp(e, d=vtm[:, (gi*4+tl)*256 + fh*128: (gi*4+tl)*256 + fh*128 + 128], ss=pv):
                    e.tensor_copy(d, ss)
                prog.add("vector", f_cp)
                vv = pub(prog, s, "vector")
                prog.wait("tensor", s["svec"], vv)
            stW.consumed(s["svec"], vv)

    scoreA = g0[:, 0:2048]; scoreB = g0[:, 2048:4096]
    ctx_tm = g0[:, 4096:6144]
    kwork = g0[:, 6144:8192]
    attA = g1[:, 0:2048]; attB = g1[:, 2048:4096]
    lnrl = g1[:, 4096:6144]
    for hp in range(8):
        g = hp // 2
        prog.wait("vector", s["smm"], s["smm"].v)
        def f_kw(e, dst=kwork, src=kfm[:, (g//2)*2048:(g//2)*2048+2048], r0=(g % 2)*64):
            e.tensor_copy(dst[0:64, :], src[r0:r0+64, :])
            e.tensor_copy(dst[64:128, :], src[r0:r0+64, :])
            e.drain()
        prog.add("vector", f_kw)
        kwv = pub(prog, s, "vector")
        prog.wait("tensor", s["svec"], kwv)
        for qt in range(NQT):
            W = (qt + 1) * 128
            nkg = (W + 511) // 512
            fns = []
            for base, pps in ((0, ps0), (64, ps1)):
                for kg in range(nkg):
                    w = min(512, W - kg*512)
                    def f_mm(e, d=pps[:, kg*512:kg*512+w],
                             l=q_slot(hp)[base:base+64, qt*128:(qt+1)*128],
                             r=kwork[base:base+64, kg*512:kg*512+w]):
                        return e.matmul(d, l, r, start=True, stop=True)
                    fns.append(f_mm)
            mmv = tensor_group(prog, s, fns)
            prog.wait("vector", s["smm"], mmv)
            for head, (sc, pps) in enumerate(((scoreA, ps0), (scoreB, ps1))):
                def f_ev(e, sc=sc[:, 0:W], ps=pps[:, 0:W]):
                    e.tensor_scalar(sc, ps, 0.125, None, OP.mult, OP.bypass)
                prog.add("vector", f_ev)
                def f_dg(e, sc=sc[:, W-128:W], ps=pps[:, W-128:W], ca=causal):
                    e.drain()
                    e.scalar_tensor_tensor(sc, ps, 0.125, ca, OP.mult, OP.add)
                prog.add("vector", f_dg)
            def f_mx(e, a=scoreA[:, 0:W], b=scoreB[:, 0:W], na=nmax(0), nb=nmax(1)):
                e.drain()
                e.reduce_max(na, a, axis=AX.X, negate=True)
                e.reduce_max(nb, b, axis=AX.X, negate=True)
            prog.add("vector", f_mx)
            exv = pub(prog, s, "vector")
            prog.wait("tensor", s["svec"], exv)
            prog.wait("scalar", s["svec"], exv)
            def f_ex(e, a=scoreA[:, 0:W], b=scoreB[:, 0:W], na=nmax(0), nb=nmax(1),
                     sa=rsum(0), sb=rsum(1)):
                e.activation(a, a, AF.Exp, bias=na, accum_out=sa)
                e.activation(b, b, AF.Exp, bias=nb, accum_out=sb)
            prog.add("scalar", f_ex)
            exs = pub(prog, s, "scalar")
            prog.wait("vector", s["ssc"], exs)
            def f_rc(e, sa=rsum(0), sb=rsum(1)):
                e.reciprocal(sa, sa)
                e.reciprocal(sb, sb)
                e.drain()
            prog.add("vector", f_rc)
            rcv = pub(prog, s, "vector")
            prog.wait("tensor", s["svec"], rcv)
            nkc = qt + 1
            cvA = transpose_block(ctx, [scoreA[:, kc*128:(kc+1)*128] for kc in range(nkc)],
                                  [attA[:, kc*128:(kc+1)*128] for kc in range(nkc)],
                                  ps0[:, 0:nkc*128])
            prog.wait("tensor", s["svec"], cvA)
            cvB = transpose_block(ctx, [scoreB[:, kc*128:(kc+1)*128] for kc in range(nkc)],
                                  [attB[:, kc*128:(kc+1)*128] for kc in range(nkc)],
                                  ps0[:, 0:nkc*128])
            prog.wait("tensor", s["svec"], cvB)
            fns = []
            for kc in range(nkc):
                def f_c1(e, d=ps1[0:64, 0:128],
                         l=vtm[:, kc*256 + g*64: kc*256 + g*64 + 64],
                         r=attA[:, kc*128:(kc+1)*128], st_=(kc == 0), sp=(kc == nkc-1)):
                    return e.matmul(d, l, r, start=st_, stop=sp)
                fns.append(f_c1)
            for kc in range(nkc):
                def f_c2(e, d=ps1[0:64, 512:640],
                         l=vtm[:, kc*256 + g*64: kc*256 + g*64 + 64],
                         r=attB[:, kc*128:(kc+1)*128], st_=(kc == 0), sp=(kc == nkc-1)):
                    return e.matmul(d, l, r, start=st_, stop=sp)
                fns.append(f_c2)
            mmc = tensor_group(prog, s, fns)
            prog.wait("vector", s["smm"], mmc)
            def f_cc(e, dA=scoreA[0:64, 0:128], sA=ps1[0:64, 0:128],
                     dB=scoreA[64:128, 0:128], sB=ps1[0:64, 512:640]):
                e.tensor_copy(dA, sA)
                e.tensor_copy(dB, sB)
                e.drain()
            prog.add("vector", f_cc)
            ccv = pub(prog, s, "vector")
            prog.wait("tensor", s["svec"], ccv)
            cvt = transpose_block(ctx, [scoreA[:, 0:128]], [ctx_tm[:, qt*128:(qt+1)*128]],
                                  ps1[:, 1024:1152])
            prog.wait("vector", s["svec"], cvt)
            def f_sc(e, t=ctx_tm[:, qt*128:(qt+1)*128], sa=rsum(0), sb=rsum(1)):
                e.tensor_scalar(t[:, 0:64], t[:, 0:64], sa, None, OP.mult, OP.bypass)
                e.tensor_scalar(t[:, 64:128], t[:, 64:128], sb, None, OP.mult, OP.bypass)
                e.drain()
            prog.add("vector", f_sc)
            scv = pub(prog, s, "vector")
            prog.wait("tensor", s["svec"], scv)

        prog.wait("sync", s["svec"], s["svec"].v)
        rotB.reset(s["svec"], s["svec"].v)
        lv = stW.group("sync", [(lnrl, dint["lnmy"][hp*128:(hp+1)*128, :])])
        prog.wait("vector", stW.ctr, lv)
        for qc in range(NQT):
            sl = rotB.i % 2
            wsl = g1[:, sl*2048:(sl+1)*2048]
            wc, wv = rotB.load("sync", [(wsl, din["b_Wo"][qc*128:(qc+1)*128, :])])
            prog.wait("tensor", wc, wv)
            fns = []
            for j in range(4):
                def f_mm(e, d=ps1[:, j*512:(j+1)*512], l=ctx_tm[:, qc*128:(qc+1)*128],
                         r=wsl[:, j*512:(j+1)*512], st_=(qc == 0), sp=(qc == NQT-1)):
                    return e.matmul(d, l, r, start=st_, stop=sp)
                fns.append(f_mm)
            mmo = tensor_group(prog, s, fns)
            rotB.consumed(s["smm"], mmo)
        prog.wait("vector", s["smm"], mmo)
        def f_xn(e, ps=ps1[:, 0:2048], ln=lnrl, dst=q_slot(hp),
                 sg=sg_c[:, hp:hp+1], mu=mu_c[:, hp:hp+1]):
            e.tensor_scalar(dst, ln, sg, mu, OP.mult, OP.add)
            e.drain()
            e.tensor_add(dst, dst, ps)
            e.drain()
        prog.add("vector", f_xn)
        xnv = pub(prog, s, "vector")
        prog.wait("tensor", s["svec"], xnv)


def transpose_block(ctx, srcs, dsts, ps_ap, width=128):
    prog, s = ctx["prog"], ctx["sems"]
    fns = []
    for i, src in enumerate(srcs):
        def f_t(e, src=src, d=ps_ap[:, i*width:(i+1)*width], idn=ctx["ident"]):
            return e.transpose(d, src, idn[:])
        fns.append(f_t)
    mmv = tensor_group(prog, s, fns)
    prog.wait("vector", s["smm"], mmv)
    for i, dst in enumerate(dsts):
        def f_cp(e, d=dst, ss=ps_ap[:, i*width:(i+1)*width]):
            e.tensor_copy(d, ss)
        prog.add("vector", f_cp)
    return pub(prog, s, "vector")


def emit_D(ctx):
    prog, s = ctx["prog"], ctx["sems"]
    nc, din, dint = ctx["nc"], ctx["din"], ctx["dint"]
    x_sb, g0, g1, g2, g3, g4, wbuf, sm, smi = (ctx[k] for k in
        ("x_sb", "g0", "g1", "g2", "g3", "g4", "wbuf", "sm", "smi"))
    ps0, ps1 = ctx["ps0"], ctx["ps1"]
    stW, stG = ctx["stW"], ctx["stG"]
    H = 16
    NC = NT

    xnfm = lambda dc: (g0 if dc < 8 else g1)[:, (dc % 8)*1024:(dc % 8)*1024+1024]
    S_all = lambda h: g3[:, h*128:(h+1)*128]
    S_recv = lambda h: g3[:, 2048 + h*128:2048 + (h+1)*128]
    q_fm = g4[:, 0:1024]; k_fm = g4[:, 1024:2048]
    v_tm = g4[:, 2048:3072]; k_tm = g4[:, 3072:4096]
    Dt = sm[:, 0:128]; qdrep = sm[:, 128:256]
    ld_col = lambda h: sm[:, 256+h:256+h+1]
    kdec = sm[:, 280:281]; cdec = sm[:, 282:283]; cdpow = sm[:, 284:285]
    col4 = sm[:, 300:304]
    kdc_c = sm[:, 304:305]; flag_c = sm[:, 306:307]
    diffTm = sm[:, 320:448]; maskT = sm[:, 448:576]; qdr_c = sm[:, 576:704]
    og_all = g2[:, 4096:5120]
    attnT = g2[:, 5120:5248]; qsc = g2[:, 5248:5376]
    ktil = g2[:, 5376:5504]
    t_fm = g2[:, 5504:6528]
    gate_h = g2[:, 6528:7552]
    mscol = sm[:, 704:712]; rstdc = sm[:, 712:720]
    mu_c = sm[:, 720:728]; sg_c = sm[:, 728:736]
    nw_h = sm[:, 736:864]
    onorm_fm = lambda hi: g2[:, hi*1024:(hi+1)*1024]

    cl = stW.group("sync", [
        (sm[:, 256:272], din["d_ld_rep"]),
        (kdc_c, din["d_kdc"]),
        (flag_c, din["d_flag"]),
        (diffTm, din["d_diffTm"]),
        (maskT, din["d_maskT"]),
        (qdr_c, din["d_qdr"]),
        (smi[:, 0:1], din["d_srow"]),
    ])
    prog.wait("vector", stW.ctr, cl)
    prog.wait("scalar", stW.ctr, cl)
    prog.wait("gpsimd", stW.ctr, cl)

    prog.wait("vector", s["smm"], s["smm"].v)
    prog.wait("vector", s["ssc"], s["ssc"].v)
    for t in range(NT):
        xt = x_sb[:, t*D:(t+1)*D]
        lnv = ln_tile(ctx, xt, g4[:, 0:D], g4[:, D:2*D], col4,
                      save_musig=(mu_c[:, t:t+1], rstdc[:, t:t+1]))
        prog.wait("tensor", s["svec"], lnv)
        for half in range(2):
            srcs = [g4[:, kc*128:(kc+1)*128] for kc in range(half*8, half*8+8)]
            dsts = [xnfm(kc)[:, t*128:(t+1)*128] for kc in range(half*8, half*8+8)]
            cv = transpose_block(ctx, srcs, dsts, ps1[:, 0:1024])
            prog.wait("tensor", s["svec"], cv)
    prog.wait("vector", s["ssc"], s["ssc"].v)
    def f_sig(e, r=rstdc[:, 0:NT], o=sg_c[:, 0:NT]):
        e.reciprocal(o, r)
        e.drain()
        e.memset(x_sb[:], 0.0)
    prog.add("vector", f_sig)
    xn_done = pub(prog, s, "vector")
    stW.consumed(s["svec"], xn_done)

    gv = stW.group("sync", [(wbuf[:, 0:2048],
                             din["d_Wg1"].rearrange("(dc p) f -> p dc f", p=128))])
    prog.wait("tensor", stW.ctr, gv)
    prog.wait("tensor", s["svec"], xn_done)
    fns = []
    for tg in range(2):
        for dc in range(NDC):
            def f_mm(e, d=ps0[:, tg*512:(tg+1)*512], l=wbuf[:, dc*128:(dc+1)*128],
                     r=xnfm(dc)[:, tg*512:(tg+1)*512], st_=(dc == 0), sp=(dc == NDC-1)):
                return e.matmul(d, l, r, start=st_, stop=sp)
            fns.append(f_mm)
    mmv = tensor_group(prog, s, fns)
    stW.consumed(s["smm"], mmv)
    prog.wait("vector", s["smm"], mmv)
    prog.add("vector", lambda e: e.tensor_copy(t_fm, ps0[:, 0:1024]))
    tfm_done = pub(prog, s, "vector")
    prog.wait("tensor", s["svec"], tfm_done)

    for h in range(H):
        prog.wait("sync", s["smm"], s["smm"].v)
        wv = stW.group("sync", [(wbuf[:, 0:2048],
            din["d_Wk"].rearrange("(dc p) f -> p dc f", p=128)[:, :, h*128:(h+1)*128])])
        prog.wait("tensor", stW.ctr, wv)
        if h > 0:
            prog.wait("scalar", stG.ctr, stG.ctr.v)
        for tg in range(2):
            fns = []
            for dc in range(NDC):
                def f_mm(e, d=ps0[:, tg*512:(tg+1)*512], l=wbuf[:, dc*128:(dc+1)*128],
                         r=xnfm(dc)[:, tg*512:(tg+1)*512], st_=(dc == 0), sp=(dc == NDC-1)):
                    return e.matmul(d, l, r, start=st_, stop=sp)
                fns.append(f_mm)
            mmv = tensor_group(prog, s, fns)
            prog.wait("scalar", s["smm"], mmv)
            def f_silu(e, d=k_fm[:, tg*512:(tg+1)*512], ss=ps0[:, tg*512:(tg+1)*512]):
                e.activation(d, ss, AF.Silu)
            prog.add("scalar", f_silu)
        kfm_done = pub(prog, s, "scalar")
        stW.consumed(s["ssc"], kfm_done)

        wv = stW.group("sync", [(wbuf[:, 0:2048],
            din["d_Wv"].rearrange("(dc p) f -> p dc f", p=128)[:, :, h*128:(h+1)*128])])
        prog.wait("tensor", stW.ctr, wv)
        prog.wait("tensor", s["ssc"], kfm_done)
        for tg in range(2):
            fns = []
            for dc in range(NDC):
                def f_mm(e, d=ps0[:, tg*512:(tg+1)*512], l=wbuf[:, dc*128:(dc+1)*128],
                         r=xnfm(dc)[:, tg*512:(tg+1)*512], st_=(dc == 0), sp=(dc == NDC-1)):
                    return e.matmul(d, l, r, start=st_, stop=sp)
                fns.append(f_mm)
            mmv = tensor_group(prog, s, fns)
            prog.wait("scalar", s["smm"], mmv)
            def f_silu(e, d=q_fm[:, tg*512:(tg+1)*512], ss=ps0[:, tg*512:(tg+1)*512]):
                e.activation(d, ss, AF.Silu)
            prog.add("scalar", f_silu)
        vfm_done = pub(prog, s, "scalar")
        stW.consumed(s["ssc"], vfm_done)
        prog.wait("tensor", s["ssc"], vfm_done)
        cv = transpose_block(ctx, [q_fm[:, c*128:(c+1)*128] for c in range(8)],
                             [v_tm[:, c*128:(c+1)*128] for c in range(8)], ps1[:, 0:1024])
        cv2 = transpose_block(ctx, [k_fm[:, c*128:(c+1)*128] for c in range(8)],
                              [k_tm[:, c*128:(c+1)*128] for c in range(8)], ps1[:, 1024:2048])
        prog.wait("gpsimd", s["svec"], cv2)
        stG.group("gpsimd", [
            (dint["kvpark"][(2*h)*128:(2*h+1)*128, :], k_fm),
            (dint["kvpark"][(2*h+1)*128:(2*h+2)*128, :], q_fm),
        ])
        def f_kd(e, ldc=ld_col(h), o=kdec, kc=kdc_c):
            e.tensor_scalar(o, kc, ldc, None, OP.mult, OP.bypass)
        prog.add("vector", f_kd)
        vv = pub(prog, s, "vector")
        prog.wait("scalar", s["svec"], vv)
        def f_kd2(e, o=kdec, ldc=ld_col(h), cd=cdec):
            e.activation(o, o, AF.Exp)
            e.activation(cd, ldc, AF.Exp, scale=128.0)
        prog.add("scalar", f_kd2)
        scv = pub(prog, s, "scalar")
        prog.wait("vector", s["ssc"], scv)
        prog.add("vector", lambda e, Sh=S_all(h): (e.memset(Sh, 0.0), e.drain()))
        for c in range(NC):
            def f_kt(e, kt=k_tm[:, c*128:(c+1)*128], kd=kdec, o=ktil):
                e.tensor_scalar(o, kt, kd, None, OP.mult, OP.bypass)
            prog.add("vector", f_kt)
            kv_ = pub(prog, s, "vector")
            prog.wait("tensor", s["svec"], kv_)
            bank = ps1[:, (c % 2)*512:(c % 2)*512+128]
            mmv = tensor_group(prog, s, [
                lambda e, d=bank, l=ktil, r=v_tm[:, c*128:(c+1)*128]:
                e.matmul(d, l, r, start=True, stop=True)])
            prog.wait("vector", s["smm"], mmv)
            def f_su(e, Sh=S_all(h), cd=cdec, ps=bank):
                e.scalar_tensor_tensor(Sh, Sh, cd, ps, OP.mult, OP.add)
            prog.add("vector", f_su)
            sv_ = pub(prog, s, "vector")
            prog.wait("tensor", s["svec"], sv_)

    p1_done = pub(prog, s, "vector")
    prog.wait("gpsimd", s["svec"], p1_done)
    sdump = stG.group("gpsimd", [(dint["s_my"], g3[:, 0:2048])])
    prog.wait("gpsimd", stG.ctr, sdump)
    csem = ctx["new_coll_sem"]()
    def f_coll(e, h=csem.h):
        e.collective_compute("AllGather", OP.bypass, replica_groups=PAIR_GROUPS,
                             ins=[dint["s_my"]], outs=[dint["s_ag"]]).then_inc(h, 1)
    prog.add("gpsimd", f_coll)
    csem.bump()
    prog.wait("gpsimd", csem)
    srv = stG.group("gpsimd", pairs=[], indirect=[
        (g3[:, 2048:4096], None, dint["s_ag"],
         IndirectOffsetOnAxis(smi[:, 0:1], 0), OP.bypass)])
    prog.wait("tensor", stG.ctr, srv)
    prog.wait("vector", stG.ctr, srv)

    prog.wait("sync", stG.ctr, stG.ctr.v)
    for grp in range(4):
        for hi in range(4):
            h = grp*4 + hi
            prog.wait("sync", s["smm"], s["smm"].v)
            kv_l = stW.group("sync", [
                (k_fm, dint["kvpark"][(2*h)*128:(2*h+1)*128, :]),
                (q_fm, dint["kvpark"][(2*h+1)*128:(2*h+2)*128, :]),
                (nw_h, din["d_normw_rep"][:, h*128:(h+1)*128]),
                (wbuf[:, 0:128], din["d_Wg2"][:, h*128:(h+1)*128]),
            ])
            prog.wait("tensor", stW.ctr, kv_l)
            if h == 0:
                prog.wait("tensor", s["svec"], tfm_done)
            fns = []
            for tt in range(NT):
                def f_mm(e, d=ps0[:, (tt % 8)*128:(tt % 8)*128+128],
                         l=t_fm[:, tt*128:(tt+1)*128], r=wbuf[:, 0:128]):
                    return e.matmul(d, l, r, start=True, stop=True)
                fns.append(f_mm)
            mmg = tensor_group(prog, s, fns)
            prog.wait("scalar", s["smm"], mmg)
            def f_sg2(e, d=gate_h, ss=ps0[:, 0:1024]):
                e.activation(d, ss, AF.Sigmoid)
            prog.add("scalar", f_sg2)
            gsv = pub(prog, s, "scalar")
            prog.wait("tensor", s["ssc"], gsv)
            prog.wait("tensor", stW.ctr, kv_l)
            cv = transpose_block(ctx, [q_fm[:, c*128:(c+1)*128] for c in range(8)],
                                 [v_tm[:, c*128:(c+1)*128] for c in range(8)], ps1[:, 0:1024])
            cv2 = transpose_block(ctx, [k_fm[:, c*128:(c+1)*128] for c in range(8)],
                                  [k_tm[:, c*128:(c+1)*128] for c in range(8)], ps1[:, 1024:2048])
            stW.consumed(s["svec"], cv2)
            qw = stW.group("sync", [(wbuf[:, 0:2048],
                din["d_Wq"].rearrange("(dc p) f -> p dc f", p=128)[:, :, h*128:(h+1)*128])])
            prog.wait("tensor", stW.ctr, qw)
            prog.wait("tensor", s["svec"], cv)
            prog.wait("scalar", s["svec"], cv)
            for tg in range(2):
                fns = []
                for dc in range(NDC):
                    def f_mm(e, d=ps0[:, tg*512:(tg+1)*512], l=wbuf[:, dc*128:(dc+1)*128],
                             r=xnfm(dc)[:, tg*512:(tg+1)*512], st_=(dc == 0), sp=(dc == NDC-1)):
                        return e.matmul(d, l, r, start=st_, stop=sp)
                    fns.append(f_mm)
                mmv = tensor_group(prog, s, fns)
                prog.wait("scalar", s["smm"], mmv)
                def f_silu(e, d=q_fm[:, tg*512:(tg+1)*512], ss=ps0[:, tg*512:(tg+1)*512]):
                    e.activation(d, ss, AF.Silu)
                prog.add("scalar", f_silu)
            qfm_done = pub(prog, s, "scalar")
            stW.consumed(s["ssc"], qfm_done)
            prog.wait("vector", s["ssc"], qfm_done)
            def f_dt(e, ldc=ld_col(h), o=Dt, df=diffTm, qr=qdrep, qc=qdr_c, kd=kdec, kc=kdc_c):
                e.tensor_scalar(o, df, ldc, None, OP.mult, OP.bypass)
                e.tensor_scalar(qr, qc, ldc, None, OP.mult, OP.bypass)
                e.tensor_scalar(kd, kc, ldc, None, OP.mult, OP.bypass)
            prog.add("vector", f_dt)
            dv_ = pub(prog, s, "vector")
            prog.wait("scalar", s["svec"], dv_)
            def f_dt2(e, o=Dt, qr=qdrep, kd=kdec, ldc=ld_col(h), cd=cdec):
                e.activation(o, o, AF.Exp)
                e.activation(qr, qr, AF.Exp)
                e.activation(kd, kd, AF.Exp)
                e.activation(cd, ldc, AF.Exp, scale=128.0)
            prog.add("scalar", f_dt2)
            dsv = pub(prog, s, "scalar")
            prog.wait("vector", s["ssc"], dsv)
            def f_dt3(e, o=Dt, mk=maskT, Sh=S_all(h), Sr=S_recv(h), fl=flag_c):
                e.tensor_mul(o, o, mk)
                e.tensor_scalar(Sh, Sr, fl, None, OP.mult, OP.bypass)
                e.drain()
            prog.add("vector", f_dt3)
            dmv = pub(prog, s, "vector")
            prog.wait("tensor", s["svec"], dmv)

            for c in range(NC):
                pS = c % 2
                b_sc = ps0[:, 1024 + pS*512: 1024 + pS*512 + 128]
                b_kv = ps1[:, pS*512: pS*512 + 128]
                b_o = ps1[:, 1024 + pS*512: 1024 + pS*512 + 128]
                mm_sc = tensor_group(prog, s, [
                    lambda e, d=b_sc, l=k_fm[:, c*128:(c+1)*128], r=q_fm[:, c*128:(c+1)*128]:
                    e.matmul(d, l, r, start=True, stop=True)])
                prog.wait("vector", s["smm"], mm_sc)
                def f_at(e, o=attnT, ps=b_sc, dt=Dt, kt=k_tm[:, c*128:(c+1)*128],
                         kd=kdec, kk=ktil, qf=q_fm[:, c*128:(c+1)*128], qd=qdrep, qq=qsc):
                    e.tensor_mul(o, ps, dt)
                    e.tensor_scalar(kk, kt, kd, None, OP.mult, OP.bypass)
                    e.tensor_mul(qq, qf, qd)
                prog.add("vector", f_at)
                at_v = pub(prog, s, "vector")
                prog.wait("tensor", s["svec"], at_v)
                mm_o3 = tensor_group(prog, s, [
                    lambda e, d=b_o, l=attnT, r=v_tm[:, c*128:(c+1)*128]:
                    e.matmul(d, l, r, start=True, stop=False),
                    lambda e, d=b_o, l=qsc, r=S_all(h):
                    e.matmul(d, l, r, start=False, stop=True),
                    lambda e, d=b_kv, l=ktil, r=v_tm[:, c*128:(c+1)*128]:
                    e.matmul(d, l, r, start=True, stop=True),
                ])
                prog.wait("vector", s["smm"], mm_o3)
                def f_su(e, Sh=S_all(h), cd=cdec, ps=b_kv, po=b_o,
                         gs=gate_h[:, c*128:(c+1)*128],
                         og=og_all[:, c*128:(c+1)*128]):
                    e.scalar_tensor_tensor(Sh, Sh, cd, ps, OP.mult, OP.add)
                    e.tensor_mul(og, po, gs)
                prog.add("vector", f_su)
                su_v = pub(prog, s, "vector")
                prog.wait("tensor", s["svec"], su_v)

            prog.wait("scalar", s["svec"], s["svec"].v)
            for c in range(NC):
                def f_sq(e, og=og_all[:, c*128:(c+1)*128], sq=attnT, ms=mscol[:, c:c+1]):
                    e.activation(sq, og, AF.Square, accum_out=ms)
                    e.drain()
                prog.add("scalar", f_sq)
            sqv = pub(prog, s, "scalar")
            prog.wait("vector", s["ssc"], sqv)
            def f_rs(e, ms=mscol[:, 0:8], o=rstdc[:, 0:8]):
                e.tensor_scalar(o, ms, 1.0/128.0, 1e-5, OP.mult, OP.add)
            prog.add("vector", f_rs)
            rsv = pub(prog, s, "vector")
            prog.wait("scalar", s["svec"], rsv)
            def f_rs2(e, o=rstdc[:, 0:8]):
                e.activation(o, o, AF.Ln)
                e.drain()
                e.activation(o, o, AF.Exp, scale=-0.5)
            prog.add("scalar", f_rs2)
            rs2v = pub(prog, s, "scalar")
            prog.wait("vector", s["ssc"], rs2v)
            def f_on(e, og=og_all.rearrange("p (c k) -> p c k", k=128),
                     rs=rstdc[:, 0:8].unsqueeze(2).broadcast_to([128, 8, 128]),
                     nw=nw_h.unsqueeze(1).broadcast_to([128, 8, 128])):
                e.tensor_tensor(og, og, rs, OP.mult)
                e.drain()
                e.tensor_tensor(og, og, nw, OP.mult)
            prog.add("vector", f_on)
            onv = pub(prog, s, "vector")
            prog.wait("tensor", s["svec"], onv)
            cvo = transpose_block(ctx, [og_all[:, c*128:(c+1)*128] for c in range(8)],
                                  [onorm_fm(hi)[:, c*128:(c+1)*128] for c in range(8)],
                                  ps1[:, 0:1024])
            prog.wait("tensor", s["svec"], cvo)

        for j in range(4):
            mmv = None
            for hi in range(4):
                h = grp*4 + hi
                ww = stW.group("sync", [(wbuf[:, hi*512:(hi+1)*512],
                                         din["d_Wo"][h*128:(h+1)*128, j*512:(j+1)*512])])
                prog.wait("tensor", stW.ctr, ww)
                fns = []
                for tt in range(NT):
                    pst = (ps0 if tt < 4 else ps1)[:, (tt % 4)*512:(tt % 4)*512+512]
                    def f_mm(e, d=pst, l=onorm_fm(hi)[:, tt*128:(tt+1)*128],
                             r=wbuf[:, hi*512:(hi+1)*512], st_=(hi == 0), sp=(hi == 3)):
                        return e.matmul(d, l, r, start=st_, stop=sp)
                    fns.append(f_mm)
                mmv = tensor_group(prog, s, fns)
                if hi == 3:
                    stW.consumed(s["smm"], mmv)
            prog.wait("vector", s["smm"], mmv)
            for tt in range(NT):
                pst = (ps0 if tt < 4 else ps1)[:, (tt % 4)*512:(tt % 4)*512+512]
                def f_acc(e, xo=x_sb[:, tt*D + j*512: tt*D + (j+1)*512], ps=pst):
                    e.tensor_add(xo, xo, ps)
                    e.drain()
                prog.add("vector", f_acc)
            av = pub(prog, s, "vector")
            prog.wait("tensor", s["svec"], av)

    for tt in range(NT):
        fns = []
        for dc in range(NDC):
            def f_t(e, src=xnfm(dc)[:, tt*128:(tt+1)*128], d=ps0[:, dc*128:(dc+1)*128],
                    idn=ctx["ident"]):
                return e.transpose(d, src, idn[:])
            fns.append(f_t)
        mmv = tensor_group(prog, s, fns)
        prog.wait("vector", s["smm"], mmv)
        def f_fin(e, ps=ps0[:, 0:2048], xo=x_sb[:, tt*D:(tt+1)*D],
                  sg=sg_c[:, tt:tt+1], mu=mu_c[:, tt:tt+1], tmp=g4[:, 0:2048]):
            e.tensor_scalar(tmp, ps, sg, mu, OP.mult, OP.add)
            e.drain()
            e.tensor_add(xo, xo, tmp)
        prog.add("vector", f_fin)
        fv = pub(prog, s, "vector")
        prog.wait("tensor", s["svec"], fv)



def make_inputs(core, full, sublayers=None):
    if sublayers is None:
        sublayers = SUBLAYERS
    r, h = core // 2, core % 2
    f32 = lambda k: np.ascontiguousarray(np.asarray(full[k], np.float32))
    rep = lambda v: np.ascontiguousarray(np.broadcast_to(
        np.asarray(v, np.float32), (128, len(np.asarray(v).reshape(-1)))))
    X = np.asarray(full["L0_attn_x"], dtype=np.float32)
    m = {
        "x_in": np.ascontiguousarray(X[r, h*TC:(h+1)*TC, :]),
        "ident": np.eye(128, dtype=np.float32),
    }
    if "C" in sublayers or "C2" in sublayers:
        m["c_weight"] = f32("L1_ffn_weight")
        m["c_bias_rep"] = rep(full["L1_ffn_bias"])
    if "A" in sublayers or "A2" in sublayers:
        h = core % 2
        m["a_wq"] = f32("L0_ffn_wq"); m["a_wk"] = f32("L0_ffn_wk")
        m["a_wv"] = f32("L0_ffn_wv"); m["a_wo"] = f32("L0_ffn_wo")
        m["a_w1"] = f32("L0_ffn_w1"); m["a_w3"] = f32("L0_ffn_w3")
        m["a_w2"] = f32("L0_ffn_w2")
        m["a_g1_rep"] = rep(full["L0_ffn_g1"]); m["a_g2_rep"] = rep(full["L0_ffn_g2"])
        m["a_wrt_rep"] = rep(full["L0_ffn_w_router"])
        msk = np.asarray(full["L0_ffn_mask"], np.float32)
        m["a_mask_my"] = np.ascontiguousarray(msk[h*SELC:(h+1)*SELC, :])
        fc = np.asarray(full["L0_ffn_freqs_cis"], np.float32)[h*SELC:(h+1)*SELC]
        cosf, sinf = fc[:, :, 0], fc[:, :, 1]
        tabA = np.concatenate([cosf, sinf], 1).reshape(4, 128, 128)
        tabB = np.concatenate([sinf, cosf], 1).reshape(4, 128, 128)
        m["a_ropeA"] = np.ascontiguousarray(tabA.transpose(1, 0, 2).reshape(128, 512))
        m["a_ropeB"] = np.ascontiguousarray(tabB.transpose(1, 0, 2).reshape(128, 512))
        m["a_iota"] = np.ascontiguousarray(
            np.broadcast_to(np.arange(T, dtype=np.float32)[None, :], (128, T)))
        pp = np.arange(128, dtype=np.float32)
        m["a_idxcol"] = np.ascontiguousarray(
            (np.arange(16, dtype=np.float32)[None, :]*128 + pp[:, None]))
        m["a_rcol"] = np.ascontiguousarray(
            (h*SELC + np.arange(4, dtype=np.float32)[None, :]*128 + pp[:, None]))
        m["a_ones1"] = np.ones((1, 128), np.float32)
        m["a_halfflag"] = np.full((128, 1), float(h), np.float32)
    if "B" in sublayers:
        h0col = (core % 2) * 1024
        m["b_Wq_my"] = np.ascontiguousarray(np.asarray(full["L0_attn_Wq"], np.float32)[:, h0col:h0col+1024])
        g0col = (core % 2) * 256
        m["b_Wk_my"] = np.ascontiguousarray(np.asarray(full["L0_attn_Wk"], np.float32)[:, g0col:g0col+256])
        m["b_Wv_my"] = np.ascontiguousarray(np.asarray(full["L0_attn_Wv"], np.float32)[:, g0col:g0col+256])
        m["b_Wo"] = f32("L0_attn_Wo")
        inv = (1.0 / 10000.0 ** (np.arange(32, dtype=np.float64) * 2.0 / 64.0))
        ang = np.arange(T, dtype=np.float64)[:, None] * inv[None, :]
        cosr = np.cos(ang).astype(np.float32).reshape(16, 128, 32)
        sinr = np.sin(ang).astype(np.float32).reshape(16, 128, 32)
        ra = np.concatenate([cosr, sinr], -1)
        rb = np.concatenate([sinr, cosr], -1)
        m["b_ropeA"] = np.ascontiguousarray(ra.transpose(1, 0, 2).reshape(128, 1024))
        m["b_ropeB"] = np.ascontiguousarray(rb.transpose(1, 0, 2).reshape(128, 1024))
        i_ = np.arange(128)
        m["b_causal"] = np.where(i_[None, :] <= i_[:, None], 0.0, -1e30).astype(np.float32)
    if "D" in sublayers:
        m["d_Wq"] = f32("L1_attn_Wq"); m["d_Wk"] = f32("L1_attn_Wk")
        m["d_Wv"] = f32("L1_attn_Wv"); m["d_Wo"] = f32("L1_attn_Wo")
        m["d_Wg1"] = f32("L1_attn_Wg1"); m["d_Wg2"] = f32("L1_attn_Wg2")
        m["d_normw_rep"] = rep(full["L1_attn_norm_w"])
        m["d_ld_rep"] = rep(full["L1_attn_log_decay"])
        i_ = np.arange(128, dtype=np.float32)
        cmi = i_[None, :] - i_[:, None]
        m["d_diffTm"] = np.where(cmi >= 0, cmi, 0.0).astype(np.float32)
        m["d_maskT"] = (cmi >= 0).astype(np.float32)
        m["d_kdc"] = np.ascontiguousarray((127.0 - i_)[:, None].astype(np.float32))
        m["d_qdr"] = np.ascontiguousarray(
            np.broadcast_to((i_ + 1.0)[None, :], (128, 128)).astype(np.float32))
        m["d_flag"] = np.full((128, 1), float(h), dtype=np.float32)
        m["d_srow"] = np.ascontiguousarray(
            ((1 - h) * 128 + np.arange(128, dtype=np.int32))[:, None])
    return m


def kernel(**inputs):
    nc = build_nc()
    in_maps = [make_inputs(c, inputs, SUBLAYERS) for c in range(8)]
    res = run_bass_kernel_spmd(nc, in_maps, core_ids=list(range(8)))
    out = np.empty((B, T, D), dtype=np.float32)
    for c in range(8):
        r, h = c // 2, c % 2
        out[r, h*TC:(h+1)*TC, :] = res.results[c]["y"]
    return out


# version 04
# speedup vs baseline: 1.0099x; 1.0099x over previous
import math
import numpy as np
from contextlib import ExitStack

import concourse.bass as bass
import concourse.mybir as mybir
from concourse.bass_utils import run_bass_kernel_spmd
from concourse.bass_core_rust import IndirectOffsetOnAxis

F32 = mybir.dt.float32
I32 = mybir.dt.int32
AF = mybir.ActivationFunctionType
OP = mybir.AluOpType
AX = mybir.AxisListType

B, T, D = 4, 2048, 2048
TC = 1024
NT = TC // 128
NDC = D // 128
FFN = 8192
SEL = 1024
SELC = SEL // 2
LN2 = 0.6931471805599453

SUBLAYERS = ["A", "B", "A2", "C", "D", "C2"]
PAIR_GROUPS = [[0, 1], [2, 3], [4, 5], [6, 7]]


class Ctr:
    def __init__(self, handle):
        self.h = handle
        self.v = 0

    def bump(self, n=1):
        self.v += n
        return self.v


class Prog:
    ENGINES = ("sync", "tensor", "vector", "scalar", "gpsimd")

    def __init__(self, nc):
        self.nc = nc
        self.q = {e: [] for e in self.ENGINES}

    def add(self, eng, fn):
        self.q[eng].append(fn)

    def wait(self, eng, ctr, val=None):
        v = ctr.v if val is None else val
        h = ctr.h
        self.add(eng, lambda e, h=h, v=v: e.wait_ge(h, v))

    def run(self, block):
        for eng in self.ENGINES:
            items = self.q[eng]
            if not items:
                continue
            def body(e, items=items):
                for fn in items:
                    fn(e)
            getattr(block, eng)(body)


class Stream:
    def __init__(self, prog, ctr):
        self.prog = prog
        self.ctr = ctr
        self.gate = None
        self.dirty = False

    def group(self, eng, pairs, indirect=None):
        if self.gate is not None:
            self.prog.wait(eng, self.gate[0], self.gate[1])
        elif self.dirty:
            self.prog.wait(eng, self.ctr, self.ctr.v)
        self.gate = None
        h = self.ctr.h
        for out_ap, in_ap in pairs:
            def fn(e, o=out_ap, i=in_ap, h=h):
                e.dma_start(out=o, in_=i).then_inc(h, 16)
            self.prog.add(eng, fn)
            self.ctr.v += 16
        if indirect:
            for out_ap, out_off, in_ap, in_off, cop in indirect:
                def fn(e, o=out_ap, oo=out_off, i=in_ap, io=in_off, c=cop, h=h):
                    e.indirect_dma_start(o, oo, i, io, compute_op=c).then_inc(h, 16)
                self.prog.add(eng, fn)
                self.ctr.v += 16
        self.dirty = True
        return self.ctr.v

    def consumed(self, ctr, val):
        self.gate = (ctr, val)
        self.dirty = False


class Rot:
    def __init__(self, prog, ctrs):
        self.prog = prog
        self.ctrs = ctrs
        self.n = len(ctrs)
        self.hist = {}
        self.i = 0

    def load(self, eng, pairs):
        slot = self.i % self.n
        c = self.ctrs[slot]
        g = self.hist.get(slot)
        if g is not None:
            self.prog.wait(eng, g[0], g[1])
        elif c.v > 0:
            self.prog.wait(eng, c, c.v)
        for out_ap, in_ap in pairs:
            def fn(e, o=out_ap, i=in_ap, h=c.h):
                e.dma_start(out=o, in_=i).then_inc(h, 16)
            self.prog.add(eng, fn)
            c.v += 16
        self.i += 1
        return (c, c.v)

    def consumed(self, ctr, val, back=0):
        slot = (self.i - 1 - back) % self.n
        self.hist[slot] = (ctr, val)

    def reset(self, ctr, val):
        for sl in range(self.n):
            self.hist[sl] = (ctr, val)


def pub(prog, s, eng):
    key = {"vector": "svec", "scalar": "ssc"}[eng]
    c = s[key]
    prog.add(eng, lambda e, h=c.h: e.drain().then_inc(h, 1))
    return c.bump()


def tensor_group(prog, s, fns):
    assert fns
    n = len(fns)
    for i, fn in enumerate(fns):
        if i == n - 1:
            def wrap(e, fn=fn, h=s["smm"].h):
                fn(e).then_inc(h, 1)
            prog.add("tensor", wrap)
        else:
            prog.add("tensor", fn)
    return s["smm"].bump()


def build_nc(sublayers=None):
    if sublayers is None:
        sublayers = SUBLAYERS
    nc = bass.Bass("TRN2", target_bir_lowering=False, debug=False)
    st = ExitStack()

    din = {}
    def I(name, shape, dt=F32):
        din[name] = nc.dram_tensor(name, list(shape), dt, kind="ExternalInput").ap()
        return din[name]

    I("x_in", (TC, D))
    I("ident", (128, 128))
    if "C" in sublayers or "C2" in sublayers:
        I("c_weight", (D, D))
        I("c_bias_rep", (128, D))
    if "A" in sublayers or "A2" in sublayers:
        I("a_wq", (D, D)); I("a_wk", (D, D)); I("a_wv", (D, D)); I("a_wo", (D, D))
        I("a_w1", (D, FFN)); I("a_w3", (D, FFN)); I("a_w2", (FFN, D))
        I("a_g1_rep", (128, D)); I("a_g2_rep", (128, D)); I("a_wrt_rep", (128, D))
        I("a_mask_my", (SELC, SEL))
        I("a_ropeA", (128, 512)); I("a_ropeB", (128, 512))
        I("a_iota", (128, T))
        I("a_idxcol", (128, 16))
        I("a_rcol", (128, 4))
        I("a_ones1", (1, 128))
        I("a_halfflag", (128, 1))
    if "B" in sublayers:
        I("b_Wq_my", (D, 1024))
        I("b_Wk_my", (D, 256)); I("b_Wv_my", (D, 256))
        I("b_Wo", (D, D))
        I("b_ropeA", (128, NQT_G * 64)) if False else None
        I("b_ropeA", (128, 1024))
        I("b_ropeB", (128, 1024))
        I("b_causal", (128, 128))
    if "D" in sublayers:
        I("d_Wq", (D, D)); I("d_Wk", (D, D)); I("d_Wv", (D, D)); I("d_Wo", (D, D))
        I("d_Wg1", (D, 128)); I("d_Wg2", (128, D))
        I("d_normw_rep", (128, D))
        I("d_ld_rep", (128, 16))
        I("d_diffTm", (128, 128))
        I("d_maskT", (128, 128))
        I("d_kdc", (128, 1))
        I("d_qdr", (128, 128))
        I("d_flag", (128, 1))
        I("d_srow", (128, 1), I32)

    y = nc.dram_tensor("y", [TC, D], F32, kind="ExternalOutput").ap()

    dint = {}
    def Dram(name, shape, dt=F32, **kw):
        dint[name] = nc.dram_tensor(name, list(shape), dt, kind="Internal", **kw).ap()
        return dint[name]

    if "C" in sublayers or "C2" in sublayers:
        Dram("wqdT", (D, D))
    if "B" in sublayers or "A" in sublayers or "A2" in sublayers:
        Dram("lnmy", (TC, D))
        Dram("ln_ag", (2 * TC, D), addr_space="Local")
    if "A" in sublayers or "A2" in sublayers:
        Dram("lgmy", (128, 8))
        Dram("lg_ag", (256, 8), addr_space="Local")
        Dram("kmy", (128, 16 * 512))
        Dram("k_ag", (256, 16 * 512), addr_space="Local")
        Dram("vmy", (SELC, D))
        Dram("v_ag", (SEL, D), addr_space="Local")
        Dram("xwmy", (SELC, D))
        Dram("xw_pad", (SEL + 1, D))
    if "D" in sublayers:
        Dram("kvpark", (32 * 128, TC))
        Dram("s_my", (128, 16 * 128))
        Dram("s_ag", (256, 16 * 128), addr_space="Local")

    sb = {}
    def S(name, shape, dt=F32):
        sb[name] = st.enter_context(nc.sbuf_tensor(name, list(shape), dt))
        return sb[name]

    x_sb = S("x", (128, NT * D))
    g0 = S("g0", (128, 8192))
    g1 = S("g1", (128, 8192))
    g2 = S("g2", (128, 8192))
    g3 = S("g3", (128, 4096))
    g4 = S("g4", (128, 4096))
    wbuf = S("wbuf", (128, 2048))
    sm = S("sm", (128, 1024), F32)
    smi = S("smi", (128, 16), I32)
    ident_sb = S("ident_sb", (128, 128))

    ps0 = st.enter_context(nc.psum_tensor("ps0", [128, 2048], F32))
    ps1 = st.enter_context(nc.psum_tensor("ps1", [128, 2048], F32))

    sems = {}
    for name in ("smm", "svec", "ssc", "sout", "stW", "stG", "stX"):
        sems[name] = Ctr(st.enter_context(nc.semaphore(name)))
    rsems = [Ctr(st.enter_context(nc.semaphore(f"rot{i}"))) for i in range(8)]

    prog = Prog(nc)
    stW = Stream(prog, sems["stW"])
    stG = Stream(prog, sems["stG"])
    stX = Stream(prog, sems["stX"])

    coll_sems = []
    def new_coll_sem():
        c = Ctr(st.enter_context(nc.semaphore(f"coll{len(coll_sems)}")))
        coll_sems.append(c)
        return c

    rotA = Rot(prog, rsems[0:4])
    rotB = Rot(prog, rsems[4:6])
    rotC = Rot(prog, rsems[6:8])
    ctx = dict(nc=nc, prog=prog, din=din, dint=dint, sb=sb, y=y,
               rotA=rotA, rotB=rotB, rotC=rotC,
               x_sb=x_sb, g0=g0, g1=g1, g2=g2, g3=g3, g4=g4, wbuf=wbuf, sm=sm,
               smi=smi,
               ident=ident_sb, ps0=ps0, ps1=ps1, sems=sems,
               stW=stW, stG=stG, stX=stX, new_coll_sem=new_coll_sem)

    emit_prelude(ctx)
    first_c = True
    for sl in sublayers:
        emit_barrier(ctx)
        if sl in ("C", "C2"):
            emit_C(ctx, first=first_c)
            first_c = False
        elif sl in ("A", "A2"):
            emit_A(ctx)
        elif sl == "B":
            emit_B(ctx)
        elif sl == "D":
            emit_D(ctx)
    emit_barrier(ctx)
    emit_finale(ctx)

    with nc.Block() as block:
        prog.run(block)
    st.close()
    return nc



def emit_barrier(ctx):
    prog, s = ctx["prog"], ctx["sems"]
    stW, stG = ctx["stW"], ctx["stG"]
    for eng in ("sync", "tensor", "vector", "scalar", "gpsimd"):
        for c in (s["smm"], s["svec"], s["ssc"], stW.ctr, stG.ctr):
            if c.v > 0:
                prog.wait(eng, c, c.v)


def emit_prelude(ctx):
    prog, din, s = ctx["prog"], ctx["din"], ctx["sems"]
    stX = ctx["stX"]
    lv = stX.group("sync", [
        (ctx["x_sb"][:].rearrange("p (t f) -> p t f", t=NT),
         din["x_in"].rearrange("(t p) f -> p t f", p=128)),
        (ctx["ident"][:], din["ident"]),
    ])
    ctx["xs_ready"] = lv
    for eng in ("tensor", "vector", "scalar", "gpsimd"):
        prog.wait(eng, stX.ctr, lv)


def emit_finale(ctx):
    prog, s, y = ctx["prog"], ctx["sems"], ctx["y"]
    prog.wait("gpsimd", s["svec"])
    prog.wait("gpsimd", s["ssc"])
    prog.wait("gpsimd", s["smm"])
    h = s["sout"].h
    def fn(e, h=h):
        e.dma_start(out=y.rearrange("(t p) f -> p t f", p=128),
                    in_=ctx["x_sb"][:].rearrange("p (t f) -> p t f", t=NT)).then_inc(h, 16)
    prog.add("gpsimd", fn)
    s["sout"].bump(16)
    prog.wait("gpsimd", s["sout"])


def ln_tile(ctx, src_ap, dst_ap, sq_ap, col4, eps=1e-5, gain_rep=None, rms=False,
            save_musig=None):
    prog, s = ctx["prog"], ctx["sems"]
    c_sum = col4[:, 0:1]; c_ssq = col4[:, 1:2]; c_rstd = col4[:, 2:3]; c_b = col4[:, 3:4]
    inv_n = 1.0 / D

    prog.wait("scalar", s["svec"], s["svec"].v)
    def f_sc(e, src=src_ap, sq=sq_ap, cs=c_ssq):
        e.activation(sq, src, AF.Square, accum_out=cs)
    prog.add("scalar", f_sc)
    sc_done = pub(prog, s, "scalar")

    if not rms:
        prog.add("vector", lambda e, src=src_ap, c=c_sum: e.reduce_sum(c, src, axis=AX.X))
        prog.add("vector", lambda e: e.drain())
    prog.wait("vector", s["ssc"], sc_done)
    if rms:
        def f_v1(e, ci=c_ssq, co=c_rstd):
            e.tensor_scalar(co, ci, inv_n, eps, OP.mult, OP.add)
        prog.add("vector", f_v1)
    else:
        def f_v1(e, cs=c_sum, co=c_rstd, cb=c_b):
            e.tensor_scalar_mul(cb, cs, inv_n)
            e.drain()
            e.tensor_scalar(co, cb, cb, None, OP.mult, OP.bypass)
        prog.add("vector", f_v1)
        def f_v2(e, cq=c_ssq, co=c_rstd):
            e.drain()
            e.scalar_tensor_tensor(co, cq, inv_n, co, OP.mult, OP.subtract)
            e.drain()
            e.tensor_scalar_add(co, co, eps)
        prog.add("vector", f_v2)
    v_done = pub(prog, s, "vector")

    prog.wait("scalar", s["svec"], v_done)
    def f_sc2(e, ci=c_rstd):
        e.activation(ci, ci, AF.Ln)
        e.drain()
        e.activation(ci, ci, AF.Exp, scale=-0.5)
    prog.add("scalar", f_sc2)
    if save_musig is not None:
        mu_ap, rs_ap = save_musig
        def f_sc2b(e, mu=mu_ap, rs=rs_ap, cb=c_b, cr=c_rstd):
            e.drain()
            e.copy(mu, cb)
            e.copy(rs, cr)
        prog.add("scalar", f_sc2b)
    sc2_done = pub(prog, s, "scalar")

    prog.wait("vector", s["ssc"], sc2_done)
    if rms:
        if gain_rep is not None:
            def f_v3(e, src=src_ap, dst=dst_ap, cr=c_rstd, g=gain_rep):
                e.scalar_tensor_tensor(dst, src, cr, g, OP.mult, OP.mult)
            prog.add("vector", f_v3)
        else:
            def f_v3(e, src=src_ap, dst=dst_ap, cr=c_rstd):
                e.tensor_scalar_mul(dst, src, cr)
            prog.add("vector", f_v3)
    else:
        def f_v3(e, src=src_ap, dst=dst_ap, cr=c_rstd, cb=c_b):
            e.tensor_scalar(cb, cb, cr, -1.0, OP.mult, OP.mult)
            e.drain()
            e.tensor_scalar(dst, src, cr, cb, OP.mult, OP.add)
        prog.add("vector", f_v3)
    return pub(prog, s, "vector")


def emit_quant_dequant(ctx, src_ap, dst_ap, scr):
    prog, s = ctx["prog"], ctx["sems"]
    NB = D // 32
    f0 = scr["f0"]; b0 = scr["b0"]; b1 = scr["b1"]; b2 = scr["b2"]
    m2 = scr["m2"]; m4 = scr["m4"]; vbuf = scr["v"]
    src3 = src_ap.rearrange("p (nb k) -> p nb k", k=32)
    dst3 = dst_ap.rearrange("p (nb k) -> p nb k", k=32)
    f03 = f0.rearrange("p (nb k) -> p nb k", k=32)

    prog.add("vector", lambda e, i=src3, o=b0: e.reduce_max(o, i, axis=AX.X, apply_absolute_value=True))
    def f_exp(e, a=b0, ee=b2, m=b1):
        e.drain()
        e.tensor_scalar_max(a, a, 2.0**-126)
        e.drain()
        e.tensor_single_scalar(m, a, 1.0, OP.is_lt)
        e.drain()
        e.tensor_scalar(ee, m, -126.0, None, OP.mult, OP.bypass)
        e.tensor_scalar(m, m, 2.0**126 - 1.0, 1.0, OP.mult, OP.add)
        e.drain()
        e.tensor_mul(a, a, m)
        e.drain()
        for k in (64, 32, 16, 8, 4, 2, 1):
            e.tensor_single_scalar(m, a, float(2.0**k), OP.is_ge)
            e.drain()
            e.scalar_tensor_tensor(ee, m, float(k), ee, OP.mult, OP.add)
            kk = min(k, 16)
            e.tensor_scalar(m, m, float(2.0**-kk) - 1.0, 1.0, OP.mult, OP.add)
            e.drain()
            for _ in range(k // kk):
                e.tensor_mul(a, a, m)
                e.drain()
        e.tensor_scalar_add(ee, ee, -2.0)
    prog.add("vector", f_exp)
    v2 = pub(prog, s, "vector")

    prog.wait("scalar", s["svec"], v2)
    def f_s2(e, i=b2, o0=b0, o1=b1):
        e.activation(o0, i, AF.Exp, scale=LN2)
        e.activation(o1, i, AF.Exp, scale=-LN2)
    prog.add("scalar", f_s2)
    sc2 = pub(prog, s, "scalar")
    prog.wait("vector", s["ssc"], sc2)

    def f_q(e, i=src3, o=f03, rs=b1, q=f0, aq=vbuf):
        e.tensor_tensor(o, i, rs.unsqueeze(2).broadcast_to([128, NB, 32]), OP.mult)
        e.drain()
        e.tensor_scalar_mul(aq, q, -1.0)
        e.drain()
        e.tensor_max(aq, aq, q)
        e.drain()
    prog.add("vector", f_q)
    def f_lad(e, q=f0, aq=vbuf, a05=m2, a1=m4, o=dst_ap):
        e.tensor_single_scalar(a05, aq, 0.25, OP.is_gt)
        e.drain()
        e.scalar_tensor_tensor(a05, aq, 0.75, a05, OP.is_ge, OP.add)
        e.drain()
        e.scalar_tensor_tensor(a05, aq, 1.25, a05, OP.is_gt, OP.add)
        e.drain()
        e.scalar_tensor_tensor(a05, aq, 1.75, a05, OP.is_ge, OP.add)
        e.drain()
        e.tensor_single_scalar(a1, aq, 2.5, OP.is_gt)
        e.drain()
        e.scalar_tensor_tensor(a1, aq, 3.5, a1, OP.is_ge, OP.add)
        e.drain()
        e.scalar_tensor_tensor(a1, a05, 0.5, a1, OP.mult, OP.add)
        e.drain()
        e.tensor_single_scalar(a05, aq, 5.0, OP.is_gt)
        e.drain()
        e.scalar_tensor_tensor(a1, a05, 2.0, a1, OP.mult, OP.add)
        e.drain()
        e.tensor_single_scalar(a05, q, 0.0, OP.is_ge)
        e.drain()
        e.scalar_tensor_tensor(a05, a05, 2.0, a1, OP.mult, OP.mult)
        e.drain()
        e.tensor_sub(o, a05, a1)
        e.drain()
    prog.add("vector", f_lad)
    def f_out(e, o=dst3, sc=b0):
        e.tensor_tensor(o, o, sc.unsqueeze(2).broadcast_to([128, NB, 32]), OP.mult)
    prog.add("vector", f_out)
    return pub(prog, s, "vector")



def emit_C(ctx, first):
    prog, s = ctx["prog"], ctx["sems"]
    nc, din, dint = ctx["nc"], ctx["din"], ctx["dint"]
    x_sb, g0, g1, g2, g3, g4, wbuf, sm = (ctx[k] for k in
        ("x_sb", "g0", "g1", "g2", "g3", "g4", "wbuf", "sm"))
    ps0, ps1 = ctx["ps0"], ctx["ps1"]
    stW, stG = ctx["stW"], ctx["stG"]
    NB = D // 32

    scr = dict(f0=g2[:, 0:D], m2=g2[:, D:2*D], m4=g3[:, 0:D], v=g3[:, D:2*D],
               b0=sm[:, 0:NB], b1=sm[:, NB:2*NB], b2=sm[:, 2*NB:3*NB])

    if first:
        for nck in range(NDC):
            lv = stW.group("sync", [(wbuf[:, 0:D], din["c_weight"][nck*128:(nck+1)*128, :])])
            prog.wait("vector", stW.ctr, lv)
            if nck >= 1:
                prog.wait("vector", s["smm"], s["smm"].v)
                prog.wait("vector", stG.ctr, stG.ctr.v)
            qv = emit_quant_dequant(ctx, wbuf[:, 0:D], g0[:, 0:D], scr)
            stW.consumed(s["svec"], qv)
            prog.wait("tensor", s["svec"], qv)
            for half in range(2):
                fns = []
                for kc in range(half*8, half*8+8):
                    def f_t(e, src=g0[:, kc*128:(kc+1)*128],
                            d=ps1[:, (kc % 8)*128:(kc % 8)*128+128], idn=ctx["ident"]):
                        return e.transpose(d, src, idn[:])
                    fns.append(f_t)
                mmv = tensor_group(prog, s, fns)
                prog.wait("vector", s["smm"], mmv)
                def f_cp(e, d=g1[:, half*1024:half*1024+1024], ss=ps1[:, 0:1024]):
                    e.tensor_copy(d, ss)
                prog.add("vector", f_cp)
                cv = pub(prog, s, "vector")
                prog.wait("tensor", s["svec"], cv)
            prog.wait("gpsimd", s["svec"], s["svec"].v)
            stG.group("gpsimd", [(
                dint["wqdT"].rearrange("(kc p) n -> p kc n", p=128)[:, :, nck*128:(nck+1)*128],
                g1[:, 0:NDC*128].rearrange("p (kc n) -> p kc n", kc=NDC))])

    afm = lambda kc: (g0 if kc < 8 else g1)[:, (kc % 8)*1024:(kc % 8)*1024+1024]
    for t in range(NT):
        xt = x_sb[:, t*D:(t+1)*D]
        if first and t == 0:
            prog.wait("vector", stG.ctr, stG.ctr.v)
        lnv = ln_tile(ctx, xt, g4[:, 0:D], g4[:, D:2*D], sm[:, 3*NB:3*NB+4])
        qv = emit_quant_dequant(ctx, g4[:, 0:D], g4[:, D:2*D], scr)
        prog.wait("tensor", s["svec"], qv)
        for half in range(2):
            fns = []
            for kc in range(half*8, half*8+8):
                def f_t(e, src=g4[:, D + kc*128: D + (kc+1)*128],
                        d=ps1[:, (kc % 8)*128:(kc % 8)*128+128], idn=ctx["ident"]):
                    return e.transpose(d, src, idn[:])
                fns.append(f_t)
            mmv = tensor_group(prog, s, fns)
            prog.wait("vector", s["smm"], mmv)
            for kc in range(half*8, half*8+8):
                def f_cp(e, d=afm(kc)[:, t*128:(t+1)*128],
                         ss=ps1[:, (kc % 8)*128:(kc % 8)*128+128]):
                    e.tensor_copy(d, ss)
                prog.add("vector", f_cp)
            cv = pub(prog, s, "vector")
            prog.wait("tensor", s["svec"], cv)
    afm_ready = s["svec"].v

    prog.wait("sync", s["svec"], afm_ready)
    bv = stW.group("sync", [(g3[:, 0:D], din["c_bias_rep"])])
    prog.wait("vector", stW.ctr, bv)
    prog.add("vector", lambda e, o=g4[:, 0:D], i=g3[:, 0:D]: e.tensor_scalar_mul(o, i, 0.5))
    hb_ready = pub(prog, s, "vector")
    stW.consumed(s["svec"], hb_ready)

    for j in range(4):
        wv = stW.group("sync", [(
            g2[:].rearrange("p (kc n) -> p kc n", kc=NDC),
            dint["wqdT"].rearrange("(kc p) n -> p kc n", p=128)[:, :, j*512:(j+1)*512])])
        prog.wait("tensor", stW.ctr, wv)
        if j == 0:
            prog.wait("tensor", s["svec"], afm_ready)
        fns = []
        for t in range(NT):
            pst = (ps0 if t < 4 else ps1)[:, (t % 4)*512:(t % 4)*512+512]
            for kc in range(NDC):
                def f_mm(e, d=pst, l=afm(kc)[:, t*128:(t+1)*128],
                         r=g2[:, kc*512:(kc+1)*512], st_=(kc == 0), sp=(kc == NDC-1)):
                    return e.matmul(d, l, r, start=st_, stop=sp)
                fns.append(f_mm)
        mmv = tensor_group(prog, s, fns)
        stW.consumed(s["smm"], mmv)
        prog.wait("vector", s["smm"], mmv)
        if j == 0:
            prog.wait("vector", s["svec"], hb_ready)
        for t in range(NT):
            pst = (ps0 if t < 4 else ps1)[:, (t % 4)*512:(t % 4)*512+512]
            def f_ev(e, ps=pst, hb=g4[:, j*512:(j+1)*512],
                     xt=x_sb[:, t*D+j*512:t*D+(j+1)*512],
                     tmp=g3[:, D+512*(t % 2):D+512*(t % 2)+512]):
                e.scalar_tensor_tensor(tmp, ps, 0.5, hb, OP.mult, OP.add)
                e.drain()
                e.tensor_add(xt, xt, tmp)
                e.drain()
            prog.add("vector", f_ev)
        ev = pub(prog, s, "vector")
        prog.wait("tensor", s["svec"], ev)


def emit_A(ctx):
    prog, s = ctx["prog"], ctx["sems"]
    nc, din, dint = ctx["nc"], ctx["din"], ctx["dint"]
    x_sb, g0, g1, g2, g3, g4, wbuf, sm, smi = (ctx[k] for k in
        ("x_sb", "g0", "g1", "g2", "g3", "g4", "wbuf", "sm", "smi"))
    ps0, ps1 = ctx["ps0"], ctx["ps1"]
    stW, stG = ctx["stW"], ctx["stG"]
    rotA = ctx["rotA"]
    NRT = 4
    NH = 16

    col4 = sm[:, 0:4]
    lgcols = sm[:, 16:32]
    cnt_c = sm[:, 32:48]; pos_c = sm[:, 48:64]
    mask_c = sm[:, 64:80]; tmp_c = sm[:, 80:96]
    rw_c = sm[:, 96:100]
    b2col = sm[:, 100:101]
    nmaxc = sm[:, 104:105]; rsumc = sm[:, 105:106]
    one1 = sm[:, 112:113]
    val_m = sm[:, 113:114]; val_d = sm[:, 114:115]
    selcols = smi[:, 0:4]
    srccols = smi[:, 8:16]

    cl = stW.group("sync", [
        (g3[:, 0:2048], din["a_iota"]),
        (sm[:, 128:144], din["a_idxcol"]),
        (sm[:, 144:148], din["a_rcol"]),
        (sm[:, 148:149], din["a_halfflag"]),
        (sm[0:1, 152:280], din["a_ones1"]),
    ])
    idxcol = sm[:, 128:144]; rcol = sm[:, 144:148]
    hf_col = sm[:, 148:149]
    ones1 = sm[0:1, 152:280]
    prog.wait("vector", stW.ctr, cl)
    prog.wait("tensor", stW.ctr, cl)
    gl = stW.group("sync", [(g2[:, 0:D], din["a_wrt_rep"])])
    wrt = g2[:, 0:D]
    prog.wait("vector", stW.ctr, gl)

    prog.wait("vector", s["smm"], s["smm"].v)
    prog.wait("vector", s["ssc"], s["ssc"].v)
    prog.wait("gpsimd", s["ssc"], s["ssc"].v)
    for t in range(NT):
        xt = x_sb[:, t*D:(t+1)*D]
        if t > 0:
            prog.wait("scalar", stG.ctr, stG.ctr.v)
            prog.wait("vector", stG.ctr, stG.ctr.v)
        lnv = ln_tile(ctx, xt, g4[:, 0:D], g4[:, D:2*D], col4)
        prog.wait("gpsimd", s["svec"], lnv)
        stG.group("gpsimd", [(dint["lnmy"][t*128:(t+1)*128, :], g4[:, 0:D])])
        def f_lg(e, h=g4[:, 0:D], w=wrt, scr=g4[:, D:2*D]):
            e.tensor_mul(scr, h, w)
            e.drain()
        prog.add("vector", f_lg)
        prog.add("vector", lambda e, scr=g4[:, D:2*D], lg=tmp_c[:, t:t+1]:
                 e.reduce_sum(lg, scr, axis=AX.X))
        def f_xu(e, xt=xt, h=g4[:, 0:D]):
            e.drain()
            e.scalar_tensor_tensor(xt, h, 0.5, xt, OP.mult, OP.add)
        prog.add("vector", f_xu)
        lgv = pub(prog, s, "vector")
    prog.wait("gpsimd", s["svec"], s["svec"].v)
    lgd = stG.group("gpsimd", [(dint["lgmy"], tmp_c[:, 0:8])])
    prog.wait("gpsimd", stG.ctr, lgd)
    csem = ctx["new_coll_sem"]()
    def f_coll(e, h=csem.h):
        e.collective_compute("AllGather", OP.bypass, replica_groups=PAIR_GROUPS,
                             ins=[dint["lgmy"]], outs=[dint["lg_ag"]]).then_inc(h, 1)
    prog.add("gpsimd", f_coll)
    csem.bump()
    prog.wait("sync", csem)
    lgl = stW.group("sync", [(lgcols[:, 0:8], dint["lg_ag"][0:128, :]),
                             (lgcols[:, 8:16], dint["lg_ag"][128:256, :])])
    prog.wait("vector", stW.ctr, lgl)
    prog.wait("scalar", stW.ctr, lgl)
    prog.wait("vector", stG.ctr, stG.ctr.v)
    prog.add("vector", lambda e: (e.memset(g4[:, 0:128], 1.0), e.drain()))
    onetile = g4[:, 0:128]

    def col_bcast(cols_ap, dst_base_fn, n=16):
        for c in range(n):
            def f_cr(e, o=g4[:, 128:256], one=onetile, cc=cols_ap[:, c:c+1]):
                e.tensor_scalar(o, one, cc, None, OP.mult, OP.bypass)
                e.drain()
            prog.add("vector", f_cr)
            crv = pub(prog, s, "vector")
            prog.wait("tensor", s["svec"], crv)
            cv = transpose_block(ctx, [g4[:, 128:256]], [dst_base_fn(c)], ps1[:, 0:128])
            prog.wait("tensor", s["svec"], cv)
        return cv

    col_bcast(lgcols, lambda c: g3[:, 2048 + c*128: 2048 + (c+1)*128])
    lg_bc = g3[:, 2048:4096]; iota_bc = g3[:, 0:2048]

    scr_rk = g0[:, 0:2048]
    for c in range(16):
        def f_rk(e, bc=lg_bc, lc=lgcols[:, c:c+1], scr=scr_rk, cnt=cnt_c[:, c:c+1],
                 it=iota_bc, ic=idxcol[:, c:c+1], eqc=tmp_c[:, c:c+1]):
            e.tensor_scalar(scr, bc, lc, 0.0, OP.is_gt, OP.add, accum_out=cnt)
            e.drain()
            e.tensor_scalar(scr, bc, lc, None, OP.is_equal, OP.bypass)
            e.drain()
            e.scalar_tensor_tensor(scr, it, ic, scr, OP.is_lt, OP.mult, accum_out=eqc)
            e.drain()
            e.tensor_add(cnt, cnt, eqc)
            e.drain()
        prog.add("vector", f_rk)
    def f_mk(e, cnt=cnt_c, mk=mask_c):
        e.tensor_single_scalar(mk, cnt, 1024.0, OP.is_lt)
        e.drain()
    prog.add("vector", f_mk)
    rkv = pub(prog, s, "vector")
    mask_bc = g4[:, 2048:4096]
    col_bcast(cnt_c, lambda c: mask_bc[:, c*128:(c+1)*128])
    def f_mb(e, d=mask_bc):
        e.tensor_single_scalar(d, d, 1024.0, OP.is_lt)
        e.drain()
    prog.add("vector", f_mb)
    mbv = pub(prog, s, "vector")
    for c in range(16):
        def f_ps(e, it=iota_bc, ic=idxcol[:, c:c+1], mb=mask_bc, scr=scr_rk,
                 pc=pos_c[:, c:c+1]):
            e.scalar_tensor_tensor(scr, it, ic, mb, OP.is_lt, OP.mult, accum_out=pc)
            e.drain()
        prog.add("vector", f_ps)
    def f_pp(e, pc=pos_c, mk=mask_c, t1=tmp_c, cn=cnt_c):
        e.scalar_tensor_tensor(t1, mk, -4096.0, pc, OP.mult, OP.add)
        e.drain()
        e.tensor_scalar_add(t1, t1, 4096.0)
        e.drain()
        e.tensor_scalar(cn, t1, 1024.0, None, OP.min, OP.bypass)
        e.drain()
    prog.add("vector", f_pp)
    pp_bc = g4[:, 2048:4096]
    col_bcast(tmp_c, lambda c: pp_bc[:, c*128:(c+1)*128])
    def f_src(e, cn=cnt_c, hfv=hf_col, dst=tmp_c[:, 0:8], dif=tmp_c[:, 8:16]):
        e.tensor_sub(dif, cn[:, 8:16], cn[:, 0:8])
        e.drain()
        e.scalar_tensor_tensor(dst, dif, hfv, cn[:, 0:8], OP.mult, OP.add)
        e.drain()
    prog.add("vector", f_src)
    def f_srci(e, dst=srccols, srcf=tmp_c[:, 0:8]):
        e.tensor_copy(dst, srcf)
        e.drain()
    prog.add("vector", f_srci)
    for rc in range(NRT):
        def f_sel(e, bc=pp_bc, rcc=rcol[:, rc:rc+1], scr=scr_rk, it=iota_bc,
                  out=tmp_c[:, 8+rc:8+rc+1]):
            e.tensor_scalar(scr, bc, rcc, None, OP.is_equal, OP.bypass)
            e.drain()
            e.tensor_mul(scr, scr, it)
            e.drain()
            e.reduce_sum(out, scr, axis=AX.X)
            e.drain()
        prog.add("vector", f_sel)
    def f_seli(e, dst=selcols, srcf=tmp_c[:, 8:12]):
        e.tensor_copy(dst, srcf)
        e.drain()
    prog.add("vector", f_seli)
    selv = pub(prog, s, "vector")

    def f_rw1(e, lg=lgcols, nm=val_m, em=tmp_c[:, 0:16] if False else None):
        e.reduce_max(nm, lg, axis=AX.X, negate=True)
        e.drain()
    prog.add("vector", f_rw1)
    r1v = pub(prog, s, "vector")
    prog.wait("tensor", s["svec"], r1v)
    cv = transpose_block(ctx, [val_m], [g4[0:1, 128:256]], ps1[0:1, 0:128])
    prog.wait("vector", s["svec"], cv)
    def f_rw2(e, row=g4[0:1, 128:256], nm=val_m[0:1, :]):
        e.reduce_max(nm, row, axis=AX.X)
        pass
    prog.add("vector", lambda e, row=g4[0:1, 128:256], nm=val_m[0:1, :]:
             e.tensor_reduce(nm, row, axis=AX.X, op=OP.min))
    r2v = pub(prog, s, "vector")
    prog.wait("tensor", s["svec"], r2v)
    mmv = tensor_group(prog, s, [lambda e, d=ps1[:, 0:1], l=ones1, r=val_m[0:1, :]:
                                 e.matmul(d, l, r, start=True, stop=True)])
    prog.wait("vector", s["smm"], mmv)
    prog.add("vector", lambda e: (e.tensor_copy(val_m, ps1[:, 0:1]), e.drain()))
    r3v = pub(prog, s, "vector")
    prog.wait("scalar", s["svec"], r3v)
    def f_rw3(e, lg=lgcols, nm=val_m, ec=scr_rk[:, 0:16]):
        e.activation(ec, lg, AF.Exp, bias=nm)
    prog.add("scalar", f_rw3)
    r4v = pub(prog, s, "scalar")
    prog.wait("vector", s["ssc"], r4v)
    def f_rw4(e, ec=scr_rk[:, 0:16], mk=mask_c, sd=val_d):
        e.tensor_mul(ec, ec, mk)
        e.drain()
        e.reduce_sum(sd, ec, axis=AX.X)
        e.drain()
    prog.add("vector", f_rw4)
    r5v = pub(prog, s, "vector")
    prog.wait("tensor", s["svec"], r5v)
    cv = transpose_block(ctx, [val_d], [g4[0:1, 128:256]], ps1[0:1, 0:128])
    prog.wait("vector", s["svec"], cv)
    prog.add("vector", lambda e, row=g4[0:1, 128:256], sd=val_d[0:1, :]:
             (e.reduce_sum(sd, row, axis=AX.X), e.drain()))
    r6v = pub(prog, s, "vector")
    prog.wait("scalar", s["svec"], r6v)
    def f_rw5(e, sd=val_d[0:1, :]):
        e.activation(sd, sd, AF.Ln)
    prog.add("scalar", f_rw5)
    r7v = pub(prog, s, "scalar")
    prog.wait("vector", s["ssc"], r7v)
    def f_rw6(e, sd=val_d[0:1, :], nm=val_m[0:1, :], b2=b2col[0:1, :]):
        e.tensor_sub(b2, nm, sd)
        e.drain()
        e.tensor_scalar_add(b2, b2, float(np.log(0.5)))
        e.drain()
    prog.add("vector", f_rw6)
    r8v = pub(prog, s, "vector")
    prog.wait("tensor", s["svec"], r8v)
    mmv = tensor_group(prog, s, [lambda e, d=ps1[:, 0:1], l=ones1, r=b2col[0:1, :]:
                                 e.matmul(d, l, r, start=True, stop=True)])
    prog.wait("vector", s["smm"], mmv)
    prog.add("vector", lambda e: (e.tensor_copy(b2col, ps1[:, 0:1]), e.drain()))
    b2v = pub(prog, s, "vector")
    prog.wait("tensor", s["svec"], b2v)

    prog.wait("gpsimd", stG.ctr, stG.ctr.v)
    csem2 = ctx["new_coll_sem"]()
    def f_coll2(e, h=csem2.h):
        e.collective_compute("AllGather", OP.bypass, replica_groups=PAIR_GROUPS,
                             ins=[dint["lnmy"]], outs=[dint["ln_ag"]]).then_inc(h, 1)
    prog.add("gpsimd", f_coll2)
    csem2.bump()
    prog.wait("gpsimd", csem2)
    prog.wait("gpsimd", s["svec"], selv)
    fgv = stG.group("gpsimd", pairs=[], indirect=[
        (g1[:, rc*2048:(rc+1)*2048], None, dint["ln_ag"],
         IndirectOffsetOnAxis(selcols[:, rc:rc+1], 0), OP.bypass) for rc in range(NRT)])
    prog.wait("vector", stG.ctr, fgv)
    prog.wait("scalar", stG.ctr, fgv)
    f_tm = g1

    for rc in range(NRT):
        def f_ls(e, f=f_tm[:, rc*2048:(rc+1)*2048], w=wrt, scr=g4[:, 0:2048],
                 out=rw_c[:, rc:rc+1]):
            e.tensor_mul(scr, f, w)
            e.drain()
            e.reduce_sum(out, scr, axis=AX.X)
            e.drain()
        prog.add("vector", f_ls)
    lsv = pub(prog, s, "vector")
    prog.wait("scalar", s["svec"], lsv)
    def f_rwf(e, rw=rw_c, b2=b2col):
        e.activation(rw, rw, AF.Exp, bias=b2)
    prog.add("scalar", f_rwf)
    rwf = pub(prog, s, "scalar")

    prog.wait("sync", s["svec"], s["svec"].v)
    g1l = stW.group("sync", [(g2[:, 0:D], din["a_g1_rep"])])
    g1rep = g2[:, 0:D]
    prog.wait("vector", stW.ctr, g1l)
    for rc in range(NRT):
        lnv = ln_tile(ctx, f_tm[:, rc*2048:(rc+1)*2048], g0[:, rc*2048:(rc+1)*2048],
                      g4[:, 0:D], col4, eps=1e-6, gain_rep=g1rep, rms=True)
    h1v = lnv
    prog.wait("tensor", s["svec"], h1v)
    for rc in range(NRT):
        for half in range(2):
            srcs = [g0[:, rc*2048 + kc*128: rc*2048 + (kc+1)*128] for kc in range(half*8, half*8+8)]
            dsts = [g2[:, (half*8+i)*512 + rc*128: (half*8+i)*512 + (rc+1)*128] for i in range(8)]
            cv = transpose_block(ctx, srcs, dsts, ps1[:, 0:1024])
            prog.wait("tensor", s["svec"], cv)
    h1fm = lambda dc: g2[:, dc*512:(dc+1)*512]

    prog.wait("sync", s["svec"], s["svec"].v)
    ra_l = stW.group("sync", [(g3[:, 2048:2560], din["a_ropeA"]),
                              (g3[:, 2560:3072], din["a_ropeB"])])
    ropeA = g3[:, 2048:2560]; ropeB = g3[:, 2560:3072]
    prog.wait("vector", stW.ctr, ra_l)

    def rope_evac_inner(ps_ap, dst_ap, rt):
        t_all = g3[:, 3072:3584]; t_cr = g3[:, 3584:4096]
        def f_r(e, ps=ps_ap, dst=dst_ap, ta=t_all, tc=t_cr,
                tA=ropeA[:, rt*128:(rt+1)*128], tB=ropeB[:, rt*128:(rt+1)*128]):
            v3 = lambda ap: ap.rearrange("p (h f) -> p h f", f=128)
            e.tensor_tensor(v3(ta), v3(ps), tA.unsqueeze(1).broadcast_to([128, 4, 128]), OP.mult)
            e.tensor_tensor(v3(tc), v3(ps), tB.unsqueeze(1).broadcast_to([128, 4, 128]), OP.mult)
            e.drain()
            ta3 = v3(ta); tc3 = v3(tc); d3 = v3(dst)
            e.tensor_tensor(d3[:, :, 0:64], ta3[:, :, 0:64], ta3[:, :, 64:128], OP.subtract)
            e.tensor_tensor(d3[:, :, 64:128], tc3[:, :, 0:64], tc3[:, :, 64:128], OP.add)
            e.drain()
        prog.add("vector", f_r)

    for fg in range(4):
        for dc in range(NDC):
            sl = rotA.i % 4
            wslot = wbuf[:, sl*512:sl*512+512]
            wc, wv = rotA.load("sync", [(wslot, din["a_wk"][dc*128:(dc+1)*128, fg*512:(fg+1)*512])])
            prog.wait("tensor", wc, wv)
            fns = []
            for rt in range(NRT):
                def f_mm(e, d=ps0[:, rt*512:(rt+1)*512], l=h1fm(dc)[:, rt*128:(rt+1)*128],
                         r=wslot, st_=(dc == 0), sp=(dc == NDC-1)):
                    return e.matmul(d, l, r, start=st_, stop=sp)
                fns.append(f_mm)
            mmv = tensor_group(prog, s, fns)
            rotA.consumed(s["smm"], mmv)
        prog.wait("vector", s["smm"], mmv)
        for rt in range(NRT):
            rope_evac_inner(ps0[:, rt*512:(rt+1)*512], g4[:, rt*512:(rt+1)*512], rt)
        kev = pub(prog, s, "vector")
        prog.wait("tensor", s["svec"], kev)
        for rt in range(NRT):
            srcs = [g4[:, rt*512 + i*128: rt*512 + (i+1)*128] for i in range(4)]
            dsts = [g4[:, 2048 + i*512 + rt*128: 2048 + i*512 + (rt+1)*128] for i in range(4)]
            cv = transpose_block(ctx, srcs, dsts, ps1[:, 0:512])
            prog.wait("tensor", s["svec"], cv)
        prog.wait("gpsimd", s["svec"], cv)
        stG.group("gpsimd", [(dint["kmy"][:, (fg*4)*512:(fg*4+4)*512],
                              g4[:, 2048:4096])])
        prog.wait("vector", stG.ctr, stG.ctr.v)
        for dc in range(NDC):
            sl = rotA.i % 4
            wslot = wbuf[:, sl*512:sl*512+512]
            wc, wv = rotA.load("sync", [(wslot, din["a_wv"][dc*128:(dc+1)*128, fg*512:(fg+1)*512])])
            prog.wait("tensor", wc, wv)
            fns = []
            for rt in range(NRT):
                def f_mm(e, d=ps0[:, rt*512:(rt+1)*512], l=h1fm(dc)[:, rt*128:(rt+1)*128],
                         r=wslot, st_=(dc == 0), sp=(dc == NDC-1)):
                    return e.matmul(d, l, r, start=st_, stop=sp)
                fns.append(f_mm)
            mmv = tensor_group(prog, s, fns)
            rotA.consumed(s["smm"], mmv)
        prog.wait("vector", s["smm"], mmv)
        for rt in range(NRT):
            def f_cp(e, d=g4[:, rt*512:(rt+1)*512], ss=ps0[:, rt*512:(rt+1)*512]):
                e.tensor_copy(d, ss)
            prog.add("vector", f_cp)
        vev = pub(prog, s, "vector")
        prog.wait("tensor", s["svec"], vev)
        prog.wait("gpsimd", s["svec"], vev)
        stG.group("gpsimd", [(dint["vmy"].rearrange("(rt p) f -> p rt f", p=128)[:, :, fg*512:(fg+1)*512],
                              g4[:, 0:2048].rearrange("p (rt f) -> p rt f", rt=NRT))])
        prog.wait("vector", stG.ctr, stG.ctr.v)

    prog.wait("vector", s["smm"], s["smm"].v)
    for fg in range(4):
        for dc in range(NDC):
            sl = rotA.i % 4
            wslot = wbuf[:, sl*512:sl*512+512]
            wc, wv = rotA.load("sync", [(wslot, din["a_wq"][dc*128:(dc+1)*128, fg*512:(fg+1)*512])])
            prog.wait("tensor", wc, wv)
            fns = []
            for rt in range(NRT):
                def f_mm(e, d=ps0[:, rt*512:(rt+1)*512], l=h1fm(dc)[:, rt*128:(rt+1)*128],
                         r=wslot, st_=(dc == 0), sp=(dc == NDC-1)):
                    return e.matmul(d, l, r, start=st_, stop=sp)
                fns.append(f_mm)
            mmv = tensor_group(prog, s, fns)
            rotA.consumed(s["smm"], mmv)
        prog.wait("vector", s["smm"], mmv)
        for rt in range(NRT):
            rope_evac_inner(ps0[:, rt*512:(rt+1)*512],
                            g0[:, rt*2048 + fg*512: rt*2048 + (fg+1)*512], rt)
        qev = pub(prog, s, "vector")
        prog.wait("tensor", s["svec"], qev)
    for rc in range(NRT):
        for half in range(2):
            srcs = [g0[:, rc*2048 + kc*128: rc*2048 + (kc+1)*128] for kc in range(half*8, half*8+8)]
            dsts = [g2[:, (half*8+i)*512 + rc*128: (half*8+i)*512 + (rc+1)*128] for i in range(8)]
            cv = transpose_block(ctx, srcs, dsts, ps1[:, 0:1024])
            prog.wait("tensor", s["svec"], cv)
    qfm = lambda hh: g2[:, hh*512:(hh+1)*512]

    prog.wait("gpsimd", stG.ctr, stG.ctr.v)
    csem3 = ctx["new_coll_sem"]()
    def f_coll3(e, h=csem3.h):
        e.collective_compute("AllGather", OP.bypass, replica_groups=PAIR_GROUPS,
                             ins=[dint["kmy"]], outs=[dint["k_ag"]]).then_inc(h, 1)
        e.collective_compute("AllGather", OP.bypass, replica_groups=PAIR_GROUPS,
                             ins=[dint["vmy"]], outs=[dint["v_ag"]]).then_inc(h, 1)
    prog.add("gpsimd", f_coll3)
    csem3.bump(2)
    prog.wait("sync", csem3)

    scale = 1.0 / math.sqrt(128.0)
    att = g3[:, 0:4096]
    kvh = g3[:, 2048:4096] if False else None
    for hh in range(NH):
        prog.wait("sync", s["smm"], s["smm"].v)
        kl = stW.group("sync", [
            (g0[:, 0:512], dint["k_ag"][0:128, hh*512:(hh+1)*512]),
            (g0[:, 512:1024], dint["k_ag"][128:256, hh*512:(hh+1)*512]),
            (g0[:, 1024:2048],
             dint["v_ag"].rearrange("(kc p) f -> p kc f", p=128)[:, :, hh*128:(hh+1)*128]),
        ])
        kfmh = g0[:, 0:1024]; vtmh = g0[:, 1024:2048]
        prog.wait("tensor", stW.ctr, kl)
        if hh == 0:
            prog.wait("tensor", s["svec"], s["svec"].v)
        if hh == 0:
            rotA.reset(s["svec"], s["svec"].v)
        nmaxq = lambda qt: sm[:, 256 + qt*2: 256 + qt*2 + 1]
        rsumq = lambda qt: sm[:, 256 + qt*2 + 1: 256 + qt*2 + 2]
        exvs = {}; nmvs = {}
        for qt in range(NRT):
            mc = rotA.ctrs[qt]
            g = rotA.hist.get(qt)
            if g is not None:
                prog.wait("sync", g[0], g[1])
            elif mc.v > 0:
                prog.wait("sync", mc, mc.v)
            def f_ld(e, o=g4[:, qt*1024:(qt+1)*1024],
                     i=din["a_mask_my"][qt*128:(qt+1)*128, :], h2=mc.h):
                e.dma_start(out=o, in_=i).then_inc(h2, 16)
            prog.add("sync", f_ld)
            mc.v += 16
            rotA.i += 1
            pbank = ps0[:, (qt % 2)*1024:(qt % 2)*1024+1024]
            if qt >= 2:
                prog.wait("tensor", s["svec"], exvs[qt-2])
            fns = []
            for kg in range(2):
                def f_mm(e, d=pbank[:, kg*512:(kg+1)*512], l=qfm(hh)[:, qt*128:(qt+1)*128],
                         r=kfmh[:, kg*512:(kg+1)*512]):
                    return e.matmul(d, l, r, start=True, stop=True)
                fns.append(f_mm)
            mmv = tensor_group(prog, s, fns)
            prog.wait("vector", s["smm"], mmv)
            prog.wait("vector", mc, mc.v)
            def f_ev(e, sc=g4[:, qt*1024:(qt+1)*1024], ps=pbank, nm=nmaxq(qt)):
                e.scalar_tensor_tensor(sc, ps, scale, sc, OP.mult, OP.add)
                e.drain()
                e.reduce_max(nm, sc, axis=AX.X, negate=True)
            prog.add("vector", f_ev)
            exvs[qt] = pub(prog, s, "vector")
            prog.wait("scalar", s["svec"], exvs[qt])
            def f_ex(e, sc=g4[:, qt*1024:(qt+1)*1024], nm=nmaxq(qt), rs=rsumq(qt)):
                e.activation(sc, sc, AF.Exp, bias=nm, accum_out=rs)
            prog.add("scalar", f_ex)
            exs = pub(prog, s, "scalar")
            prog.wait("vector", s["ssc"], exs)
            def f_nm(e, sc=g4[:, qt*1024:(qt+1)*1024], rs=rsumq(qt)):
                e.reciprocal(rs, rs)
                e.drain()
                e.tensor_scalar(sc, sc, rs, None, OP.mult, OP.bypass)
                e.drain()
            prog.add("vector", f_nm)
            nmvs[qt] = pub(prog, s, "vector")
        for qt in range(NRT):
            prog.wait("tensor", s["svec"], nmvs[qt])
            srcs = [g4[:, qt*1024 + kc*128: qt*1024 + (kc+1)*128] for kc in range(8)]
            dsts = [att[:, kc*512 + qt*128: kc*512 + (qt+1)*128] for kc in range(8)]
            cv = transpose_block(ctx, srcs, dsts, ps1[:, 0:1024])
            prog.wait("tensor", s["svec"], cv)
            rotA.hist[qt] = (s["svec"], cv)
        fns = []
        for kc in range(8):
            def f_mm(e, d=ps1[:, 1024:1536], l=vtmh[:, kc*128:(kc+1)*128],
                     r=att[:, kc*512:(kc+1)*512], st_=(kc == 0), sp=(kc == 7)):
                return e.matmul(d, l, r, start=st_, stop=sp)
            fns.append(f_mm)
        mmc = tensor_group(prog, s, fns)
        prog.wait("vector", s["smm"], mmc)
        def f_cx(e, d=qfm(hh), ss=ps1[:, 1024:1536]):
            e.tensor_copy(d, ss)
            e.drain()
        prog.add("vector", f_cx)
        cxv = pub(prog, s, "vector")
        prog.wait("tensor", s["svec"], cxv)
        stW.consumed(s["svec"], cxv)

    ctxs = qfm
    g2l = stW.group("sync", [(g4[:, 0:D], din["a_g2_rep"])])
    g2rep = g4[:, 0:D]
    prog.wait("vector", stW.ctr, g2l)
    rotC = ctx["rotC"]
    rotC.reset(s["smm"], s["smm"].v)
    for j in range(4):
        for hc in range(NH):
            sl = rotC.i % 2
            ww_c, ww_v = rotC.load("sync", [(wbuf[:, sl*512:sl*512+512],
                                     din["a_wo"][hc*128:(hc+1)*128, j*512:(j+1)*512])])
            prog.wait("tensor", ww_c, ww_v)
            fns = []
            for rt in range(NRT):
                pst = ps0[:, rt*512:(rt+1)*512]
                def f_mm(e, d=pst, l=ctxs(hc)[:, rt*128:(rt+1)*128],
                         r=wbuf[:, sl*512:sl*512+512], st_=(hc == 0), sp=(hc == NH-1)):
                    return e.matmul(d, l, r, start=st_, stop=sp)
                fns.append(f_mm)
            mmo = tensor_group(prog, s, fns)
            rotC.consumed(s["smm"], mmo)
        prog.wait("vector", s["smm"], mmo)
        for rt in range(NRT):
            def f_x2(e, f=f_tm[:, rt*2048 + j*512: rt*2048 + (j+1)*512],
                     ps=ps0[:, rt*512:(rt+1)*512]):
                e.tensor_add(f, f, ps)
                e.drain()
            prog.add("vector", f_x2)
        x2v = pub(prog, s, "vector")
        prog.wait("tensor", s["svec"], x2v)
    x2 = f_tm

    for rc in range(NRT):
        lnv = ln_tile(ctx, x2[:, rc*2048:(rc+1)*2048], g0[:, rc*2048:(rc+1)*2048],
                      g4[:, D:2*D], col4, eps=1e-6, gain_rep=g2rep, rms=True)
    prog.wait("tensor", s["svec"], lnv)
    for rc in range(NRT):
        for half in range(2):
            srcs = [g0[:, rc*2048 + kc*128: rc*2048 + (kc+1)*128] for kc in range(half*8, half*8+8)]
            dsts = [g2[:, (half*8+i)*512 + rc*128: (half*8+i)*512 + (rc+1)*128] for i in range(8)]
            cv = transpose_block(ctx, srcs, dsts, ps1[:, 0:1024])
            prog.wait("tensor", s["svec"], cv)
    h2fm = lambda dc: g2[:, dc*512:(dc+1)*512]

    rotB = ctx["rotB"]
    rotB.reset(s["smm"], s["smm"].v)
    rotC.reset(s["smm"], s["smm"].v)
    g3A = g3[:, 0:2048]; g3B = g3[:, 2048:4096]
    tv_hist = {}
    for gg in range(8):
        for fi in range(8):
            ffc = gg*8 + fi
            p = fi % 2
            w1c, w1v = rotB.load("sync", [(g3A,
                din["a_w1"].rearrange("(dc p) f -> p dc f", p=128)[:, :, ffc*128:(ffc+1)*128])])
            prog.wait("tensor", w1c, w1v)
            if (gg, fi) != (0, 0):
                pv = tv_hist.get((gg*8 + fi - 2))
                if pv is not None:
                    prog.wait("tensor", s["svec"], pv)
            fns = []
            for dc in range(NDC):
                def f_mm(e, d=ps0[:, p*1024:p*1024+512], l=g3A[:, dc*128:(dc+1)*128],
                         r=h2fm(dc), st_=(dc == 0), sp=(dc == NDC-1)):
                    return e.matmul(d, l, r, start=st_, stop=sp)
                fns.append(f_mm)
            mm1 = tensor_group(prog, s, fns)
            rotB.consumed(s["smm"], mm1, back=0)
            w3c, w3v = rotB.load("sync", [(g3B,
                din["a_w3"].rearrange("(dc p) f -> p dc f", p=128)[:, :, ffc*128:(ffc+1)*128])])
            prog.wait("tensor", w3c, w3v)
            fns = []
            for dc in range(NDC):
                def f_mm(e, d=ps0[:, p*1024+512:p*1024+1024], l=g3B[:, dc*128:(dc+1)*128],
                         r=h2fm(dc), st_=(dc == 0), sp=(dc == NDC-1)):
                    return e.matmul(d, l, r, start=st_, stop=sp)
                fns.append(f_mm)
            mm3 = tensor_group(prog, s, fns)
            rotB.consumed(s["smm"], mm3, back=0)
            prog.wait("scalar", s["smm"], mm3)
            def f_t1(e, d=g0[:, p*512:(p+1)*512], ss=ps0[:, p*1024:p*1024+512]):
                e.activation(d, ss, AF.Silu)
            prog.add("scalar", f_t1)
            t1v = pub(prog, s, "scalar")
            prog.wait("vector", s["ssc"], t1v)
            def f_t(e, d=g4[:, fi*512:(fi+1)*512], t1=g0[:, p*512:(p+1)*512],
                    ps3=ps0[:, p*1024+512:p*1024+1024]):
                e.tensor_mul(d, t1, ps3)
                e.drain()
            prog.add("vector", f_t)
            tv_hist[ffc] = pub(prog, s, "vector")
        prog.wait("tensor", s["svec"], tv_hist[gg*8 + 7])
        for j in range(4):
            for fi in range(8):
                ffc = gg*8 + fi
                w2c, w2v = rotC.load("sync", [(wbuf[:, (fi % 2)*512:(fi % 2)*512+512],
                    din["a_w2"][ffc*128:(ffc+1)*128, j*512:(j+1)*512])])
                prog.wait("tensor", w2c, w2v)
                fns = []
                for rt in range(NRT):
                    def f_mm(e, d=ps1[:, rt*512:(rt+1)*512],
                             l=g4[:, fi*512 + rt*128: fi*512 + (rt+1)*128],
                             r=wbuf[:, (fi % 2)*512:(fi % 2)*512+512],
                             st_=(fi == 0), sp=(fi == 7)):
                        return e.matmul(d, l, r, start=st_, stop=sp)
                    fns.append(f_mm)
                mm2 = tensor_group(prog, s, fns)
                rotC.consumed(s["smm"], mm2, back=0)
            prog.wait("vector", s["smm"], mm2)
            for rt in range(NRT):
                def f_o2(e, x=x2[:, rt*2048 + j*512: rt*2048 + (j+1)*512],
                         ps=ps1[:, rt*512:(rt+1)*512]):
                    e.tensor_add(x, x, ps)
                    e.drain()
                prog.add("vector", f_o2)
            o2v = pub(prog, s, "vector")
            prog.wait("tensor", s["svec"], o2v)

    prog.wait("vector", s["ssc"], rwf)
    for rt in range(NRT):
        def f_xw(e, x=x2[:, rt*2048:(rt+1)*2048], rw=rw_c[:, rt:rt+1],
                 d=g0[:, rt*2048:(rt+1)*2048]):
            e.tensor_scalar(d, x, rw, None, OP.mult, OP.bypass)
            e.drain()
        prog.add("vector", f_xw)
    xwv = pub(prog, s, "vector")
    prog.wait("gpsimd", s["svec"], xwv)
    xwd = stG.group("gpsimd", [(dint["xwmy"].rearrange("(rt p) f -> p rt f", p=128),
                                g0[:].rearrange("p (rt f) -> p rt f", rt=NRT))])
    prog.add("vector", lambda e: (e.memset(g4[0:1, 0:D], 0.0), e.drain()))
    zv = pub(prog, s, "vector")
    prog.wait("gpsimd", s["svec"], zv)
    zd = stG.group("gpsimd", [(dint["xw_pad"][SEL:SEL+1, :], g4[0:1, 0:D])])
    prog.wait("gpsimd", stG.ctr, stG.ctr.v)
    csem4 = ctx["new_coll_sem"]()
    def f_coll4(e, h=csem4.h):
        e.collective_compute("AllGather", OP.bypass, replica_groups=PAIR_GROUPS,
                             ins=[dint["xwmy"]], outs=[dint["xw_pad"][0:SEL, :]]).then_inc(h, 1)
    prog.add("gpsimd", f_coll4)
    csem4.bump()
    prog.wait("gpsimd", csem4)
    for t in range(NT):
        gb = stG.group("gpsimd", pairs=[], indirect=[
            (g4[:, (t % 2)*2048:(t % 2)*2048+2048], None, dint["xw_pad"],
             IndirectOffsetOnAxis(srccols[:, t:t+1], 0), OP.bypass)])
        prog.wait("vector", stG.ctr, gb)
        def f_bg(e, x=x_sb[:, t*D:(t+1)*D], gsl=g4[:, (t % 2)*2048:(t % 2)*2048+2048]):
            e.tensor_add(x, x, gsl)
            e.drain()
        prog.add("vector", f_bg)
        bgv = pub(prog, s, "vector")
        stG.consumed(s["svec"], bgv)


def emit_B(ctx):
    prog, s = ctx["prog"], ctx["sems"]
    nc, din, dint = ctx["nc"], ctx["din"], ctx["dint"]
    x_sb, g0, g1, g2, g3, g4, wbuf, sm, smi = (ctx[k] for k in
        ("x_sb", "g0", "g1", "g2", "g3", "g4", "wbuf", "sm", "smi"))
    ps0, ps1 = ctx["ps0"], ctx["ps1"]
    stW, stG = ctx["stW"], ctx["stG"]
    rotA, rotB = ctx["rotA"], ctx["rotB"]
    NQT = T // 128

    col4 = sm[:, 0:4]
    mu_c = sm[:, 8:16]; rstd_c = sm[:, 16:24]; sg_c = sm[:, 24:32]
    causal = sm[:, 128:256]
    nmax = lambda head: sm[:, 256 + head*2: 256 + head*2 + 1]
    rsum = lambda head: sm[:, 256 + head*2 + 1: 256 + head*2 + 2]
    ropetab = lambda sl: g0[:, sl]
    q_slot = lambda hp: x_sb[:, hp*2048:(hp+1)*2048]
    lnfm = lambda dc: g2[:, dc*512:(dc+1)*512]
    kfm = g3
    vtm = g4

    cl = stW.group("sync", [(causal, din["b_causal"])])
    prog.wait("vector", stW.ctr, cl)

    prog.wait("vector", s["smm"], s["smm"].v)
    prog.wait("vector", s["ssc"], s["ssc"].v)
    prog.wait("gpsimd", s["ssc"], s["ssc"].v)
    for t in range(NT):
        xt = x_sb[:, t*D:(t+1)*D]
        lnv = ln_tile(ctx, xt, g4[:, 0:D], g4[:, D:2*D], col4,
                      save_musig=(mu_c[:, t:t+1], rstd_c[:, t:t+1]))
        prog.wait("gpsimd", s["svec"], lnv)
        stG.group("gpsimd", [(dint["lnmy"][t*128:(t+1)*128, :], g4[:, 0:D])])
        if t < NT - 1:
            prog.wait("scalar", stG.ctr, stG.ctr.v)
            prog.wait("vector", stG.ctr, stG.ctr.v)
    prog.wait("vector", s["ssc"], s["ssc"].v)
    def f_sig(e, r=rstd_c[:, 0:NT], o=sg_c[:, 0:NT]):
        e.reciprocal(o, r)
    prog.add("vector", f_sig)
    lnall = pub(prog, s, "vector")
    csem = ctx["new_coll_sem"]()
    prog.wait("gpsimd", stG.ctr, stG.ctr.v)
    def f_coll(e, h=csem.h):
        e.collective_compute("AllGather", OP.bypass, replica_groups=PAIR_GROUPS,
                             ins=[dint["lnmy"]], outs=[dint["ln_ag"]]).then_inc(h, 1)
    prog.add("gpsimd", f_coll)
    csem.bump()
    prog.wait("sync", csem)

    prog.wait("vector", stG.ctr, stG.ctr.v)
    rt = stW.group("sync", [(g0[:, 0:1024], din["b_ropeA"]),
                            (g0[:, 1024:2048], din["b_ropeB"])])
    prog.wait("vector", stW.ctr, rt)
    qtm = g0[:, 2048:4096]

    def rope_evac(ps_ap, dst_ap, gt, nh):
        t_all = g0[:, 4096:4608]; t_cr = g0[:, 4608:5120]
        def f_r(e, ps=ps_ap, dst=dst_ap, ta=t_all[:, 0:nh*64], tc=t_cr[:, 0:nh*64],
                tA=g0[:, gt*64:(gt+1)*64], tB=g0[:, 1024+gt*64:1024+(gt+1)*64], nh=nh):
            v3 = lambda ap: ap.rearrange("p (h f) -> p h f", f=64)
            e.tensor_tensor(v3(ta), v3(ps), tA.unsqueeze(1).broadcast_to([128, nh, 64]), OP.mult)
            e.tensor_tensor(v3(tc), v3(ps), tB.unsqueeze(1).broadcast_to([128, nh, 64]), OP.mult)
            e.drain()
            ta3 = v3(ta); tc3 = v3(tc); d3 = v3(dst)
            e.tensor_tensor(d3[:, :, 0:32], ta3[:, :, 0:32], ta3[:, :, 32:64], OP.subtract)
            e.tensor_tensor(d3[:, :, 32:64], tc3[:, :, 0:32], tc3[:, :, 32:64], OP.add)
            e.drain()
        prog.add("vector", f_r)

    for gi in range(4):
        for tl in range(4):
            lv = stW.group("sync", [(g1[:, 0:2048],
                                     dint["ln_ag"][gi*512 + tl*128: gi*512 + (tl+1)*128, :])])
            prog.wait("tensor", stW.ctr, lv)
            if gi == 0 and tl == 0:
                prog.wait("tensor", s["svec"], lnall)
            for half in range(2):
                srcs = [g1[:, kc*128:(kc+1)*128] for kc in range(half*8, half*8+8)]
                dsts = [lnfm(kc)[:, tl*128:(tl+1)*128] for kc in range(half*8, half*8+8)]
                cv = transpose_block(ctx, srcs, dsts, ps1[:, 0:1024])
                prog.wait("tensor", s["svec"], cv)
            stW.consumed(s["svec"], cv)
        for fg in range(2):
            for dc in range(NDC):
                sl = rotA.i % 4
                wslot = wbuf[:, sl*512:sl*512+512]
                wc, wv = rotA.load("sync", [(wslot,
                    din["b_Wq_my"][dc*128:(dc+1)*128, fg*512:(fg+1)*512])])
                prog.wait("tensor", wc, wv)
                fns = []
                for tl in range(4):
                    def f_mm(e, d=ps0[:, tl*512:(tl+1)*512], l=lnfm(dc)[:, tl*128:(tl+1)*128],
                             r=wslot, st_=(dc == 0), sp=(dc == NDC-1)):
                        return e.matmul(d, l, r, start=st_, stop=sp)
                    fns.append(f_mm)
                mmv = tensor_group(prog, s, fns)
                rotA.consumed(s["smm"], mmv)
            prog.wait("vector", s["smm"], mmv)
            for tl in range(4):
                rope_evac(ps0[:, tl*512:(tl+1)*512], qtm[:, tl*512:(tl+1)*512],
                          gi*4 + tl, 8)
            qv = pub(prog, s, "vector")
            prog.wait("tensor", s["svec"], qv)
            for tl in range(4):
                srcs = [qtm[:, tl*512 + i*128: tl*512 + (i+1)*128] for i in range(4)]
                dsts = [q_slot(fg*4 + i)[:, (gi*4+tl)*128:(gi*4+tl+1)*128] for i in range(4)]
                cv = transpose_block(ctx, srcs, dsts, ps1[:, 0:512])
                prog.wait("tensor", s["svec"], cv)
        for dc in range(NDC):
            sl = rotA.i % 4
            wslot = wbuf[:, sl*512:sl*512+256]
            wc, wv = rotA.load("sync", [(wslot, din["b_Wk_my"][dc*128:(dc+1)*128, :])])
            prog.wait("tensor", wc, wv)
            fns = []
            for tl in range(4):
                def f_mm(e, d=ps0[:, tl*512:tl*512+256], l=lnfm(dc)[:, tl*128:(tl+1)*128],
                         r=wslot, st_=(dc == 0), sp=(dc == NDC-1)):
                    return e.matmul(d, l, r, start=st_, stop=sp)
                fns.append(f_mm)
            mmv = tensor_group(prog, s, fns)
            rotA.consumed(s["smm"], mmv)
        prog.wait("vector", s["smm"], mmv)
        for tl in range(4):
            rope_evac(ps0[:, tl*512:tl*512+256], qtm[:, tl*256:(tl+1)*256], gi*4 + tl, 4)
        kv_ = pub(prog, s, "vector")
        prog.wait("tensor", s["svec"], kv_)
        for tl in range(4):
            srcs = [qtm[:, tl*256 + i*128: tl*256 + (i+1)*128] for i in range(2)]
            dsts = [kfm[:, i*2048 + (gi*4+tl)*128: i*2048 + (gi*4+tl+1)*128] for i in range(2)]
            cv = transpose_block(ctx, srcs, dsts, ps1[:, 0:256])
            prog.wait("tensor", s["svec"], cv)
        for fh in range(2):
            prog.wait("sync", s["smm"], s["smm"].v)
            wv = stW.group("sync", [(wbuf[:, 0:2048],
                din["b_Wv_my"].rearrange("(dc p) f -> p dc f", p=128)[:, :, fh*128:(fh+1)*128])])
            prog.wait("tensor", stW.ctr, wv)
            for tl in range(4):
                pv = ps1[:, (tl % 2)*1024: (tl % 2)*1024 + 128]
                fns = []
                for dc in range(NDC):
                    def f_mm(e, d=pv, l=lnfm(dc)[:, tl*128:(tl+1)*128],
                             r=wbuf[:, dc*128:(dc+1)*128],
                             st_=(dc == 0), sp=(dc == NDC-1)):
                        return e.matmul(d, l, r, start=st_, stop=sp)
                    fns.append(f_mm)
                mmv = tensor_group(prog, s, fns)
                prog.wait("vector", s["smm"], mmv)
                def f_cp(e, d=vtm[:, (gi*4+tl)*256 + fh*128: (gi*4+tl)*256 + fh*128 + 128], ss=pv):
                    e.tensor_copy(d, ss)
                prog.add("vector", f_cp)
                vv = pub(prog, s, "vector")
                prog.wait("tensor", s["svec"], vv)
            stW.consumed(s["svec"], vv)

    scoreA = g0[:, 0:2048]; scoreB = g0[:, 2048:4096]
    ctx_tm = g0[:, 4096:6144]
    kwork = g0[:, 6144:8192]
    attA = g1[:, 0:2048]; attB = g1[:, 2048:4096]
    lnrl = g1[:, 4096:6144]
    for hp in range(8):
        g = hp // 2
        prog.wait("vector", s["smm"], s["smm"].v)
        def f_kw(e, dst=kwork, src=kfm[:, (g//2)*2048:(g//2)*2048+2048], r0=(g % 2)*64):
            e.tensor_copy(dst[0:64, :], src[r0:r0+64, :])
            e.tensor_copy(dst[64:128, :], src[r0:r0+64, :])
            e.drain()
        prog.add("vector", f_kw)
        kwv = pub(prog, s, "vector")
        prog.wait("tensor", s["svec"], kwv)
        for qt in range(NQT):
            W = (qt + 1) * 128
            nkg = (W + 511) // 512
            fns = []
            for base, pps in ((0, ps0), (64, ps1)):
                for kg in range(nkg):
                    w = min(512, W - kg*512)
                    def f_mm(e, d=pps[:, kg*512:kg*512+w],
                             l=q_slot(hp)[base:base+64, qt*128:(qt+1)*128],
                             r=kwork[base:base+64, kg*512:kg*512+w]):
                        return e.matmul(d, l, r, start=True, stop=True)
                    fns.append(f_mm)
            mmv = tensor_group(prog, s, fns)
            prog.wait("vector", s["smm"], mmv)
            for head, (sc, pps) in enumerate(((scoreA, ps0), (scoreB, ps1))):
                def f_ev(e, sc=sc[:, 0:W], ps=pps[:, 0:W]):
                    e.tensor_scalar(sc, ps, 0.125, None, OP.mult, OP.bypass)
                prog.add("vector", f_ev)
                def f_dg(e, sc=sc[:, W-128:W], ps=pps[:, W-128:W], ca=causal):
                    e.drain()
                    e.scalar_tensor_tensor(sc, ps, 0.125, ca, OP.mult, OP.add)
                prog.add("vector", f_dg)
            def f_mx(e, a=scoreA[:, 0:W], b=scoreB[:, 0:W], na=nmax(0), nb=nmax(1)):
                e.drain()
                e.reduce_max(na, a, axis=AX.X, negate=True)
                e.reduce_max(nb, b, axis=AX.X, negate=True)
            prog.add("vector", f_mx)
            exv = pub(prog, s, "vector")
            prog.wait("tensor", s["svec"], exv)
            prog.wait("scalar", s["svec"], exv)
            def f_ex(e, a=scoreA[:, 0:W], b=scoreB[:, 0:W], na=nmax(0), nb=nmax(1),
                     sa=rsum(0), sb=rsum(1)):
                e.activation(a, a, AF.Exp, bias=na, accum_out=sa)
                e.activation(b, b, AF.Exp, bias=nb, accum_out=sb)
            prog.add("scalar", f_ex)
            exs = pub(prog, s, "scalar")
            prog.wait("vector", s["ssc"], exs)
            def f_rc(e, sa=rsum(0), sb=rsum(1)):
                e.reciprocal(sa, sa)
                e.reciprocal(sb, sb)
                e.drain()
            prog.add("vector", f_rc)
            rcv = pub(prog, s, "vector")
            prog.wait("tensor", s["svec"], rcv)
            nkc = qt + 1
            cvA = transpose_block(ctx, [scoreA[:, kc*128:(kc+1)*128] for kc in range(nkc)],
                                  [attA[:, kc*128:(kc+1)*128] for kc in range(nkc)],
                                  ps0[:, 0:nkc*128])
            prog.wait("tensor", s["svec"], cvA)
            cvB = transpose_block(ctx, [scoreB[:, kc*128:(kc+1)*128] for kc in range(nkc)],
                                  [attB[:, kc*128:(kc+1)*128] for kc in range(nkc)],
                                  ps0[:, 0:nkc*128])
            prog.wait("tensor", s["svec"], cvB)
            fns = []
            for kc in range(nkc):
                def f_c1(e, d=ps1[0:64, 0:128],
                         l=vtm[:, kc*256 + g*64: kc*256 + g*64 + 64],
                         r=attA[:, kc*128:(kc+1)*128], st_=(kc == 0), sp=(kc == nkc-1)):
                    return e.matmul(d, l, r, start=st_, stop=sp)
                fns.append(f_c1)
            for kc in range(nkc):
                def f_c2(e, d=ps1[0:64, 512:640],
                         l=vtm[:, kc*256 + g*64: kc*256 + g*64 + 64],
                         r=attB[:, kc*128:(kc+1)*128], st_=(kc == 0), sp=(kc == nkc-1)):
                    return e.matmul(d, l, r, start=st_, stop=sp)
                fns.append(f_c2)
            mmc = tensor_group(prog, s, fns)
            prog.wait("vector", s["smm"], mmc)
            def f_cc(e, dA=scoreA[0:64, 0:128], sA=ps1[0:64, 0:128],
                     dB=scoreA[64:128, 0:128], sB=ps1[0:64, 512:640]):
                e.tensor_copy(dA, sA)
                e.tensor_copy(dB, sB)
                e.drain()
            prog.add("vector", f_cc)
            ccv = pub(prog, s, "vector")
            prog.wait("tensor", s["svec"], ccv)
            cvt = transpose_block(ctx, [scoreA[:, 0:128]], [ctx_tm[:, qt*128:(qt+1)*128]],
                                  ps1[:, 1024:1152])
            prog.wait("vector", s["svec"], cvt)
            def f_sc(e, t=ctx_tm[:, qt*128:(qt+1)*128], sa=rsum(0), sb=rsum(1)):
                e.tensor_scalar(t[:, 0:64], t[:, 0:64], sa, None, OP.mult, OP.bypass)
                e.tensor_scalar(t[:, 64:128], t[:, 64:128], sb, None, OP.mult, OP.bypass)
                e.drain()
            prog.add("vector", f_sc)
            scv = pub(prog, s, "vector")
            prog.wait("tensor", s["svec"], scv)

        prog.wait("sync", s["svec"], s["svec"].v)
        rotB.reset(s["svec"], s["svec"].v)
        lv = stW.group("sync", [(lnrl, dint["lnmy"][hp*128:(hp+1)*128, :])])
        prog.wait("vector", stW.ctr, lv)
        for qc in range(NQT):
            sl = rotB.i % 2
            wsl = g1[:, sl*2048:(sl+1)*2048]
            wc, wv = rotB.load("sync", [(wsl, din["b_Wo"][qc*128:(qc+1)*128, :])])
            prog.wait("tensor", wc, wv)
            fns = []
            for j in range(4):
                def f_mm(e, d=ps1[:, j*512:(j+1)*512], l=ctx_tm[:, qc*128:(qc+1)*128],
                         r=wsl[:, j*512:(j+1)*512], st_=(qc == 0), sp=(qc == NQT-1)):
                    return e.matmul(d, l, r, start=st_, stop=sp)
                fns.append(f_mm)
            mmo = tensor_group(prog, s, fns)
            rotB.consumed(s["smm"], mmo)
        prog.wait("vector", s["smm"], mmo)
        def f_xn(e, ps=ps1[:, 0:2048], ln=lnrl, dst=q_slot(hp),
                 sg=sg_c[:, hp:hp+1], mu=mu_c[:, hp:hp+1]):
            e.tensor_scalar(dst, ln, sg, mu, OP.mult, OP.add)
            e.drain()
            e.tensor_add(dst, dst, ps)
            e.drain()
        prog.add("vector", f_xn)
        xnv = pub(prog, s, "vector")
        prog.wait("tensor", s["svec"], xnv)


def transpose_block(ctx, srcs, dsts, ps_ap, width=128):
    prog, s = ctx["prog"], ctx["sems"]
    fns = []
    for i, src in enumerate(srcs):
        def f_t(e, src=src, d=ps_ap[:, i*width:(i+1)*width], idn=ctx["ident"]):
            return e.transpose(d, src, idn[:])
        fns.append(f_t)
    mmv = tensor_group(prog, s, fns)
    prog.wait("vector", s["smm"], mmv)
    n = len(dsts)
    stride = None
    if n > 1 and all(d.tensor is dsts[0].tensor for d in dsts):
        offs = [d.offset for d in dsts]
        st0 = offs[1] - offs[0]
        if st0 > 0 and all(offs[i+1] - offs[i] == st0 for i in range(n-1)):
            stride = st0
    if stride is not None and all(d.partition_size() == 128 for d in dsts):
        import concourse.mybir as _mb
        base = dsts[0]
        t = base.tensor
        dst3 = bass.AP(tensor=t, offset=base.offset,
                       ap=[[t.shape[1], 128], [stride, n], [1, width]])
        src3 = ps_ap[:, 0:n*width].rearrange("p (n w) -> p n w", w=width)
        def f_cp(e, d=dst3, ss=src3):
            e.tensor_copy(d, ss)
        prog.add("vector", f_cp)
    else:
        for i, dst in enumerate(dsts):
            def f_cp(e, d=dst, ss=ps_ap[:, i*width:(i+1)*width]):
                e.tensor_copy(d, ss)
            prog.add("vector", f_cp)
    return pub(prog, s, "vector")


def emit_D(ctx):
    prog, s = ctx["prog"], ctx["sems"]
    nc, din, dint = ctx["nc"], ctx["din"], ctx["dint"]
    x_sb, g0, g1, g2, g3, g4, wbuf, sm, smi = (ctx[k] for k in
        ("x_sb", "g0", "g1", "g2", "g3", "g4", "wbuf", "sm", "smi"))
    ps0, ps1 = ctx["ps0"], ctx["ps1"]
    stW, stG = ctx["stW"], ctx["stG"]
    H = 16
    NC = NT

    xnfm = lambda dc: (g0 if dc < 8 else g1)[:, (dc % 8)*1024:(dc % 8)*1024+1024]
    S_all = lambda h: g3[:, h*128:(h+1)*128]
    S_recv = lambda h: g3[:, 2048 + h*128:2048 + (h+1)*128]
    q_fm = g4[:, 0:1024]; k_fm = g4[:, 1024:2048]
    v_tm = g4[:, 2048:3072]; k_tm = g4[:, 3072:4096]
    Dt = sm[:, 0:128]; qdrep = sm[:, 128:256]
    ld_col = lambda h: sm[:, 256+h:256+h+1]
    kdec = sm[:, 280:281]; cdec = sm[:, 282:283]; cdpow = sm[:, 284:285]
    col4 = sm[:, 300:304]
    kdc_c = sm[:, 304:305]; flag_c = sm[:, 306:307]
    diffTm = sm[:, 320:448]; maskT = sm[:, 448:576]; qdr_c = sm[:, 576:704]
    og_all = g2[:, 4096:5120]
    attnT = g2[:, 5120:5248]; qsc = g2[:, 5248:5376]
    ktil = g2[:, 5376:5504]
    t_fm = g2[:, 5504:6528]
    gate_h = g2[:, 6528:7552]
    mscol = sm[:, 704:712]; rstdc = sm[:, 712:720]
    mu_c = sm[:, 720:728]; sg_c = sm[:, 728:736]
    nw_h = sm[:, 736:864]
    onorm_fm = lambda hi: g2[:, hi*1024:(hi+1)*1024]

    cl = stW.group("sync", [
        (sm[:, 256:272], din["d_ld_rep"]),
        (kdc_c, din["d_kdc"]),
        (flag_c, din["d_flag"]),
        (diffTm, din["d_diffTm"]),
        (maskT, din["d_maskT"]),
        (qdr_c, din["d_qdr"]),
        (smi[:, 0:1], din["d_srow"]),
    ])
    prog.wait("vector", stW.ctr, cl)
    prog.wait("scalar", stW.ctr, cl)
    prog.wait("gpsimd", stW.ctr, cl)

    prog.wait("vector", s["smm"], s["smm"].v)
    prog.wait("vector", s["ssc"], s["ssc"].v)
    for t in range(NT):
        xt = x_sb[:, t*D:(t+1)*D]
        lnv = ln_tile(ctx, xt, g4[:, 0:D], g4[:, D:2*D], col4,
                      save_musig=(mu_c[:, t:t+1], rstdc[:, t:t+1]))
        prog.wait("tensor", s["svec"], lnv)
        for half in range(2):
            srcs = [g4[:, kc*128:(kc+1)*128] for kc in range(half*8, half*8+8)]
            dsts = [xnfm(kc)[:, t*128:(t+1)*128] for kc in range(half*8, half*8+8)]
            cv = transpose_block(ctx, srcs, dsts, ps1[:, 0:1024])
            prog.wait("tensor", s["svec"], cv)
    prog.wait("vector", s["ssc"], s["ssc"].v)
    def f_sig(e, r=rstdc[:, 0:NT], o=sg_c[:, 0:NT]):
        e.reciprocal(o, r)
        e.drain()
        e.memset(x_sb[:], 0.0)
    prog.add("vector", f_sig)
    xn_done = pub(prog, s, "vector")
    stW.consumed(s["svec"], xn_done)

    gv = stW.group("sync", [(wbuf[:, 0:2048],
                             din["d_Wg1"].rearrange("(dc p) f -> p dc f", p=128))])
    prog.wait("tensor", stW.ctr, gv)
    prog.wait("tensor", s["svec"], xn_done)
    fns = []
    for tg in range(2):
        for dc in range(NDC):
            def f_mm(e, d=ps0[:, tg*512:(tg+1)*512], l=wbuf[:, dc*128:(dc+1)*128],
                     r=xnfm(dc)[:, tg*512:(tg+1)*512], st_=(dc == 0), sp=(dc == NDC-1)):
                return e.matmul(d, l, r, start=st_, stop=sp)
            fns.append(f_mm)
    mmv = tensor_group(prog, s, fns)
    stW.consumed(s["smm"], mmv)
    prog.wait("vector", s["smm"], mmv)
    prog.add("vector", lambda e: e.tensor_copy(t_fm, ps0[:, 0:1024]))
    tfm_done = pub(prog, s, "vector")
    prog.wait("tensor", s["svec"], tfm_done)

    for h in range(H):
        prog.wait("sync", s["smm"], s["smm"].v)
        wv = stW.group("sync", [(wbuf[:, 0:2048],
            din["d_Wk"].rearrange("(dc p) f -> p dc f", p=128)[:, :, h*128:(h+1)*128])])
        prog.wait("tensor", stW.ctr, wv)
        if h > 0:
            prog.wait("scalar", stG.ctr, stG.ctr.v)
        for tg in range(2):
            fns = []
            for dc in range(NDC):
                def f_mm(e, d=ps0[:, tg*512:(tg+1)*512], l=wbuf[:, dc*128:(dc+1)*128],
                         r=xnfm(dc)[:, tg*512:(tg+1)*512], st_=(dc == 0), sp=(dc == NDC-1)):
                    return e.matmul(d, l, r, start=st_, stop=sp)
                fns.append(f_mm)
            mmv = tensor_group(prog, s, fns)
            prog.wait("scalar", s["smm"], mmv)
            def f_silu(e, d=k_fm[:, tg*512:(tg+1)*512], ss=ps0[:, tg*512:(tg+1)*512]):
                e.activation(d, ss, AF.Silu)
            prog.add("scalar", f_silu)
        kfm_done = pub(prog, s, "scalar")
        stW.consumed(s["ssc"], kfm_done)

        wv = stW.group("sync", [(wbuf[:, 0:2048],
            din["d_Wv"].rearrange("(dc p) f -> p dc f", p=128)[:, :, h*128:(h+1)*128])])
        prog.wait("tensor", stW.ctr, wv)
        prog.wait("tensor", s["ssc"], kfm_done)
        for tg in range(2):
            fns = []
            for dc in range(NDC):
                def f_mm(e, d=ps0[:, tg*512:(tg+1)*512], l=wbuf[:, dc*128:(dc+1)*128],
                         r=xnfm(dc)[:, tg*512:(tg+1)*512], st_=(dc == 0), sp=(dc == NDC-1)):
                    return e.matmul(d, l, r, start=st_, stop=sp)
                fns.append(f_mm)
            mmv = tensor_group(prog, s, fns)
            prog.wait("scalar", s["smm"], mmv)
            def f_silu(e, d=q_fm[:, tg*512:(tg+1)*512], ss=ps0[:, tg*512:(tg+1)*512]):
                e.activation(d, ss, AF.Silu)
            prog.add("scalar", f_silu)
        vfm_done = pub(prog, s, "scalar")
        stW.consumed(s["ssc"], vfm_done)
        prog.wait("tensor", s["ssc"], vfm_done)
        cv = transpose_block(ctx, [q_fm[:, c*128:(c+1)*128] for c in range(8)],
                             [v_tm[:, c*128:(c+1)*128] for c in range(8)], ps1[:, 0:1024])
        cv2 = transpose_block(ctx, [k_fm[:, c*128:(c+1)*128] for c in range(8)],
                              [k_tm[:, c*128:(c+1)*128] for c in range(8)], ps1[:, 1024:2048])
        prog.wait("gpsimd", s["svec"], cv2)
        stG.group("gpsimd", [
            (dint["kvpark"][(2*h)*128:(2*h+1)*128, :], k_fm),
            (dint["kvpark"][(2*h+1)*128:(2*h+2)*128, :], q_fm),
        ])
        def f_kd(e, ldc=ld_col(h), o=kdec, kc=kdc_c):
            e.tensor_scalar(o, kc, ldc, None, OP.mult, OP.bypass)
        prog.add("vector", f_kd)
        vv = pub(prog, s, "vector")
        prog.wait("scalar", s["svec"], vv)
        def f_kd2(e, o=kdec, ldc=ld_col(h), cd=cdec):
            e.activation(o, o, AF.Exp)
            e.activation(cd, ldc, AF.Exp, scale=128.0)
        prog.add("scalar", f_kd2)
        scv = pub(prog, s, "scalar")
        prog.wait("vector", s["ssc"], scv)
        prog.add("vector", lambda e, Sh=S_all(h): (e.memset(Sh, 0.0), e.drain()))
        for c in range(NC):
            def f_kt(e, kt=k_tm[:, c*128:(c+1)*128], kd=kdec, o=ktil):
                e.tensor_scalar(o, kt, kd, None, OP.mult, OP.bypass)
            prog.add("vector", f_kt)
            kv_ = pub(prog, s, "vector")
            prog.wait("tensor", s["svec"], kv_)
            bank = ps1[:, (c % 2)*512:(c % 2)*512+128]
            mmv = tensor_group(prog, s, [
                lambda e, d=bank, l=ktil, r=v_tm[:, c*128:(c+1)*128]:
                e.matmul(d, l, r, start=True, stop=True)])
            prog.wait("vector", s["smm"], mmv)
            def f_su(e, Sh=S_all(h), cd=cdec, ps=bank):
                e.scalar_tensor_tensor(Sh, Sh, cd, ps, OP.mult, OP.add)
            prog.add("vector", f_su)
            sv_ = pub(prog, s, "vector")
            prog.wait("tensor", s["svec"], sv_)

    p1_done = pub(prog, s, "vector")
    prog.wait("gpsimd", s["svec"], p1_done)
    sdump = stG.group("gpsimd", [(dint["s_my"], g3[:, 0:2048])])
    prog.wait("gpsimd", stG.ctr, sdump)
    csem = ctx["new_coll_sem"]()
    def f_coll(e, h=csem.h):
        e.collective_compute("AllGather", OP.bypass, replica_groups=PAIR_GROUPS,
                             ins=[dint["s_my"]], outs=[dint["s_ag"]]).then_inc(h, 1)
    prog.add("gpsimd", f_coll)
    csem.bump()
    prog.wait("gpsimd", csem)
    srv = stG.group("gpsimd", pairs=[], indirect=[
        (g3[:, 2048:4096], None, dint["s_ag"],
         IndirectOffsetOnAxis(smi[:, 0:1], 0), OP.bypass)])
    prog.wait("tensor", stG.ctr, srv)
    prog.wait("vector", stG.ctr, srv)

    prog.wait("sync", stG.ctr, stG.ctr.v)
    for grp in range(4):
        for hi in range(4):
            h = grp*4 + hi
            prog.wait("sync", s["smm"], s["smm"].v)
            kv_l = stW.group("sync", [
                (k_fm, dint["kvpark"][(2*h)*128:(2*h+1)*128, :]),
                (q_fm, dint["kvpark"][(2*h+1)*128:(2*h+2)*128, :]),
                (nw_h, din["d_normw_rep"][:, h*128:(h+1)*128]),
                (wbuf[:, 0:128], din["d_Wg2"][:, h*128:(h+1)*128]),
            ])
            prog.wait("tensor", stW.ctr, kv_l)
            if h == 0:
                prog.wait("tensor", s["svec"], tfm_done)
            fns = []
            for tt in range(NT):
                def f_mm(e, d=ps0[:, (tt % 8)*128:(tt % 8)*128+128],
                         l=t_fm[:, tt*128:(tt+1)*128], r=wbuf[:, 0:128]):
                    return e.matmul(d, l, r, start=True, stop=True)
                fns.append(f_mm)
            mmg = tensor_group(prog, s, fns)
            prog.wait("scalar", s["smm"], mmg)
            def f_sg2(e, d=gate_h, ss=ps0[:, 0:1024]):
                e.activation(d, ss, AF.Sigmoid)
            prog.add("scalar", f_sg2)
            gsv = pub(prog, s, "scalar")
            prog.wait("tensor", s["ssc"], gsv)
            prog.wait("tensor", stW.ctr, kv_l)
            cv = transpose_block(ctx, [q_fm[:, c*128:(c+1)*128] for c in range(8)],
                                 [v_tm[:, c*128:(c+1)*128] for c in range(8)], ps1[:, 0:1024])
            cv2 = transpose_block(ctx, [k_fm[:, c*128:(c+1)*128] for c in range(8)],
                                  [k_tm[:, c*128:(c+1)*128] for c in range(8)], ps1[:, 1024:2048])
            stW.consumed(s["svec"], cv2)
            qw = stW.group("sync", [(wbuf[:, 0:2048],
                din["d_Wq"].rearrange("(dc p) f -> p dc f", p=128)[:, :, h*128:(h+1)*128])])
            prog.wait("tensor", stW.ctr, qw)
            prog.wait("tensor", s["svec"], cv)
            prog.wait("scalar", s["svec"], cv)
            for tg in range(2):
                fns = []
                for dc in range(NDC):
                    def f_mm(e, d=ps0[:, tg*512:(tg+1)*512], l=wbuf[:, dc*128:(dc+1)*128],
                             r=xnfm(dc)[:, tg*512:(tg+1)*512], st_=(dc == 0), sp=(dc == NDC-1)):
                        return e.matmul(d, l, r, start=st_, stop=sp)
                    fns.append(f_mm)
                mmv = tensor_group(prog, s, fns)
                prog.wait("scalar", s["smm"], mmv)
                def f_silu(e, d=q_fm[:, tg*512:(tg+1)*512], ss=ps0[:, tg*512:(tg+1)*512]):
                    e.activation(d, ss, AF.Silu)
                prog.add("scalar", f_silu)
            qfm_done = pub(prog, s, "scalar")
            stW.consumed(s["ssc"], qfm_done)
            prog.wait("vector", s["ssc"], qfm_done)
            def f_dt(e, ldc=ld_col(h), o=Dt, df=diffTm, qr=qdrep, qc=qdr_c, kd=kdec, kc=kdc_c):
                e.tensor_scalar(o, df, ldc, None, OP.mult, OP.bypass)
                e.tensor_scalar(qr, qc, ldc, None, OP.mult, OP.bypass)
                e.tensor_scalar(kd, kc, ldc, None, OP.mult, OP.bypass)
            prog.add("vector", f_dt)
            dv_ = pub(prog, s, "vector")
            prog.wait("scalar", s["svec"], dv_)
            def f_dt2(e, o=Dt, qr=qdrep, kd=kdec, ldc=ld_col(h), cd=cdec):
                e.activation(o, o, AF.Exp)
                e.activation(qr, qr, AF.Exp)
                e.activation(kd, kd, AF.Exp)
                e.activation(cd, ldc, AF.Exp, scale=128.0)
            prog.add("scalar", f_dt2)
            dsv = pub(prog, s, "scalar")
            prog.wait("vector", s["ssc"], dsv)
            def f_dt3(e, o=Dt, mk=maskT, Sh=S_all(h), Sr=S_recv(h), fl=flag_c):
                e.tensor_mul(o, o, mk)
                e.tensor_scalar(Sh, Sr, fl, None, OP.mult, OP.bypass)
                e.drain()
            prog.add("vector", f_dt3)
            dmv = pub(prog, s, "vector")
            prog.wait("tensor", s["svec"], dmv)

            for c in range(NC):
                pS = c % 2
                b_sc = ps0[:, 1024 + pS*512: 1024 + pS*512 + 128]
                b_kv = ps1[:, pS*512: pS*512 + 128]
                b_o = ps1[:, 1024 + pS*512: 1024 + pS*512 + 128]
                mm_sc = tensor_group(prog, s, [
                    lambda e, d=b_sc, l=k_fm[:, c*128:(c+1)*128], r=q_fm[:, c*128:(c+1)*128]:
                    e.matmul(d, l, r, start=True, stop=True)])
                prog.wait("vector", s["smm"], mm_sc)
                def f_at(e, o=attnT, ps=b_sc, dt=Dt, kt=k_tm[:, c*128:(c+1)*128],
                         kd=kdec, kk=ktil, qf=q_fm[:, c*128:(c+1)*128], qd=qdrep, qq=qsc):
                    e.tensor_mul(o, ps, dt)
                    e.tensor_scalar(kk, kt, kd, None, OP.mult, OP.bypass)
                    e.tensor_mul(qq, qf, qd)
                prog.add("vector", f_at)
                at_v = pub(prog, s, "vector")
                prog.wait("tensor", s["svec"], at_v)
                mm_o3 = tensor_group(prog, s, [
                    lambda e, d=b_o, l=attnT, r=v_tm[:, c*128:(c+1)*128]:
                    e.matmul(d, l, r, start=True, stop=False),
                    lambda e, d=b_o, l=qsc, r=S_all(h):
                    e.matmul(d, l, r, start=False, stop=True),
                    lambda e, d=b_kv, l=ktil, r=v_tm[:, c*128:(c+1)*128]:
                    e.matmul(d, l, r, start=True, stop=True),
                ])
                prog.wait("vector", s["smm"], mm_o3)
                def f_su(e, Sh=S_all(h), cd=cdec, ps=b_kv, po=b_o,
                         gs=gate_h[:, c*128:(c+1)*128],
                         og=og_all[:, c*128:(c+1)*128]):
                    e.scalar_tensor_tensor(Sh, Sh, cd, ps, OP.mult, OP.add)
                    e.tensor_mul(og, po, gs)
                prog.add("vector", f_su)
                su_v = pub(prog, s, "vector")
                prog.wait("tensor", s["svec"], su_v)

            prog.wait("scalar", s["svec"], s["svec"].v)
            for c in range(NC):
                def f_sq(e, og=og_all[:, c*128:(c+1)*128], sq=attnT, ms=mscol[:, c:c+1]):
                    e.activation(sq, og, AF.Square, accum_out=ms)
                    e.drain()
                prog.add("scalar", f_sq)
            sqv = pub(prog, s, "scalar")
            prog.wait("vector", s["ssc"], sqv)
            def f_rs(e, ms=mscol[:, 0:8], o=rstdc[:, 0:8]):
                e.tensor_scalar(o, ms, 1.0/128.0, 1e-5, OP.mult, OP.add)
            prog.add("vector", f_rs)
            rsv = pub(prog, s, "vector")
            prog.wait("scalar", s["svec"], rsv)
            def f_rs2(e, o=rstdc[:, 0:8]):
                e.activation(o, o, AF.Ln)
                e.drain()
                e.activation(o, o, AF.Exp, scale=-0.5)
            prog.add("scalar", f_rs2)
            rs2v = pub(prog, s, "scalar")
            prog.wait("vector", s["ssc"], rs2v)
            def f_on(e, og=og_all.rearrange("p (c k) -> p c k", k=128),
                     rs=rstdc[:, 0:8].unsqueeze(2).broadcast_to([128, 8, 128]),
                     nw=nw_h.unsqueeze(1).broadcast_to([128, 8, 128])):
                e.tensor_tensor(og, og, rs, OP.mult)
                e.drain()
                e.tensor_tensor(og, og, nw, OP.mult)
            prog.add("vector", f_on)
            onv = pub(prog, s, "vector")
            prog.wait("tensor", s["svec"], onv)
            cvo = transpose_block(ctx, [og_all[:, c*128:(c+1)*128] for c in range(8)],
                                  [onorm_fm(hi)[:, c*128:(c+1)*128] for c in range(8)],
                                  ps1[:, 0:1024])
            prog.wait("tensor", s["svec"], cvo)

        for j in range(4):
            mmv = None
            for hi in range(4):
                h = grp*4 + hi
                ww = stW.group("sync", [(wbuf[:, hi*512:(hi+1)*512],
                                         din["d_Wo"][h*128:(h+1)*128, j*512:(j+1)*512])])
                prog.wait("tensor", stW.ctr, ww)
                fns = []
                for tt in range(NT):
                    pst = (ps0 if tt < 4 else ps1)[:, (tt % 4)*512:(tt % 4)*512+512]
                    def f_mm(e, d=pst, l=onorm_fm(hi)[:, tt*128:(tt+1)*128],
                             r=wbuf[:, hi*512:(hi+1)*512], st_=(hi == 0), sp=(hi == 3)):
                        return e.matmul(d, l, r, start=st_, stop=sp)
                    fns.append(f_mm)
                mmv = tensor_group(prog, s, fns)
                if hi == 3:
                    stW.consumed(s["smm"], mmv)
            prog.wait("vector", s["smm"], mmv)
            for tt in range(NT):
                pst = (ps0 if tt < 4 else ps1)[:, (tt % 4)*512:(tt % 4)*512+512]
                def f_acc(e, xo=x_sb[:, tt*D + j*512: tt*D + (j+1)*512], ps=pst):
                    e.tensor_add(xo, xo, ps)
                    e.drain()
                prog.add("vector", f_acc)
            av = pub(prog, s, "vector")
            prog.wait("tensor", s["svec"], av)

    for tt in range(NT):
        fns = []
        for dc in range(NDC):
            def f_t(e, src=xnfm(dc)[:, tt*128:(tt+1)*128], d=ps0[:, dc*128:(dc+1)*128],
                    idn=ctx["ident"]):
                return e.transpose(d, src, idn[:])
            fns.append(f_t)
        mmv = tensor_group(prog, s, fns)
        prog.wait("vector", s["smm"], mmv)
        def f_fin(e, ps=ps0[:, 0:2048], xo=x_sb[:, tt*D:(tt+1)*D],
                  sg=sg_c[:, tt:tt+1], mu=mu_c[:, tt:tt+1], tmp=g4[:, 0:2048]):
            e.tensor_scalar(tmp, ps, sg, mu, OP.mult, OP.add)
            e.drain()
            e.tensor_add(xo, xo, tmp)
        prog.add("vector", f_fin)
        fv = pub(prog, s, "vector")
        prog.wait("tensor", s["svec"], fv)



def make_inputs(core, full, sublayers=None):
    if sublayers is None:
        sublayers = SUBLAYERS
    r, h = core // 2, core % 2
    f32 = lambda k: np.ascontiguousarray(np.asarray(full[k], np.float32))
    rep = lambda v: np.ascontiguousarray(np.broadcast_to(
        np.asarray(v, np.float32), (128, len(np.asarray(v).reshape(-1)))))
    X = np.asarray(full["L0_attn_x"], dtype=np.float32)
    m = {
        "x_in": np.ascontiguousarray(X[r, h*TC:(h+1)*TC, :]),
        "ident": np.eye(128, dtype=np.float32),
    }
    if "C" in sublayers or "C2" in sublayers:
        m["c_weight"] = f32("L1_ffn_weight")
        m["c_bias_rep"] = rep(full["L1_ffn_bias"])
    if "A" in sublayers or "A2" in sublayers:
        h = core % 2
        m["a_wq"] = f32("L0_ffn_wq"); m["a_wk"] = f32("L0_ffn_wk")
        m["a_wv"] = f32("L0_ffn_wv"); m["a_wo"] = f32("L0_ffn_wo")
        m["a_w1"] = f32("L0_ffn_w1"); m["a_w3"] = f32("L0_ffn_w3")
        m["a_w2"] = f32("L0_ffn_w2")
        m["a_g1_rep"] = rep(full["L0_ffn_g1"]); m["a_g2_rep"] = rep(full["L0_ffn_g2"])
        m["a_wrt_rep"] = rep(full["L0_ffn_w_router"])
        msk = np.asarray(full["L0_ffn_mask"], np.float32)
        m["a_mask_my"] = np.ascontiguousarray(msk[h*SELC:(h+1)*SELC, :])
        fc = np.asarray(full["L0_ffn_freqs_cis"], np.float32)[h*SELC:(h+1)*SELC]
        cosf, sinf = fc[:, :, 0], fc[:, :, 1]
        tabA = np.concatenate([cosf, sinf], 1).reshape(4, 128, 128)
        tabB = np.concatenate([sinf, cosf], 1).reshape(4, 128, 128)
        m["a_ropeA"] = np.ascontiguousarray(tabA.transpose(1, 0, 2).reshape(128, 512))
        m["a_ropeB"] = np.ascontiguousarray(tabB.transpose(1, 0, 2).reshape(128, 512))
        m["a_iota"] = np.ascontiguousarray(
            np.broadcast_to(np.arange(T, dtype=np.float32)[None, :], (128, T)))
        pp = np.arange(128, dtype=np.float32)
        m["a_idxcol"] = np.ascontiguousarray(
            (np.arange(16, dtype=np.float32)[None, :]*128 + pp[:, None]))
        m["a_rcol"] = np.ascontiguousarray(
            (h*SELC + np.arange(4, dtype=np.float32)[None, :]*128 + pp[:, None]))
        m["a_ones1"] = np.ones((1, 128), np.float32)
        m["a_halfflag"] = np.full((128, 1), float(h), np.float32)
    if "B" in sublayers:
        h0col = (core % 2) * 1024
        m["b_Wq_my"] = np.ascontiguousarray(np.asarray(full["L0_attn_Wq"], np.float32)[:, h0col:h0col+1024])
        g0col = (core % 2) * 256
        m["b_Wk_my"] = np.ascontiguousarray(np.asarray(full["L0_attn_Wk"], np.float32)[:, g0col:g0col+256])
        m["b_Wv_my"] = np.ascontiguousarray(np.asarray(full["L0_attn_Wv"], np.float32)[:, g0col:g0col+256])
        m["b_Wo"] = f32("L0_attn_Wo")
        inv = (1.0 / 10000.0 ** (np.arange(32, dtype=np.float64) * 2.0 / 64.0))
        ang = np.arange(T, dtype=np.float64)[:, None] * inv[None, :]
        cosr = np.cos(ang).astype(np.float32).reshape(16, 128, 32)
        sinr = np.sin(ang).astype(np.float32).reshape(16, 128, 32)
        ra = np.concatenate([cosr, sinr], -1)
        rb = np.concatenate([sinr, cosr], -1)
        m["b_ropeA"] = np.ascontiguousarray(ra.transpose(1, 0, 2).reshape(128, 1024))
        m["b_ropeB"] = np.ascontiguousarray(rb.transpose(1, 0, 2).reshape(128, 1024))
        i_ = np.arange(128)
        m["b_causal"] = np.where(i_[None, :] <= i_[:, None], 0.0, -1e30).astype(np.float32)
    if "D" in sublayers:
        m["d_Wq"] = f32("L1_attn_Wq"); m["d_Wk"] = f32("L1_attn_Wk")
        m["d_Wv"] = f32("L1_attn_Wv"); m["d_Wo"] = f32("L1_attn_Wo")
        m["d_Wg1"] = f32("L1_attn_Wg1"); m["d_Wg2"] = f32("L1_attn_Wg2")
        m["d_normw_rep"] = rep(full["L1_attn_norm_w"])
        m["d_ld_rep"] = rep(full["L1_attn_log_decay"])
        i_ = np.arange(128, dtype=np.float32)
        cmi = i_[None, :] - i_[:, None]
        m["d_diffTm"] = np.where(cmi >= 0, cmi, 0.0).astype(np.float32)
        m["d_maskT"] = (cmi >= 0).astype(np.float32)
        m["d_kdc"] = np.ascontiguousarray((127.0 - i_)[:, None].astype(np.float32))
        m["d_qdr"] = np.ascontiguousarray(
            np.broadcast_to((i_ + 1.0)[None, :], (128, 128)).astype(np.float32))
        m["d_flag"] = np.full((128, 1), float(h), dtype=np.float32)
        m["d_srow"] = np.ascontiguousarray(
            ((1 - h) * 128 + np.arange(128, dtype=np.int32))[:, None])
    return m


def kernel(**inputs):
    nc = build_nc()
    in_maps = [make_inputs(c, inputs, SUBLAYERS) for c in range(8)]
    res = run_bass_kernel_spmd(nc, in_maps, core_ids=list(range(8)))
    out = np.empty((B, T, D), dtype=np.float32)
    for c in range(8):
        r, h = c // 2, c % 2
        out[r, h*TC:(h+1)*TC, :] = res.results[c]["y"]
    return out


# version 05
# speedup vs baseline: 1.0216x; 1.0115x over previous
import math
import numpy as np
from contextlib import ExitStack

import concourse.bass as bass
import concourse.mybir as mybir
from concourse.bass_utils import run_bass_kernel_spmd
from concourse.bass_core_rust import IndirectOffsetOnAxis

F32 = mybir.dt.float32
I32 = mybir.dt.int32
AF = mybir.ActivationFunctionType
OP = mybir.AluOpType
AX = mybir.AxisListType

B, T, D = 4, 2048, 2048
TC = 1024
NT = TC // 128
NDC = D // 128
FFN = 8192
SEL = 1024
SELC = SEL // 2
LN2 = 0.6931471805599453

SUBLAYERS = ["A", "B", "A2", "C", "D", "C2"]
PAIR_GROUPS = [[0, 1], [2, 3], [4, 5], [6, 7]]


class Ctr:
    def __init__(self, handle):
        self.h = handle
        self.v = 0

    def bump(self, n=1):
        self.v += n
        return self.v


class Prog:
    ENGINES = ("sync", "tensor", "vector", "scalar", "gpsimd")

    def __init__(self, nc):
        self.nc = nc
        self.q = {e: [] for e in self.ENGINES}

    def add(self, eng, fn):
        self.q[eng].append(fn)

    def wait(self, eng, ctr, val=None):
        v = ctr.v if val is None else val
        h = ctr.h
        self.add(eng, lambda e, h=h, v=v: e.wait_ge(h, v))

    def run(self, block):
        for eng in self.ENGINES:
            items = self.q[eng]
            if not items:
                continue
            def body(e, items=items):
                for fn in items:
                    fn(e)
            getattr(block, eng)(body)


class Stream:
    def __init__(self, prog, ctr):
        self.prog = prog
        self.ctr = ctr
        self.gate = None
        self.dirty = False

    def group(self, eng, pairs, indirect=None):
        if self.gate is not None:
            self.prog.wait(eng, self.gate[0], self.gate[1])
        elif self.dirty:
            self.prog.wait(eng, self.ctr, self.ctr.v)
        self.gate = None
        h = self.ctr.h
        for out_ap, in_ap in pairs:
            def fn(e, o=out_ap, i=in_ap, h=h):
                e.dma_start(out=o, in_=i).then_inc(h, 16)
            self.prog.add(eng, fn)
            self.ctr.v += 16
        if indirect:
            for out_ap, out_off, in_ap, in_off, cop in indirect:
                def fn(e, o=out_ap, oo=out_off, i=in_ap, io=in_off, c=cop, h=h):
                    e.indirect_dma_start(o, oo, i, io, compute_op=c).then_inc(h, 16)
                self.prog.add(eng, fn)
                self.ctr.v += 16
        self.dirty = True
        return self.ctr.v

    def consumed(self, ctr, val):
        self.gate = (ctr, val)
        self.dirty = False


class Rot:
    def __init__(self, prog, ctrs):
        self.prog = prog
        self.ctrs = ctrs
        self.n = len(ctrs)
        self.hist = {}
        self.i = 0

    def load(self, eng, pairs):
        slot = self.i % self.n
        c = self.ctrs[slot]
        g = self.hist.get(slot)
        if g is not None:
            self.prog.wait(eng, g[0], g[1])
        elif c.v > 0:
            self.prog.wait(eng, c, c.v)
        for out_ap, in_ap in pairs:
            def fn(e, o=out_ap, i=in_ap, h=c.h):
                e.dma_start(out=o, in_=i).then_inc(h, 16)
            self.prog.add(eng, fn)
            c.v += 16
        self.i += 1
        return (c, c.v)

    def consumed(self, ctr, val, back=0):
        slot = (self.i - 1 - back) % self.n
        self.hist[slot] = (ctr, val)

    def reset(self, ctr, val):
        for sl in range(self.n):
            self.hist[sl] = (ctr, val)


def pub(prog, s, eng):
    key = {"vector": "svec", "scalar": "ssc"}[eng]
    c = s[key]
    prog.add(eng, lambda e, h=c.h: e.drain().then_inc(h, 1))
    return c.bump()


def tensor_group(prog, s, fns):
    assert fns
    n = len(fns)
    for i, fn in enumerate(fns):
        if i == n - 1:
            def wrap(e, fn=fn, h=s["smm"].h):
                fn(e).then_inc(h, 1)
            prog.add("tensor", wrap)
        else:
            prog.add("tensor", fn)
    return s["smm"].bump()


def build_nc(sublayers=None):
    if sublayers is None:
        sublayers = SUBLAYERS
    nc = bass.Bass("TRN2", target_bir_lowering=False, debug=False)
    st = ExitStack()

    din = {}
    def I(name, shape, dt=F32):
        din[name] = nc.dram_tensor(name, list(shape), dt, kind="ExternalInput").ap()
        return din[name]

    I("x_in", (TC, D))
    I("ident", (128, 128))
    if "C" in sublayers or "C2" in sublayers:
        I("c_weight", (D, D))
        I("c_bias_rep", (128, D))
    if "A" in sublayers or "A2" in sublayers:
        I("a_wq", (D, D)); I("a_wk", (D, D)); I("a_wv", (D, D)); I("a_wo", (D, D))
        I("a_w1", (D, FFN)); I("a_w3", (D, FFN)); I("a_w2", (FFN, D))
        I("a_g1_rep", (128, D)); I("a_g2_rep", (128, D)); I("a_wrt_rep", (128, D))
        I("a_mask_my", (SELC, SEL))
        I("a_ropeA", (128, 512)); I("a_ropeB", (128, 512))
        I("a_iota", (128, T))
        I("a_idxcol", (128, 16))
        I("a_rcol", (128, 4))
        I("a_ones1", (1, 128))
        I("a_halfflag", (128, 1))
    if "B" in sublayers:
        I("b_Wq_my", (D, 1024))
        I("b_Wk_my", (D, 256)); I("b_Wv_my", (D, 256))
        I("b_Wo", (D, D))
        I("b_ropeA", (128, NQT_G * 64)) if False else None
        I("b_ropeA", (128, 1024))
        I("b_ropeB", (128, 1024))
        I("b_causal", (128, 128))
    if "D" in sublayers:
        I("d_Wq", (D, D)); I("d_Wk", (D, D)); I("d_Wv", (D, D)); I("d_Wo", (D, D))
        I("d_Wg1", (D, 128)); I("d_Wg2", (128, D))
        I("d_normw_rep", (128, D))
        I("d_ld_rep", (128, 16))
        I("d_diffTm", (128, 128))
        I("d_maskT", (128, 128))
        I("d_kdc", (128, 1))
        I("d_qdr", (128, 128))
        I("d_flag", (128, 1))
        I("d_srow", (128, 1), I32)

    y = nc.dram_tensor("y", [TC, D], F32, kind="ExternalOutput").ap()

    dint = {}
    def Dram(name, shape, dt=F32, **kw):
        dint[name] = nc.dram_tensor(name, list(shape), dt, kind="Internal", **kw).ap()
        return dint[name]

    if "C" in sublayers or "C2" in sublayers:
        Dram("wqdT", (D, D))
    if "B" in sublayers or "A" in sublayers or "A2" in sublayers:
        Dram("lnmy", (TC, D))
        Dram("ln_ag", (2 * TC, D), addr_space="Local")
    if "A" in sublayers or "A2" in sublayers:
        Dram("lgmy", (128, 8))
        Dram("lg_ag", (256, 8), addr_space="Local")
        Dram("kmy", (128, 16 * 512))
        Dram("k_ag", (256, 16 * 512), addr_space="Local")
        Dram("vmy", (SELC, D))
        Dram("v_ag", (SEL, D), addr_space="Local")
        Dram("xwmy", (SELC, D))
        Dram("xw_pad", (SEL + 1, D))
    if "D" in sublayers:
        Dram("kvpark", (32 * 128, TC))
        Dram("s_my", (128, 16 * 128))
        Dram("s_ag", (256, 16 * 128), addr_space="Local")

    sb = {}
    def S(name, shape, dt=F32):
        sb[name] = st.enter_context(nc.sbuf_tensor(name, list(shape), dt))
        return sb[name]

    x_sb = S("x", (128, NT * D))
    g0 = S("g0", (128, 8192))
    g1 = S("g1", (128, 8192))
    g2 = S("g2", (128, 8192))
    g3 = S("g3", (128, 4096))
    g4 = S("g4", (128, 4096))
    wbuf = S("wbuf", (128, 2048))
    sm = S("sm", (128, 1024), F32)
    smi = S("smi", (128, 16), I32)
    ident_sb = S("ident_sb", (128, 128))

    ps0 = st.enter_context(nc.psum_tensor("ps0", [128, 2048], F32))
    ps1 = st.enter_context(nc.psum_tensor("ps1", [128, 2048], F32))

    sems = {}
    for name in ("smm", "svec", "ssc", "sout", "stW", "stG", "stX"):
        sems[name] = Ctr(st.enter_context(nc.semaphore(name)))
    rsems = [Ctr(st.enter_context(nc.semaphore(f"rot{i}"))) for i in range(8)]

    prog = Prog(nc)
    stW = Stream(prog, sems["stW"])
    stG = Stream(prog, sems["stG"])
    stX = Stream(prog, sems["stX"])

    coll_sems = []
    def new_coll_sem():
        c = Ctr(st.enter_context(nc.semaphore(f"coll{len(coll_sems)}")))
        coll_sems.append(c)
        return c

    rotA = Rot(prog, rsems[0:4])
    rotB = Rot(prog, rsems[4:6])
    rotC = Rot(prog, rsems[6:8])
    ctx = dict(nc=nc, prog=prog, din=din, dint=dint, sb=sb, y=y,
               rotA=rotA, rotB=rotB, rotC=rotC,
               x_sb=x_sb, g0=g0, g1=g1, g2=g2, g3=g3, g4=g4, wbuf=wbuf, sm=sm,
               smi=smi,
               ident=ident_sb, ps0=ps0, ps1=ps1, sems=sems,
               stW=stW, stG=stG, stX=stX, new_coll_sem=new_coll_sem)

    emit_prelude(ctx)
    first_c = True
    for sl in sublayers:
        emit_barrier(ctx)
        if sl in ("C", "C2"):
            emit_C(ctx, first=first_c)
            first_c = False
        elif sl in ("A", "A2"):
            emit_A(ctx)
        elif sl == "B":
            emit_B(ctx)
        elif sl == "D":
            emit_D(ctx)
    emit_barrier(ctx)
    emit_finale(ctx)

    with nc.Block() as block:
        prog.run(block)
    st.close()
    return nc



def emit_barrier(ctx):
    prog, s = ctx["prog"], ctx["sems"]
    stW, stG = ctx["stW"], ctx["stG"]
    for eng in ("sync", "tensor", "vector", "scalar", "gpsimd"):
        for c in (s["smm"], s["svec"], s["ssc"], stW.ctr, stG.ctr):
            if c.v > 0:
                prog.wait(eng, c, c.v)


def emit_prelude(ctx):
    prog, din, s = ctx["prog"], ctx["din"], ctx["sems"]
    stX = ctx["stX"]
    lv = stX.group("sync", [
        (ctx["x_sb"][:].rearrange("p (t f) -> p t f", t=NT),
         din["x_in"].rearrange("(t p) f -> p t f", p=128)),
        (ctx["ident"][:], din["ident"]),
    ])
    ctx["xs_ready"] = lv
    for eng in ("tensor", "vector", "scalar", "gpsimd"):
        prog.wait(eng, stX.ctr, lv)


def emit_finale(ctx):
    prog, s, y = ctx["prog"], ctx["sems"], ctx["y"]
    prog.wait("gpsimd", s["svec"])
    prog.wait("gpsimd", s["ssc"])
    prog.wait("gpsimd", s["smm"])
    h = s["sout"].h
    def fn(e, h=h):
        e.dma_start(out=y.rearrange("(t p) f -> p t f", p=128),
                    in_=ctx["x_sb"][:].rearrange("p (t f) -> p t f", t=NT)).then_inc(h, 16)
    prog.add("gpsimd", fn)
    s["sout"].bump(16)
    prog.wait("gpsimd", s["sout"])


def ln_tile(ctx, src_ap, dst_ap, sq_ap, col4, eps=1e-5, gain_rep=None, rms=False,
            save_musig=None):
    prog, s = ctx["prog"], ctx["sems"]
    c_sum = col4[:, 0:1]; c_ssq = col4[:, 1:2]; c_rstd = col4[:, 2:3]; c_b = col4[:, 3:4]
    inv_n = 1.0 / D

    prog.wait("scalar", s["svec"], s["svec"].v)
    def f_sc(e, src=src_ap, sq=sq_ap, cs=c_ssq):
        e.activation(sq, src, AF.Square, accum_out=cs)
    prog.add("scalar", f_sc)
    sc_done = pub(prog, s, "scalar")

    if not rms:
        prog.add("vector", lambda e, src=src_ap, c=c_sum: e.reduce_sum(c, src, axis=AX.X))
        prog.add("vector", lambda e: e.drain())
    prog.wait("vector", s["ssc"], sc_done)
    if rms:
        def f_v1(e, ci=c_ssq, co=c_rstd):
            e.tensor_scalar(co, ci, inv_n, eps, OP.mult, OP.add)
        prog.add("vector", f_v1)
    else:
        def f_v1(e, cs=c_sum, co=c_rstd, cb=c_b):
            e.tensor_scalar_mul(cb, cs, inv_n)
            e.drain()
            e.tensor_scalar(co, cb, cb, None, OP.mult, OP.bypass)
        prog.add("vector", f_v1)
        def f_v2(e, cq=c_ssq, co=c_rstd):
            e.drain()
            e.scalar_tensor_tensor(co, cq, inv_n, co, OP.mult, OP.subtract)
            e.drain()
            e.tensor_scalar_add(co, co, eps)
        prog.add("vector", f_v2)
    v_done = pub(prog, s, "vector")

    prog.wait("scalar", s["svec"], v_done)
    def f_sc2(e, ci=c_rstd):
        e.activation(ci, ci, AF.Ln)
        e.drain()
        e.activation(ci, ci, AF.Exp, scale=-0.5)
    prog.add("scalar", f_sc2)
    if save_musig is not None:
        mu_ap, rs_ap = save_musig
        def f_sc2b(e, mu=mu_ap, rs=rs_ap, cb=c_b, cr=c_rstd):
            e.drain()
            e.copy(mu, cb)
            e.copy(rs, cr)
        prog.add("scalar", f_sc2b)
    sc2_done = pub(prog, s, "scalar")

    prog.wait("vector", s["ssc"], sc2_done)
    if rms:
        if gain_rep is not None:
            def f_v3(e, src=src_ap, dst=dst_ap, cr=c_rstd, g=gain_rep):
                e.scalar_tensor_tensor(dst, src, cr, g, OP.mult, OP.mult)
            prog.add("vector", f_v3)
        else:
            def f_v3(e, src=src_ap, dst=dst_ap, cr=c_rstd):
                e.tensor_scalar_mul(dst, src, cr)
            prog.add("vector", f_v3)
    else:
        def f_v3(e, src=src_ap, dst=dst_ap, cr=c_rstd, cb=c_b):
            e.tensor_scalar(cb, cb, cr, -1.0, OP.mult, OP.mult)
            e.drain()
            e.tensor_scalar(dst, src, cr, cb, OP.mult, OP.add)
        prog.add("vector", f_v3)
    return pub(prog, s, "vector")


def emit_quant_dequant(ctx, src_ap, dst_ap, scr):
    prog, s = ctx["prog"], ctx["sems"]
    NB = D // 32
    f0 = scr["f0"]; b0 = scr["b0"]; b1 = scr["b1"]; b2 = scr["b2"]
    m2 = scr["m2"]; m4 = scr["m4"]; vbuf = scr["v"]
    src3 = src_ap.rearrange("p (nb k) -> p nb k", k=32)
    dst3 = dst_ap.rearrange("p (nb k) -> p nb k", k=32)
    f03 = f0.rearrange("p (nb k) -> p nb k", k=32)

    prog.add("vector", lambda e, i=src3, o=b0: e.reduce_max(o, i, axis=AX.X, apply_absolute_value=True))
    def f_exp(e, a=b0, ee=b2, m=b1):
        e.drain()
        e.tensor_scalar_max(a, a, 2.0**-126)
        e.drain()
        e.tensor_single_scalar(m, a, 1.0, OP.is_lt)
        e.drain()
        e.tensor_scalar(ee, m, -126.0, None, OP.mult, OP.bypass)
        e.tensor_scalar(m, m, 2.0**126 - 1.0, 1.0, OP.mult, OP.add)
        e.drain()
        e.tensor_mul(a, a, m)
        e.drain()
        for k in (64, 32, 16, 8, 4, 2, 1):
            e.tensor_single_scalar(m, a, float(2.0**k), OP.is_ge)
            e.drain()
            e.scalar_tensor_tensor(ee, m, float(k), ee, OP.mult, OP.add)
            kk = min(k, 16)
            e.tensor_scalar(m, m, float(2.0**-kk) - 1.0, 1.0, OP.mult, OP.add)
            e.drain()
            for _ in range(k // kk):
                e.tensor_mul(a, a, m)
                e.drain()
        e.tensor_scalar_add(ee, ee, -2.0)
    prog.add("vector", f_exp)
    v2 = pub(prog, s, "vector")

    prog.wait("scalar", s["svec"], v2)
    def f_s2(e, i=b2, o0=b0, o1=b1):
        e.activation(o0, i, AF.Exp, scale=LN2)
        e.activation(o1, i, AF.Exp, scale=-LN2)
    prog.add("scalar", f_s2)
    sc2 = pub(prog, s, "scalar")
    prog.wait("vector", s["ssc"], sc2)

    def f_q(e, i=src3, o=f03, rs=b1, q=f0, aq=vbuf):
        e.tensor_tensor(o, i, rs.unsqueeze(2).broadcast_to([128, NB, 32]), OP.mult)
        e.drain()
        e.tensor_scalar_mul(aq, q, -1.0)
        e.drain()
        e.tensor_max(aq, aq, q)
        e.drain()
    prog.add("vector", f_q)
    def f_lad(e, q=f0, aq=vbuf, a05=m2, a1=m4, o=dst_ap):
        e.tensor_single_scalar(a05, aq, 0.25, OP.is_gt)
        e.drain()
        e.scalar_tensor_tensor(a05, aq, 0.75, a05, OP.is_ge, OP.add)
        e.drain()
        e.scalar_tensor_tensor(a05, aq, 1.25, a05, OP.is_gt, OP.add)
        e.drain()
        e.scalar_tensor_tensor(a05, aq, 1.75, a05, OP.is_ge, OP.add)
        e.drain()
        e.tensor_single_scalar(a1, aq, 2.5, OP.is_gt)
        e.drain()
        e.scalar_tensor_tensor(a1, aq, 3.5, a1, OP.is_ge, OP.add)
        e.drain()
        e.scalar_tensor_tensor(a1, a05, 0.5, a1, OP.mult, OP.add)
        e.drain()
        e.tensor_single_scalar(a05, aq, 5.0, OP.is_gt)
        e.drain()
        e.scalar_tensor_tensor(a1, a05, 2.0, a1, OP.mult, OP.add)
        e.drain()
        e.tensor_single_scalar(a05, q, 0.0, OP.is_ge)
        e.drain()
        e.scalar_tensor_tensor(a05, a05, 2.0, a1, OP.mult, OP.mult)
        e.drain()
        e.tensor_sub(o, a05, a1)
        e.drain()
    prog.add("vector", f_lad)
    def f_out(e, o=dst3, sc=b0):
        e.tensor_tensor(o, o, sc.unsqueeze(2).broadcast_to([128, NB, 32]), OP.mult)
    prog.add("vector", f_out)
    return pub(prog, s, "vector")



def emit_C(ctx, first):
    prog, s = ctx["prog"], ctx["sems"]
    nc, din, dint = ctx["nc"], ctx["din"], ctx["dint"]
    x_sb, g0, g1, g2, g3, g4, wbuf, sm = (ctx[k] for k in
        ("x_sb", "g0", "g1", "g2", "g3", "g4", "wbuf", "sm"))
    ps0, ps1 = ctx["ps0"], ctx["ps1"]
    stW, stG = ctx["stW"], ctx["stG"]
    NB = D // 32

    scr = dict(f0=g2[:, 0:D], m2=g2[:, D:2*D], m4=g3[:, 0:D], v=g3[:, D:2*D],
               b0=sm[:, 0:NB], b1=sm[:, NB:2*NB], b2=sm[:, 2*NB:3*NB])

    if first:
        for nck in range(NDC):
            lv = stW.group("sync", [(wbuf[:, 0:D], din["c_weight"][nck*128:(nck+1)*128, :])])
            prog.wait("vector", stW.ctr, lv)
            if nck >= 1:
                prog.wait("vector", s["smm"], s["smm"].v)
                prog.wait("vector", stG.ctr, stG.ctr.v)
            qv = emit_quant_dequant(ctx, wbuf[:, 0:D], g0[:, 0:D], scr)
            stW.consumed(s["svec"], qv)
            prog.wait("tensor", s["svec"], qv)
            for half in range(2):
                fns = []
                for kc in range(half*8, half*8+8):
                    def f_t(e, src=g0[:, kc*128:(kc+1)*128],
                            d=ps1[:, (kc % 8)*128:(kc % 8)*128+128], idn=ctx["ident"]):
                        return e.transpose(d, src, idn[:])
                    fns.append(f_t)
                mmv = tensor_group(prog, s, fns)
                prog.wait("vector", s["smm"], mmv)
                def f_cp(e, d=g1[:, half*1024:half*1024+1024], ss=ps1[:, 0:1024]):
                    e.tensor_copy(d, ss)
                prog.add("vector", f_cp)
                cv = pub(prog, s, "vector")
                prog.wait("tensor", s["svec"], cv)
            prog.wait("gpsimd", s["svec"], s["svec"].v)
            stG.group("gpsimd", [(
                dint["wqdT"].rearrange("(kc p) n -> p kc n", p=128)[:, :, nck*128:(nck+1)*128],
                g1[:, 0:NDC*128].rearrange("p (kc n) -> p kc n", kc=NDC))])

    afm = lambda kc: (g0 if kc < 8 else g1)[:, (kc % 8)*1024:(kc % 8)*1024+1024]
    for t in range(NT):
        xt = x_sb[:, t*D:(t+1)*D]
        if first and t == 0:
            prog.wait("vector", stG.ctr, stG.ctr.v)
        lnv = ln_tile(ctx, xt, g4[:, 0:D], g4[:, D:2*D], sm[:, 3*NB:3*NB+4])
        qv = emit_quant_dequant(ctx, g4[:, 0:D], g4[:, D:2*D], scr)
        prog.wait("tensor", s["svec"], qv)
        for half in range(2):
            fns = []
            for kc in range(half*8, half*8+8):
                def f_t(e, src=g4[:, D + kc*128: D + (kc+1)*128],
                        d=ps1[:, (kc % 8)*128:(kc % 8)*128+128], idn=ctx["ident"]):
                    return e.transpose(d, src, idn[:])
                fns.append(f_t)
            mmv = tensor_group(prog, s, fns)
            prog.wait("vector", s["smm"], mmv)
            for kc in range(half*8, half*8+8):
                def f_cp(e, d=afm(kc)[:, t*128:(t+1)*128],
                         ss=ps1[:, (kc % 8)*128:(kc % 8)*128+128]):
                    e.tensor_copy(d, ss)
                prog.add("vector", f_cp)
            cv = pub(prog, s, "vector")
            prog.wait("tensor", s["svec"], cv)
    afm_ready = s["svec"].v

    prog.wait("sync", s["svec"], afm_ready)
    bv = stW.group("sync", [(g3[:, 0:D], din["c_bias_rep"])])
    prog.wait("vector", stW.ctr, bv)
    prog.add("vector", lambda e, o=g4[:, 0:D], i=g3[:, 0:D]: e.tensor_scalar_mul(o, i, 0.5))
    hb_ready = pub(prog, s, "vector")
    stW.consumed(s["svec"], hb_ready)

    for j in range(4):
        wv = stW.group("sync", [(
            g2[:].rearrange("p (kc n) -> p kc n", kc=NDC),
            dint["wqdT"].rearrange("(kc p) n -> p kc n", p=128)[:, :, j*512:(j+1)*512])])
        prog.wait("tensor", stW.ctr, wv)
        if j == 0:
            prog.wait("tensor", s["svec"], afm_ready)
        fns = []
        for t in range(NT):
            pst = (ps0 if t < 4 else ps1)[:, (t % 4)*512:(t % 4)*512+512]
            for kc in range(NDC):
                def f_mm(e, d=pst, l=afm(kc)[:, t*128:(t+1)*128],
                         r=g2[:, kc*512:(kc+1)*512], st_=(kc == 0), sp=(kc == NDC-1)):
                    return e.matmul(d, l, r, start=st_, stop=sp)
                fns.append(f_mm)
        mmv = tensor_group(prog, s, fns)
        stW.consumed(s["smm"], mmv)
        prog.wait("vector", s["smm"], mmv)
        if j == 0:
            prog.wait("vector", s["svec"], hb_ready)
        for t in range(NT):
            pst = (ps0 if t < 4 else ps1)[:, (t % 4)*512:(t % 4)*512+512]
            def f_ev(e, ps=pst, hb=g4[:, j*512:(j+1)*512],
                     xt=x_sb[:, t*D+j*512:t*D+(j+1)*512],
                     tmp=g3[:, D+512*(t % 2):D+512*(t % 2)+512]):
                e.scalar_tensor_tensor(tmp, ps, 0.5, hb, OP.mult, OP.add)
                e.drain()
                e.tensor_add(xt, xt, tmp)
                e.drain()
            prog.add("vector", f_ev)
        ev = pub(prog, s, "vector")
        prog.wait("tensor", s["svec"], ev)


def emit_A(ctx):
    prog, s = ctx["prog"], ctx["sems"]
    nc, din, dint = ctx["nc"], ctx["din"], ctx["dint"]
    x_sb, g0, g1, g2, g3, g4, wbuf, sm, smi = (ctx[k] for k in
        ("x_sb", "g0", "g1", "g2", "g3", "g4", "wbuf", "sm", "smi"))
    ps0, ps1 = ctx["ps0"], ctx["ps1"]
    stW, stG = ctx["stW"], ctx["stG"]
    rotA = ctx["rotA"]
    NRT = 4
    NH = 16

    col4 = sm[:, 0:4]
    lgcols = sm[:, 16:32]
    cnt_c = sm[:, 32:48]; pos_c = sm[:, 48:64]
    mask_c = sm[:, 64:80]; tmp_c = sm[:, 80:96]
    rw_c = sm[:, 96:100]
    b2col = sm[:, 100:101]
    nmaxc = sm[:, 104:105]; rsumc = sm[:, 105:106]
    one1 = sm[:, 112:113]
    val_m = sm[:, 113:114]; val_d = sm[:, 114:115]
    selcols = smi[:, 0:4]
    srccols = smi[:, 8:16]

    cl = stW.group("sync", [
        (g3[:, 0:2048], din["a_iota"]),
        (sm[:, 128:144], din["a_idxcol"]),
        (sm[:, 144:148], din["a_rcol"]),
        (sm[:, 148:149], din["a_halfflag"]),
        (sm[0:1, 152:280], din["a_ones1"]),
    ])
    idxcol = sm[:, 128:144]; rcol = sm[:, 144:148]
    hf_col = sm[:, 148:149]
    ones1 = sm[0:1, 152:280]
    prog.wait("vector", stW.ctr, cl)
    prog.wait("tensor", stW.ctr, cl)
    gl = stW.group("sync", [(g2[:, 0:D], din["a_wrt_rep"])])
    wrt = g2[:, 0:D]
    prog.wait("vector", stW.ctr, gl)

    prog.wait("vector", s["smm"], s["smm"].v)
    prog.wait("vector", s["ssc"], s["ssc"].v)
    prog.wait("gpsimd", s["ssc"], s["ssc"].v)
    for t in range(NT):
        xt = x_sb[:, t*D:(t+1)*D]
        if t > 0:
            prog.wait("scalar", stG.ctr, stG.ctr.v)
            prog.wait("vector", stG.ctr, stG.ctr.v)
        lnv = ln_tile(ctx, xt, g4[:, 0:D], g4[:, D:2*D], col4)
        prog.wait("gpsimd", s["svec"], lnv)
        stG.group("gpsimd", [(dint["lnmy"][t*128:(t+1)*128, :], g4[:, 0:D])])
        def f_lg(e, h=g4[:, 0:D], w=wrt, scr=g4[:, D:2*D]):
            e.tensor_mul(scr, h, w)
            e.drain()
        prog.add("vector", f_lg)
        prog.add("vector", lambda e, scr=g4[:, D:2*D], lg=tmp_c[:, t:t+1]:
                 e.reduce_sum(lg, scr, axis=AX.X))
        def f_xu(e, xt=xt, h=g4[:, 0:D]):
            e.drain()
            e.scalar_tensor_tensor(xt, h, 0.5, xt, OP.mult, OP.add)
        prog.add("vector", f_xu)
        lgv = pub(prog, s, "vector")
    prog.wait("gpsimd", s["svec"], s["svec"].v)
    lgd = stG.group("gpsimd", [(dint["lgmy"], tmp_c[:, 0:8])])
    prog.wait("gpsimd", stG.ctr, lgd)
    csem = ctx["new_coll_sem"]()
    def f_coll(e, h=csem.h):
        e.collective_compute("AllGather", OP.bypass, replica_groups=PAIR_GROUPS,
                             ins=[dint["lgmy"]], outs=[dint["lg_ag"]]).then_inc(h, 1)
    prog.add("gpsimd", f_coll)
    csem.bump()
    prog.wait("sync", csem)
    lgl = stW.group("sync", [(lgcols[:, 0:8], dint["lg_ag"][0:128, :]),
                             (lgcols[:, 8:16], dint["lg_ag"][128:256, :])])
    prog.wait("vector", stW.ctr, lgl)
    prog.wait("scalar", stW.ctr, lgl)
    prog.wait("vector", stG.ctr, stG.ctr.v)
    prog.add("vector", lambda e: (e.memset(g4[:, 0:128], 1.0), e.drain()))
    onetile = g4[:, 0:128]

    def col_bcast(cols_ap, dst_base_fn, n=16):
        for c in range(n):
            def f_cr(e, o=g4[:, 128:256], one=onetile, cc=cols_ap[:, c:c+1]):
                e.tensor_scalar(o, one, cc, None, OP.mult, OP.bypass)
                e.drain()
            prog.add("vector", f_cr)
            crv = pub(prog, s, "vector")
            prog.wait("tensor", s["svec"], crv)
            cv = transpose_block(ctx, [g4[:, 128:256]], [dst_base_fn(c)], ps1[:, 0:128])
            prog.wait("tensor", s["svec"], cv)
        return cv

    col_bcast(lgcols, lambda c: g3[:, 2048 + c*128: 2048 + (c+1)*128])
    lg_bc = g3[:, 2048:4096]; iota_bc = g3[:, 0:2048]

    scr_rk = g0[:, 0:2048]
    for c in range(16):
        def f_rk(e, bc=lg_bc, lc=lgcols[:, c:c+1], scr=scr_rk, cnt=cnt_c[:, c:c+1],
                 it=iota_bc, ic=idxcol[:, c:c+1], eqc=tmp_c[:, c:c+1]):
            e.tensor_scalar(scr, bc, lc, 0.0, OP.is_gt, OP.add, accum_out=cnt)
            e.drain()
            e.tensor_scalar(scr, bc, lc, None, OP.is_equal, OP.bypass)
            e.drain()
            e.scalar_tensor_tensor(scr, it, ic, scr, OP.is_lt, OP.mult, accum_out=eqc)
            e.drain()
            e.tensor_add(cnt, cnt, eqc)
            e.drain()
        prog.add("vector", f_rk)
    def f_mk(e, cnt=cnt_c, mk=mask_c):
        e.tensor_single_scalar(mk, cnt, 1024.0, OP.is_lt)
        e.drain()
    prog.add("vector", f_mk)
    rkv = pub(prog, s, "vector")
    mask_bc = g4[:, 2048:4096]
    col_bcast(cnt_c, lambda c: mask_bc[:, c*128:(c+1)*128])
    def f_mb(e, d=mask_bc):
        e.tensor_single_scalar(d, d, 1024.0, OP.is_lt)
        e.drain()
    prog.add("vector", f_mb)
    mbv = pub(prog, s, "vector")
    for c in range(16):
        def f_ps(e, it=iota_bc, ic=idxcol[:, c:c+1], mb=mask_bc, scr=scr_rk,
                 pc=pos_c[:, c:c+1]):
            e.scalar_tensor_tensor(scr, it, ic, mb, OP.is_lt, OP.mult, accum_out=pc)
            e.drain()
        prog.add("vector", f_ps)
    def f_pp(e, pc=pos_c, mk=mask_c, t1=tmp_c, cn=cnt_c):
        e.scalar_tensor_tensor(t1, mk, -4096.0, pc, OP.mult, OP.add)
        e.drain()
        e.tensor_scalar_add(t1, t1, 4096.0)
        e.drain()
        e.tensor_scalar(cn, t1, 1024.0, None, OP.min, OP.bypass)
        e.drain()
    prog.add("vector", f_pp)
    pp_bc = g4[:, 2048:4096]
    col_bcast(tmp_c, lambda c: pp_bc[:, c*128:(c+1)*128])
    def f_src(e, cn=cnt_c, hfv=hf_col, dst=tmp_c[:, 0:8], dif=tmp_c[:, 8:16]):
        e.tensor_sub(dif, cn[:, 8:16], cn[:, 0:8])
        e.drain()
        e.scalar_tensor_tensor(dst, dif, hfv, cn[:, 0:8], OP.mult, OP.add)
        e.drain()
    prog.add("vector", f_src)
    def f_srci(e, dst=srccols, srcf=tmp_c[:, 0:8]):
        e.tensor_copy(dst, srcf)
        e.drain()
    prog.add("vector", f_srci)
    for rc in range(NRT):
        def f_sel(e, bc=pp_bc, rcc=rcol[:, rc:rc+1], scr=scr_rk, it=iota_bc,
                  out=tmp_c[:, 8+rc:8+rc+1]):
            e.tensor_scalar(scr, bc, rcc, None, OP.is_equal, OP.bypass)
            e.drain()
            e.tensor_mul(scr, scr, it)
            e.drain()
            e.reduce_sum(out, scr, axis=AX.X)
            e.drain()
        prog.add("vector", f_sel)
    def f_seli(e, dst=selcols, srcf=tmp_c[:, 8:12]):
        e.tensor_copy(dst, srcf)
        e.drain()
    prog.add("vector", f_seli)
    selv = pub(prog, s, "vector")

    def f_rw1(e, lg=lgcols, nm=val_m, em=tmp_c[:, 0:16] if False else None):
        e.reduce_max(nm, lg, axis=AX.X, negate=True)
        e.drain()
    prog.add("vector", f_rw1)
    r1v = pub(prog, s, "vector")
    prog.wait("tensor", s["svec"], r1v)
    cv = transpose_block(ctx, [val_m], [g4[0:1, 128:256]], ps1[0:1, 0:128])
    prog.wait("vector", s["svec"], cv)
    def f_rw2(e, row=g4[0:1, 128:256], nm=val_m[0:1, :]):
        e.reduce_max(nm, row, axis=AX.X)
        pass
    prog.add("vector", lambda e, row=g4[0:1, 128:256], nm=val_m[0:1, :]:
             e.tensor_reduce(nm, row, axis=AX.X, op=OP.min))
    r2v = pub(prog, s, "vector")
    prog.wait("tensor", s["svec"], r2v)
    mmv = tensor_group(prog, s, [lambda e, d=ps1[:, 0:1], l=ones1, r=val_m[0:1, :]:
                                 e.matmul(d, l, r, start=True, stop=True)])
    prog.wait("vector", s["smm"], mmv)
    prog.add("vector", lambda e: (e.tensor_copy(val_m, ps1[:, 0:1]), e.drain()))
    r3v = pub(prog, s, "vector")
    prog.wait("scalar", s["svec"], r3v)
    def f_rw3(e, lg=lgcols, nm=val_m, ec=scr_rk[:, 0:16]):
        e.activation(ec, lg, AF.Exp, bias=nm)
    prog.add("scalar", f_rw3)
    r4v = pub(prog, s, "scalar")
    prog.wait("vector", s["ssc"], r4v)
    def f_rw4(e, ec=scr_rk[:, 0:16], mk=mask_c, sd=val_d):
        e.tensor_mul(ec, ec, mk)
        e.drain()
        e.reduce_sum(sd, ec, axis=AX.X)
        e.drain()
    prog.add("vector", f_rw4)
    r5v = pub(prog, s, "vector")
    prog.wait("tensor", s["svec"], r5v)
    cv = transpose_block(ctx, [val_d], [g4[0:1, 128:256]], ps1[0:1, 0:128])
    prog.wait("vector", s["svec"], cv)
    prog.add("vector", lambda e, row=g4[0:1, 128:256], sd=val_d[0:1, :]:
             (e.reduce_sum(sd, row, axis=AX.X), e.drain()))
    r6v = pub(prog, s, "vector")
    prog.wait("scalar", s["svec"], r6v)
    def f_rw5(e, sd=val_d[0:1, :]):
        e.activation(sd, sd, AF.Ln)
    prog.add("scalar", f_rw5)
    r7v = pub(prog, s, "scalar")
    prog.wait("vector", s["ssc"], r7v)
    def f_rw6(e, sd=val_d[0:1, :], nm=val_m[0:1, :], b2=b2col[0:1, :]):
        e.tensor_sub(b2, nm, sd)
        e.drain()
        e.tensor_scalar_add(b2, b2, float(np.log(0.5)))
        e.drain()
    prog.add("vector", f_rw6)
    r8v = pub(prog, s, "vector")
    prog.wait("tensor", s["svec"], r8v)
    mmv = tensor_group(prog, s, [lambda e, d=ps1[:, 0:1], l=ones1, r=b2col[0:1, :]:
                                 e.matmul(d, l, r, start=True, stop=True)])
    prog.wait("vector", s["smm"], mmv)
    prog.add("vector", lambda e: (e.tensor_copy(b2col, ps1[:, 0:1]), e.drain()))
    b2v = pub(prog, s, "vector")
    prog.wait("tensor", s["svec"], b2v)

    prog.wait("gpsimd", stG.ctr, stG.ctr.v)
    csem2 = ctx["new_coll_sem"]()
    def f_coll2(e, h=csem2.h):
        e.collective_compute("AllGather", OP.bypass, replica_groups=PAIR_GROUPS,
                             ins=[dint["lnmy"]], outs=[dint["ln_ag"]]).then_inc(h, 1)
    prog.add("gpsimd", f_coll2)
    csem2.bump()
    prog.wait("gpsimd", csem2)
    prog.wait("gpsimd", s["svec"], selv)
    fgv = stG.group("gpsimd", pairs=[], indirect=[
        (g1[:, rc*2048:(rc+1)*2048], None, dint["ln_ag"],
         IndirectOffsetOnAxis(selcols[:, rc:rc+1], 0), OP.bypass) for rc in range(NRT)])
    prog.wait("vector", stG.ctr, fgv)
    prog.wait("scalar", stG.ctr, fgv)
    f_tm = g1

    for rc in range(NRT):
        def f_ls(e, f=f_tm[:, rc*2048:(rc+1)*2048], w=wrt, scr=g4[:, 0:2048],
                 out=rw_c[:, rc:rc+1]):
            e.tensor_mul(scr, f, w)
            e.drain()
            e.reduce_sum(out, scr, axis=AX.X)
            e.drain()
        prog.add("vector", f_ls)
    lsv = pub(prog, s, "vector")
    prog.wait("scalar", s["svec"], lsv)
    def f_rwf(e, rw=rw_c, b2=b2col):
        e.activation(rw, rw, AF.Exp, bias=b2)
    prog.add("scalar", f_rwf)
    rwf = pub(prog, s, "scalar")

    prog.wait("sync", s["svec"], s["svec"].v)
    g1l = stW.group("sync", [(g2[:, 0:D], din["a_g1_rep"])])
    g1rep = g2[:, 0:D]
    prog.wait("vector", stW.ctr, g1l)
    for rc in range(NRT):
        lnv = ln_tile(ctx, f_tm[:, rc*2048:(rc+1)*2048], g0[:, rc*2048:(rc+1)*2048],
                      g4[:, 0:D], col4, eps=1e-6, gain_rep=g1rep, rms=True)
    h1v = lnv
    prog.wait("tensor", s["svec"], h1v)
    for rc in range(NRT):
        for half in range(2):
            srcs = [g0[:, rc*2048 + kc*128: rc*2048 + (kc+1)*128] for kc in range(half*8, half*8+8)]
            dsts = [g2[:, (half*8+i)*512 + rc*128: (half*8+i)*512 + (rc+1)*128] for i in range(8)]
            cv = transpose_block(ctx, srcs, dsts, ps1[:, 0:1024])
            prog.wait("tensor", s["svec"], cv)
    h1fm = lambda dc: g2[:, dc*512:(dc+1)*512]

    prog.wait("sync", s["svec"], s["svec"].v)
    ra_l = stW.group("sync", [(g3[:, 2048:2560], din["a_ropeA"]),
                              (g3[:, 2560:3072], din["a_ropeB"])])
    ropeA = g3[:, 2048:2560]; ropeB = g3[:, 2560:3072]
    prog.wait("vector", stW.ctr, ra_l)

    def rope_evac_inner(ps_ap, dst_ap, rt):
        t_all = g3[:, 3072:3584]; t_cr = g3[:, 3584:4096]
        def f_r(e, ps=ps_ap, dst=dst_ap, ta=t_all, tc=t_cr,
                tA=ropeA[:, rt*128:(rt+1)*128], tB=ropeB[:, rt*128:(rt+1)*128]):
            v3 = lambda ap: ap.rearrange("p (h f) -> p h f", f=128)
            e.tensor_tensor(v3(ta), v3(ps), tA.unsqueeze(1).broadcast_to([128, 4, 128]), OP.mult)
            e.tensor_tensor(v3(tc), v3(ps), tB.unsqueeze(1).broadcast_to([128, 4, 128]), OP.mult)
            e.drain()
            ta3 = v3(ta); tc3 = v3(tc); d3 = v3(dst)
            e.tensor_tensor(d3[:, :, 0:64], ta3[:, :, 0:64], ta3[:, :, 64:128], OP.subtract)
            e.tensor_tensor(d3[:, :, 64:128], tc3[:, :, 0:64], tc3[:, :, 64:128], OP.add)
            e.drain()
        prog.add("vector", f_r)

    for fg in range(4):
        for dc in range(NDC):
            sl = rotA.i % 4
            wslot = wbuf[:, sl*512:sl*512+512]
            wc, wv = rotA.load("sync", [(wslot, din["a_wk"][dc*128:(dc+1)*128, fg*512:(fg+1)*512])])
            prog.wait("tensor", wc, wv)
            fns = []
            for rt in range(NRT):
                def f_mm(e, d=ps0[:, rt*512:(rt+1)*512], l=h1fm(dc)[:, rt*128:(rt+1)*128],
                         r=wslot, st_=(dc == 0), sp=(dc == NDC-1)):
                    return e.matmul(d, l, r, start=st_, stop=sp)
                fns.append(f_mm)
            mmv = tensor_group(prog, s, fns)
            rotA.consumed(s["smm"], mmv)
        prog.wait("vector", s["smm"], mmv)
        for rt in range(NRT):
            rope_evac_inner(ps0[:, rt*512:(rt+1)*512], g4[:, rt*512:(rt+1)*512], rt)
        kev = pub(prog, s, "vector")
        prog.wait("tensor", s["svec"], kev)
        for rt in range(NRT):
            srcs = [g4[:, rt*512 + i*128: rt*512 + (i+1)*128] for i in range(4)]
            dsts = [g4[:, 2048 + i*512 + rt*128: 2048 + i*512 + (rt+1)*128] for i in range(4)]
            cv = transpose_block(ctx, srcs, dsts, ps1[:, 0:512])
            prog.wait("tensor", s["svec"], cv)
        prog.wait("gpsimd", s["svec"], cv)
        stG.group("gpsimd", [(dint["kmy"][:, (fg*4)*512:(fg*4+4)*512],
                              g4[:, 2048:4096])])
        prog.wait("vector", stG.ctr, stG.ctr.v)
        for dc in range(NDC):
            sl = rotA.i % 4
            wslot = wbuf[:, sl*512:sl*512+512]
            wc, wv = rotA.load("sync", [(wslot, din["a_wv"][dc*128:(dc+1)*128, fg*512:(fg+1)*512])])
            prog.wait("tensor", wc, wv)
            fns = []
            for rt in range(NRT):
                def f_mm(e, d=ps0[:, rt*512:(rt+1)*512], l=h1fm(dc)[:, rt*128:(rt+1)*128],
                         r=wslot, st_=(dc == 0), sp=(dc == NDC-1)):
                    return e.matmul(d, l, r, start=st_, stop=sp)
                fns.append(f_mm)
            mmv = tensor_group(prog, s, fns)
            rotA.consumed(s["smm"], mmv)
        prog.wait("vector", s["smm"], mmv)
        for rt in range(NRT):
            def f_cp(e, d=g4[:, rt*512:(rt+1)*512], ss=ps0[:, rt*512:(rt+1)*512]):
                e.tensor_copy(d, ss)
            prog.add("vector", f_cp)
        vev = pub(prog, s, "vector")
        prog.wait("tensor", s["svec"], vev)
        prog.wait("gpsimd", s["svec"], vev)
        stG.group("gpsimd", [(dint["vmy"].rearrange("(rt p) f -> p rt f", p=128)[:, :, fg*512:(fg+1)*512],
                              g4[:, 0:2048].rearrange("p (rt f) -> p rt f", rt=NRT))])
        prog.wait("vector", stG.ctr, stG.ctr.v)

    prog.wait("vector", s["smm"], s["smm"].v)
    for fg in range(4):
        for dc in range(NDC):
            sl = rotA.i % 4
            wslot = wbuf[:, sl*512:sl*512+512]
            wc, wv = rotA.load("sync", [(wslot, din["a_wq"][dc*128:(dc+1)*128, fg*512:(fg+1)*512])])
            prog.wait("tensor", wc, wv)
            fns = []
            for rt in range(NRT):
                def f_mm(e, d=ps0[:, rt*512:(rt+1)*512], l=h1fm(dc)[:, rt*128:(rt+1)*128],
                         r=wslot, st_=(dc == 0), sp=(dc == NDC-1)):
                    return e.matmul(d, l, r, start=st_, stop=sp)
                fns.append(f_mm)
            mmv = tensor_group(prog, s, fns)
            rotA.consumed(s["smm"], mmv)
        prog.wait("vector", s["smm"], mmv)
        for rt in range(NRT):
            rope_evac_inner(ps0[:, rt*512:(rt+1)*512],
                            g0[:, rt*2048 + fg*512: rt*2048 + (fg+1)*512], rt)
        qev = pub(prog, s, "vector")
        prog.wait("tensor", s["svec"], qev)
    for rc in range(NRT):
        for half in range(2):
            srcs = [g0[:, rc*2048 + kc*128: rc*2048 + (kc+1)*128] for kc in range(half*8, half*8+8)]
            dsts = [g2[:, (half*8+i)*512 + rc*128: (half*8+i)*512 + (rc+1)*128] for i in range(8)]
            cv = transpose_block(ctx, srcs, dsts, ps1[:, 0:1024])
            prog.wait("tensor", s["svec"], cv)
    qfm = lambda hh: g2[:, hh*512:(hh+1)*512]

    prog.wait("gpsimd", stG.ctr, stG.ctr.v)
    csem3 = ctx["new_coll_sem"]()
    def f_coll3(e, h=csem3.h):
        e.collective_compute("AllGather", OP.bypass, replica_groups=PAIR_GROUPS,
                             ins=[dint["kmy"]], outs=[dint["k_ag"]]).then_inc(h, 1)
        e.collective_compute("AllGather", OP.bypass, replica_groups=PAIR_GROUPS,
                             ins=[dint["vmy"]], outs=[dint["v_ag"]]).then_inc(h, 1)
    prog.add("gpsimd", f_coll3)
    csem3.bump(2)
    prog.wait("sync", csem3)

    scale = 1.0 / math.sqrt(128.0)
    att = g3[:, 0:4096]
    kvh = g3[:, 2048:4096] if False else None
    for hh in range(NH):
        prog.wait("sync", s["smm"], s["smm"].v)
        kl = stW.group("sync", [
            (g0[:, 0:512], dint["k_ag"][0:128, hh*512:(hh+1)*512]),
            (g0[:, 512:1024], dint["k_ag"][128:256, hh*512:(hh+1)*512]),
            (g0[:, 1024:2048],
             dint["v_ag"].rearrange("(kc p) f -> p kc f", p=128)[:, :, hh*128:(hh+1)*128]),
        ])
        kfmh = g0[:, 0:1024]; vtmh = g0[:, 1024:2048]
        prog.wait("tensor", stW.ctr, kl)
        if hh == 0:
            prog.wait("tensor", s["svec"], s["svec"].v)
        if hh == 0:
            rotA.reset(s["svec"], s["svec"].v)
        nmaxq = lambda qt: sm[:, 256 + qt*2: 256 + qt*2 + 1]
        rsumq = lambda qt: sm[:, 256 + qt*2 + 1: 256 + qt*2 + 2]
        exvs = {}; nmvs = {}
        for qt in range(NRT):
            mc = rotA.ctrs[qt]
            g = rotA.hist.get(qt)
            if g is not None:
                prog.wait("sync", g[0], g[1])
            elif mc.v > 0:
                prog.wait("sync", mc, mc.v)
            def f_ld(e, o=g4[:, qt*1024:(qt+1)*1024],
                     i=din["a_mask_my"][qt*128:(qt+1)*128, :], h2=mc.h):
                e.dma_start(out=o, in_=i).then_inc(h2, 16)
            prog.add("sync", f_ld)
            mc.v += 16
            rotA.i += 1
            pbank = ps0[:, (qt % 2)*1024:(qt % 2)*1024+1024]
            if qt >= 2:
                prog.wait("tensor", s["svec"], exvs[qt-2])
            fns = []
            for kg in range(2):
                def f_mm(e, d=pbank[:, kg*512:(kg+1)*512], l=qfm(hh)[:, qt*128:(qt+1)*128],
                         r=kfmh[:, kg*512:(kg+1)*512]):
                    return e.matmul(d, l, r, start=True, stop=True)
                fns.append(f_mm)
            mmv = tensor_group(prog, s, fns)
            prog.wait("vector", s["smm"], mmv)
            prog.wait("vector", mc, mc.v)
            def f_ev(e, sc=g4[:, qt*1024:(qt+1)*1024], ps=pbank, nm=nmaxq(qt)):
                e.scalar_tensor_tensor(sc, ps, scale, sc, OP.mult, OP.add)
                e.drain()
                e.reduce_max(nm, sc, axis=AX.X, negate=True)
            prog.add("vector", f_ev)
            exvs[qt] = pub(prog, s, "vector")
            prog.wait("scalar", s["svec"], exvs[qt])
            def f_ex(e, sc=g4[:, qt*1024:(qt+1)*1024], nm=nmaxq(qt), rs=rsumq(qt)):
                e.activation(sc, sc, AF.Exp, bias=nm, accum_out=rs)
            prog.add("scalar", f_ex)
            exs = pub(prog, s, "scalar")
            prog.wait("vector", s["ssc"], exs)
            def f_nm(e, sc=g4[:, qt*1024:(qt+1)*1024], rs=rsumq(qt)):
                e.reciprocal(rs, rs)
                e.drain()
                e.tensor_scalar(sc, sc, rs, None, OP.mult, OP.bypass)
                e.drain()
            prog.add("vector", f_nm)
            nmvs[qt] = pub(prog, s, "vector")
        for qt in range(NRT):
            prog.wait("tensor", s["svec"], nmvs[qt])
            srcs = [g4[:, qt*1024 + kc*128: qt*1024 + (kc+1)*128] for kc in range(8)]
            dsts = [att[:, kc*512 + qt*128: kc*512 + (qt+1)*128] for kc in range(8)]
            cv = transpose_block(ctx, srcs, dsts, ps1[:, 0:1024])
            prog.wait("tensor", s["svec"], cv)
            rotA.hist[qt] = (s["svec"], cv)
        fns = []
        for kc in range(8):
            def f_mm(e, d=ps1[:, 1024:1536], l=vtmh[:, kc*128:(kc+1)*128],
                     r=att[:, kc*512:(kc+1)*512], st_=(kc == 0), sp=(kc == 7)):
                return e.matmul(d, l, r, start=st_, stop=sp)
            fns.append(f_mm)
        mmc = tensor_group(prog, s, fns)
        prog.wait("vector", s["smm"], mmc)
        def f_cx(e, d=qfm(hh), ss=ps1[:, 1024:1536]):
            e.tensor_copy(d, ss)
            e.drain()
        prog.add("vector", f_cx)
        cxv = pub(prog, s, "vector")
        prog.wait("tensor", s["svec"], cxv)
        stW.consumed(s["svec"], cxv)

    ctxs = qfm
    g2l = stW.group("sync", [(g4[:, 0:D], din["a_g2_rep"])])
    g2rep = g4[:, 0:D]
    prog.wait("vector", stW.ctr, g2l)
    rotC = ctx["rotC"]
    rotC.reset(s["smm"], s["smm"].v)
    for j in range(4):
        for hc in range(NH):
            sl = rotC.i % 2
            ww_c, ww_v = rotC.load("sync", [(wbuf[:, sl*512:sl*512+512],
                                     din["a_wo"][hc*128:(hc+1)*128, j*512:(j+1)*512])])
            prog.wait("tensor", ww_c, ww_v)
            fns = []
            for rt in range(NRT):
                pst = ps0[:, rt*512:(rt+1)*512]
                def f_mm(e, d=pst, l=ctxs(hc)[:, rt*128:(rt+1)*128],
                         r=wbuf[:, sl*512:sl*512+512], st_=(hc == 0), sp=(hc == NH-1)):
                    return e.matmul(d, l, r, start=st_, stop=sp)
                fns.append(f_mm)
            mmo = tensor_group(prog, s, fns)
            rotC.consumed(s["smm"], mmo)
        prog.wait("vector", s["smm"], mmo)
        for rt in range(NRT):
            def f_x2(e, f=f_tm[:, rt*2048 + j*512: rt*2048 + (j+1)*512],
                     ps=ps0[:, rt*512:(rt+1)*512]):
                e.tensor_add(f, f, ps)
                e.drain()
            prog.add("vector", f_x2)
        x2v = pub(prog, s, "vector")
        prog.wait("tensor", s["svec"], x2v)
    x2 = f_tm

    for rc in range(NRT):
        lnv = ln_tile(ctx, x2[:, rc*2048:(rc+1)*2048], g0[:, rc*2048:(rc+1)*2048],
                      g4[:, D:2*D], col4, eps=1e-6, gain_rep=g2rep, rms=True)
    prog.wait("tensor", s["svec"], lnv)
    for rc in range(NRT):
        for half in range(2):
            srcs = [g0[:, rc*2048 + kc*128: rc*2048 + (kc+1)*128] for kc in range(half*8, half*8+8)]
            dsts = [g2[:, (half*8+i)*512 + rc*128: (half*8+i)*512 + (rc+1)*128] for i in range(8)]
            cv = transpose_block(ctx, srcs, dsts, ps1[:, 0:1024])
            prog.wait("tensor", s["svec"], cv)
    h2fm = lambda dc: g2[:, dc*512:(dc+1)*512]

    rotB = ctx["rotB"]
    rotB.reset(s["smm"], s["smm"].v)
    rotC.reset(s["smm"], s["smm"].v)
    g3A = g3[:, 0:2048]; g3B = g3[:, 2048:4096]
    tv_hist = {}
    for gg in range(8):
        for fi in range(8):
            ffc = gg*8 + fi
            p = fi % 2
            w1c, w1v = rotB.load("sync", [(g3A,
                din["a_w1"].rearrange("(dc p) f -> p dc f", p=128)[:, :, ffc*128:(ffc+1)*128])])
            prog.wait("tensor", w1c, w1v)
            if (gg, fi) != (0, 0):
                pv = tv_hist.get((gg*8 + fi - 2))
                if pv is not None:
                    prog.wait("tensor", s["svec"], pv)
            fns = []
            for dc in range(NDC):
                def f_mm(e, d=ps0[:, p*1024:p*1024+512], l=g3A[:, dc*128:(dc+1)*128],
                         r=h2fm(dc), st_=(dc == 0), sp=(dc == NDC-1)):
                    return e.matmul(d, l, r, start=st_, stop=sp)
                fns.append(f_mm)
            mm1 = tensor_group(prog, s, fns)
            rotB.consumed(s["smm"], mm1, back=0)
            w3c, w3v = rotB.load("sync", [(g3B,
                din["a_w3"].rearrange("(dc p) f -> p dc f", p=128)[:, :, ffc*128:(ffc+1)*128])])
            prog.wait("tensor", w3c, w3v)
            fns = []
            for dc in range(NDC):
                def f_mm(e, d=ps0[:, p*1024+512:p*1024+1024], l=g3B[:, dc*128:(dc+1)*128],
                         r=h2fm(dc), st_=(dc == 0), sp=(dc == NDC-1)):
                    return e.matmul(d, l, r, start=st_, stop=sp)
                fns.append(f_mm)
            mm3 = tensor_group(prog, s, fns)
            rotB.consumed(s["smm"], mm3, back=0)
            prog.wait("scalar", s["smm"], mm3)
            def f_t1(e, d=g0[:, p*512:(p+1)*512], ss=ps0[:, p*1024:p*1024+512]):
                e.activation(d, ss, AF.Silu)
            prog.add("scalar", f_t1)
            t1v = pub(prog, s, "scalar")
            prog.wait("vector", s["ssc"], t1v)
            def f_t(e, d=g4[:, fi*512:(fi+1)*512], t1=g0[:, p*512:(p+1)*512],
                    ps3=ps0[:, p*1024+512:p*1024+1024]):
                e.tensor_mul(d, t1, ps3)
                e.drain()
            prog.add("vector", f_t)
            tv_hist[ffc] = pub(prog, s, "vector")
        prog.wait("tensor", s["svec"], tv_hist[gg*8 + 7])
        for j in range(4):
            for fi in range(8):
                ffc = gg*8 + fi
                w2c, w2v = rotC.load("sync", [(wbuf[:, (fi % 2)*512:(fi % 2)*512+512],
                    din["a_w2"][ffc*128:(ffc+1)*128, j*512:(j+1)*512])])
                prog.wait("tensor", w2c, w2v)
                fns = []
                for rt in range(NRT):
                    def f_mm(e, d=ps1[:, rt*512:(rt+1)*512],
                             l=g4[:, fi*512 + rt*128: fi*512 + (rt+1)*128],
                             r=wbuf[:, (fi % 2)*512:(fi % 2)*512+512],
                             st_=(fi == 0), sp=(fi == 7)):
                        return e.matmul(d, l, r, start=st_, stop=sp)
                    fns.append(f_mm)
                mm2 = tensor_group(prog, s, fns)
                rotC.consumed(s["smm"], mm2, back=0)
            prog.wait("vector", s["smm"], mm2)
            for rt in range(NRT):
                def f_o2(e, x=x2[:, rt*2048 + j*512: rt*2048 + (j+1)*512],
                         ps=ps1[:, rt*512:(rt+1)*512]):
                    e.tensor_add(x, x, ps)
                    e.drain()
                prog.add("vector", f_o2)
            o2v = pub(prog, s, "vector")
            prog.wait("tensor", s["svec"], o2v)

    prog.wait("vector", s["ssc"], rwf)
    for rt in range(NRT):
        def f_xw(e, x=x2[:, rt*2048:(rt+1)*2048], rw=rw_c[:, rt:rt+1],
                 d=g0[:, rt*2048:(rt+1)*2048]):
            e.tensor_scalar(d, x, rw, None, OP.mult, OP.bypass)
            e.drain()
        prog.add("vector", f_xw)
    xwv = pub(prog, s, "vector")
    prog.wait("gpsimd", s["svec"], xwv)
    xwd = stG.group("gpsimd", [(dint["xwmy"].rearrange("(rt p) f -> p rt f", p=128),
                                g0[:].rearrange("p (rt f) -> p rt f", rt=NRT))])
    prog.add("vector", lambda e: (e.memset(g4[0:1, 0:D], 0.0), e.drain()))
    zv = pub(prog, s, "vector")
    prog.wait("gpsimd", s["svec"], zv)
    zd = stG.group("gpsimd", [(dint["xw_pad"][SEL:SEL+1, :], g4[0:1, 0:D])])
    prog.wait("gpsimd", stG.ctr, stG.ctr.v)
    csem4 = ctx["new_coll_sem"]()
    def f_coll4(e, h=csem4.h):
        e.collective_compute("AllGather", OP.bypass, replica_groups=PAIR_GROUPS,
                             ins=[dint["xwmy"]], outs=[dint["xw_pad"][0:SEL, :]]).then_inc(h, 1)
    prog.add("gpsimd", f_coll4)
    csem4.bump()
    prog.wait("gpsimd", csem4)
    for t in range(NT):
        gb = stG.group("gpsimd", pairs=[], indirect=[
            (g4[:, (t % 2)*2048:(t % 2)*2048+2048], None, dint["xw_pad"],
             IndirectOffsetOnAxis(srccols[:, t:t+1], 0), OP.bypass)])
        prog.wait("vector", stG.ctr, gb)
        def f_bg(e, x=x_sb[:, t*D:(t+1)*D], gsl=g4[:, (t % 2)*2048:(t % 2)*2048+2048]):
            e.tensor_add(x, x, gsl)
            e.drain()
        prog.add("vector", f_bg)
        bgv = pub(prog, s, "vector")
        stG.consumed(s["svec"], bgv)


def emit_B(ctx):
    prog, s = ctx["prog"], ctx["sems"]
    nc, din, dint = ctx["nc"], ctx["din"], ctx["dint"]
    x_sb, g0, g1, g2, g3, g4, wbuf, sm, smi = (ctx[k] for k in
        ("x_sb", "g0", "g1", "g2", "g3", "g4", "wbuf", "sm", "smi"))
    ps0, ps1 = ctx["ps0"], ctx["ps1"]
    stW, stG = ctx["stW"], ctx["stG"]
    rotA, rotB = ctx["rotA"], ctx["rotB"]
    NQT = T // 128

    col4 = sm[:, 0:4]
    mu_c = sm[:, 8:16]; rstd_c = sm[:, 16:24]; sg_c = sm[:, 24:32]
    causal = sm[:, 128:256]
    nmax = lambda head: sm[:, 256 + head*2: 256 + head*2 + 1]
    rsum = lambda head: sm[:, 256 + head*2 + 1: 256 + head*2 + 2]
    ropetab = lambda sl: g0[:, sl]
    q_slot = lambda hp: x_sb[:, hp*2048:(hp+1)*2048]
    lnfm = lambda dc: g2[:, dc*512:(dc+1)*512]
    kfm = g3
    vtm = g4

    cl = stW.group("sync", [(causal, din["b_causal"])])
    prog.wait("vector", stW.ctr, cl)

    prog.wait("vector", s["smm"], s["smm"].v)
    prog.wait("vector", s["ssc"], s["ssc"].v)
    prog.wait("gpsimd", s["ssc"], s["ssc"].v)
    for t in range(NT):
        xt = x_sb[:, t*D:(t+1)*D]
        lnv = ln_tile(ctx, xt, g4[:, 0:D], g4[:, D:2*D], col4,
                      save_musig=(mu_c[:, t:t+1], rstd_c[:, t:t+1]))
        prog.wait("gpsimd", s["svec"], lnv)
        stG.group("gpsimd", [(dint["lnmy"][t*128:(t+1)*128, :], g4[:, 0:D])])
        if t < NT - 1:
            prog.wait("scalar", stG.ctr, stG.ctr.v)
            prog.wait("vector", stG.ctr, stG.ctr.v)
    prog.wait("vector", s["ssc"], s["ssc"].v)
    def f_sig(e, r=rstd_c[:, 0:NT], o=sg_c[:, 0:NT]):
        e.reciprocal(o, r)
    prog.add("vector", f_sig)
    lnall = pub(prog, s, "vector")
    csem = ctx["new_coll_sem"]()
    prog.wait("gpsimd", stG.ctr, stG.ctr.v)
    def f_coll(e, h=csem.h):
        e.collective_compute("AllGather", OP.bypass, replica_groups=PAIR_GROUPS,
                             ins=[dint["lnmy"]], outs=[dint["ln_ag"]]).then_inc(h, 1)
    prog.add("gpsimd", f_coll)
    csem.bump()
    prog.wait("sync", csem)

    prog.wait("vector", stG.ctr, stG.ctr.v)
    rt = stW.group("sync", [(g0[:, 0:1024], din["b_ropeA"]),
                            (g0[:, 1024:2048], din["b_ropeB"])])
    prog.wait("vector", stW.ctr, rt)
    qtm = g0[:, 2048:4096]

    def rope_evac(ps_ap, dst_ap, gt, nh):
        t_all = g0[:, 4096:4608]; t_cr = g0[:, 4608:5120]
        def f_r(e, ps=ps_ap, dst=dst_ap, ta=t_all[:, 0:nh*64], tc=t_cr[:, 0:nh*64],
                tA=g0[:, gt*64:(gt+1)*64], tB=g0[:, 1024+gt*64:1024+(gt+1)*64], nh=nh):
            v3 = lambda ap: ap.rearrange("p (h f) -> p h f", f=64)
            e.tensor_tensor(v3(ta), v3(ps), tA.unsqueeze(1).broadcast_to([128, nh, 64]), OP.mult)
            e.tensor_tensor(v3(tc), v3(ps), tB.unsqueeze(1).broadcast_to([128, nh, 64]), OP.mult)
            e.drain()
            ta3 = v3(ta); tc3 = v3(tc); d3 = v3(dst)
            e.tensor_tensor(d3[:, :, 0:32], ta3[:, :, 0:32], ta3[:, :, 32:64], OP.subtract)
            e.tensor_tensor(d3[:, :, 32:64], tc3[:, :, 0:32], tc3[:, :, 32:64], OP.add)
            e.drain()
        prog.add("vector", f_r)

    for gi in range(4):
        for tl in range(4):
            lv = stW.group("sync", [(g1[:, 0:2048],
                                     dint["ln_ag"][gi*512 + tl*128: gi*512 + (tl+1)*128, :])])
            prog.wait("tensor", stW.ctr, lv)
            if gi == 0 and tl == 0:
                prog.wait("tensor", s["svec"], lnall)
            for half in range(2):
                srcs = [g1[:, kc*128:(kc+1)*128] for kc in range(half*8, half*8+8)]
                dsts = [lnfm(kc)[:, tl*128:(tl+1)*128] for kc in range(half*8, half*8+8)]
                cv = transpose_block(ctx, srcs, dsts, ps1[:, 0:1024])
                prog.wait("tensor", s["svec"], cv)
            stW.consumed(s["svec"], cv)
        for fg in range(2):
            for dc in range(NDC):
                sl = rotA.i % 4
                wslot = wbuf[:, sl*512:sl*512+512]
                wc, wv = rotA.load("sync", [(wslot,
                    din["b_Wq_my"][dc*128:(dc+1)*128, fg*512:(fg+1)*512])])
                prog.wait("tensor", wc, wv)
                fns = []
                for tl in range(4):
                    def f_mm(e, d=ps0[:, tl*512:(tl+1)*512], l=lnfm(dc)[:, tl*128:(tl+1)*128],
                             r=wslot, st_=(dc == 0), sp=(dc == NDC-1)):
                        return e.matmul(d, l, r, start=st_, stop=sp)
                    fns.append(f_mm)
                mmv = tensor_group(prog, s, fns)
                rotA.consumed(s["smm"], mmv)
            prog.wait("vector", s["smm"], mmv)
            for tl in range(4):
                rope_evac(ps0[:, tl*512:(tl+1)*512], qtm[:, tl*512:(tl+1)*512],
                          gi*4 + tl, 8)
            qv = pub(prog, s, "vector")
            prog.wait("tensor", s["svec"], qv)
            for tl in range(4):
                srcs = [qtm[:, tl*512 + i*128: tl*512 + (i+1)*128] for i in range(4)]
                dsts = [q_slot(fg*4 + i)[:, (gi*4+tl)*128:(gi*4+tl+1)*128] for i in range(4)]
                cv = transpose_block(ctx, srcs, dsts, ps1[:, 0:512])
                prog.wait("tensor", s["svec"], cv)
        for dc in range(NDC):
            sl = rotA.i % 4
            wslot = wbuf[:, sl*512:sl*512+256]
            wc, wv = rotA.load("sync", [(wslot, din["b_Wk_my"][dc*128:(dc+1)*128, :])])
            prog.wait("tensor", wc, wv)
            fns = []
            for tl in range(4):
                def f_mm(e, d=ps0[:, tl*512:tl*512+256], l=lnfm(dc)[:, tl*128:(tl+1)*128],
                         r=wslot, st_=(dc == 0), sp=(dc == NDC-1)):
                    return e.matmul(d, l, r, start=st_, stop=sp)
                fns.append(f_mm)
            mmv = tensor_group(prog, s, fns)
            rotA.consumed(s["smm"], mmv)
        prog.wait("vector", s["smm"], mmv)
        for tl in range(4):
            rope_evac(ps0[:, tl*512:tl*512+256], qtm[:, tl*256:(tl+1)*256], gi*4 + tl, 4)
        kv_ = pub(prog, s, "vector")
        prog.wait("tensor", s["svec"], kv_)
        for tl in range(4):
            srcs = [qtm[:, tl*256 + i*128: tl*256 + (i+1)*128] for i in range(2)]
            dsts = [kfm[:, i*2048 + (gi*4+tl)*128: i*2048 + (gi*4+tl+1)*128] for i in range(2)]
            cv = transpose_block(ctx, srcs, dsts, ps1[:, 0:256])
            prog.wait("tensor", s["svec"], cv)
        for fh in range(2):
            prog.wait("sync", s["smm"], s["smm"].v)
            wv = stW.group("sync", [(wbuf[:, 0:2048],
                din["b_Wv_my"].rearrange("(dc p) f -> p dc f", p=128)[:, :, fh*128:(fh+1)*128])])
            prog.wait("tensor", stW.ctr, wv)
            for tl in range(4):
                pv = ps1[:, (tl % 2)*1024: (tl % 2)*1024 + 128]
                fns = []
                for dc in range(NDC):
                    def f_mm(e, d=pv, l=lnfm(dc)[:, tl*128:(tl+1)*128],
                             r=wbuf[:, dc*128:(dc+1)*128],
                             st_=(dc == 0), sp=(dc == NDC-1)):
                        return e.matmul(d, l, r, start=st_, stop=sp)
                    fns.append(f_mm)
                mmv = tensor_group(prog, s, fns)
                prog.wait("vector", s["smm"], mmv)
                def f_cp(e, d=vtm[:, (gi*4+tl)*256 + fh*128: (gi*4+tl)*256 + fh*128 + 128], ss=pv):
                    e.tensor_copy(d, ss)
                prog.add("vector", f_cp)
                vv = pub(prog, s, "vector")
                prog.wait("tensor", s["svec"], vv)
            stW.consumed(s["svec"], vv)

    scoreA = g0[:, 0:2048]; scoreB = g0[:, 2048:4096]
    ctx_tm = g0[:, 4096:6144]
    kwork = g0[:, 6144:8192]
    attA = g1[:, 0:2048]; attB = g1[:, 2048:4096]
    lnrl = g1[:, 4096:6144]
    for hp in range(8):
        g = hp // 2
        prog.wait("vector", s["smm"], s["smm"].v)
        def f_kw(e, dst=kwork, src=kfm[:, (g//2)*2048:(g//2)*2048+2048], r0=(g % 2)*64):
            e.tensor_copy(dst[0:64, :], src[r0:r0+64, :])
            e.tensor_copy(dst[64:128, :], src[r0:r0+64, :])
            e.drain()
        prog.add("vector", f_kw)
        kwv = pub(prog, s, "vector")
        prog.wait("tensor", s["svec"], kwv)
        for qt in range(NQT):
            W = (qt + 1) * 128
            nkg = (W + 511) // 512
            fns = []
            for base, pps in ((0, ps0), (64, ps1)):
                for kg in range(nkg):
                    w = min(512, W - kg*512)
                    def f_mm(e, d=pps[:, kg*512:kg*512+w],
                             l=q_slot(hp)[base:base+64, qt*128:(qt+1)*128],
                             r=kwork[base:base+64, kg*512:kg*512+w]):
                        return e.matmul(d, l, r, start=True, stop=True)
                    fns.append(f_mm)
            mmv = tensor_group(prog, s, fns)
            prog.wait("scalar", s["smm"], mmv)
            if W > 128:
                for head, (sc, pps) in enumerate(((scoreA, ps0), (scoreB, ps1))):
                    def f_evs(e, d=sc[:, 0:W-128], ps=pps[:, 0:W-128]):
                        e.activation(d, ps, AF.Copy, scale=0.125)
                    prog.add("scalar", f_evs)
            sce = pub(prog, s, "scalar")
            prog.wait("vector", s["smm"], mmv)
            prog.wait("vector", s["ssc"], sce)
            for head, (sc, pps) in enumerate(((scoreA, ps0), (scoreB, ps1))):
                def f_dg(e, sc=sc[:, W-128:W], ps=pps[:, W-128:W], ca=causal):
                    e.scalar_tensor_tensor(sc, ps, 0.125, ca, OP.mult, OP.add)
                    e.drain()
                prog.add("vector", f_dg)
            def f_mx(e, a=scoreA[:, 0:W], b=scoreB[:, 0:W], na=nmax(0), nb=nmax(1)):
                e.reduce_max(na, a, axis=AX.X, negate=True)
                e.reduce_max(nb, b, axis=AX.X, negate=True)
                e.drain()
            prog.add("vector", f_mx)
            exv = pub(prog, s, "vector")
            prog.wait("tensor", s["svec"], exv)
            prog.wait("tensor", s["ssc"], sce)
            prog.wait("scalar", s["svec"], exv)
            def f_ex(e, a=scoreA[:, 0:W], b=scoreB[:, 0:W], na=nmax(0), nb=nmax(1),
                     sa=rsum(0), sb=rsum(1)):
                e.activation(a, a, AF.Exp, bias=na, accum_out=sa)
                e.activation(b, b, AF.Exp, bias=nb, accum_out=sb)
            prog.add("scalar", f_ex)
            exs = pub(prog, s, "scalar")
            prog.wait("vector", s["ssc"], exs)
            def f_rc(e, sa=rsum(0), sb=rsum(1)):
                e.reciprocal(sa, sa)
                e.reciprocal(sb, sb)
                e.drain()
            prog.add("vector", f_rc)
            rcv = pub(prog, s, "vector")
            prog.wait("tensor", s["svec"], rcv)
            nkc = qt + 1
            cvA = transpose_block(ctx, [scoreA[:, kc*128:(kc+1)*128] for kc in range(nkc)],
                                  [attA[:, kc*128:(kc+1)*128] for kc in range(nkc)],
                                  ps0[:, 0:nkc*128])
            prog.wait("tensor", s["svec"], cvA)
            cvB = transpose_block(ctx, [scoreB[:, kc*128:(kc+1)*128] for kc in range(nkc)],
                                  [attB[:, kc*128:(kc+1)*128] for kc in range(nkc)],
                                  ps0[:, 0:nkc*128])
            prog.wait("tensor", s["svec"], cvB)
            fns = []
            for kc in range(nkc):
                def f_c1(e, d=ps1[0:64, 0:128],
                         l=vtm[:, kc*256 + g*64: kc*256 + g*64 + 64],
                         r=attA[:, kc*128:(kc+1)*128], st_=(kc == 0), sp=(kc == nkc-1)):
                    return e.matmul(d, l, r, start=st_, stop=sp)
                fns.append(f_c1)
            for kc in range(nkc):
                def f_c2(e, d=ps1[0:64, 512:640],
                         l=vtm[:, kc*256 + g*64: kc*256 + g*64 + 64],
                         r=attB[:, kc*128:(kc+1)*128], st_=(kc == 0), sp=(kc == nkc-1)):
                    return e.matmul(d, l, r, start=st_, stop=sp)
                fns.append(f_c2)
            mmc = tensor_group(prog, s, fns)
            prog.wait("vector", s["smm"], mmc)
            def f_cc(e, dA=scoreA[0:64, 0:128], sA=ps1[0:64, 0:128],
                     dB=scoreA[64:128, 0:128], sB=ps1[0:64, 512:640]):
                e.tensor_copy(dA, sA)
                e.tensor_copy(dB, sB)
                e.drain()
            prog.add("vector", f_cc)
            ccv = pub(prog, s, "vector")
            prog.wait("tensor", s["svec"], ccv)
            cvt = transpose_block(ctx, [scoreA[:, 0:128]], [ctx_tm[:, qt*128:(qt+1)*128]],
                                  ps1[:, 1024:1152])
            prog.wait("vector", s["svec"], cvt)
            def f_sc(e, t=ctx_tm[:, qt*128:(qt+1)*128], sa=rsum(0), sb=rsum(1)):
                e.tensor_scalar(t[:, 0:64], t[:, 0:64], sa, None, OP.mult, OP.bypass)
                e.tensor_scalar(t[:, 64:128], t[:, 64:128], sb, None, OP.mult, OP.bypass)
                e.drain()
            prog.add("vector", f_sc)
            scv = pub(prog, s, "vector")
            prog.wait("tensor", s["svec"], scv)

        prog.wait("sync", s["svec"], s["svec"].v)
        rotB.reset(s["svec"], s["svec"].v)
        lv = stW.group("sync", [(lnrl, dint["lnmy"][hp*128:(hp+1)*128, :])])
        prog.wait("vector", stW.ctr, lv)
        for qc in range(NQT):
            sl = rotB.i % 2
            wsl = g1[:, sl*2048:(sl+1)*2048]
            wc, wv = rotB.load("sync", [(wsl, din["b_Wo"][qc*128:(qc+1)*128, :])])
            prog.wait("tensor", wc, wv)
            fns = []
            for j in range(4):
                def f_mm(e, d=ps1[:, j*512:(j+1)*512], l=ctx_tm[:, qc*128:(qc+1)*128],
                         r=wsl[:, j*512:(j+1)*512], st_=(qc == 0), sp=(qc == NQT-1)):
                    return e.matmul(d, l, r, start=st_, stop=sp)
                fns.append(f_mm)
            mmo = tensor_group(prog, s, fns)
            rotB.consumed(s["smm"], mmo)
        prog.wait("vector", s["smm"], mmo)
        def f_xn(e, ps=ps1[:, 0:2048], ln=lnrl, dst=q_slot(hp),
                 sg=sg_c[:, hp:hp+1], mu=mu_c[:, hp:hp+1]):
            e.tensor_scalar(dst, ln, sg, mu, OP.mult, OP.add)
            e.drain()
            e.tensor_add(dst, dst, ps)
            e.drain()
        prog.add("vector", f_xn)
        xnv = pub(prog, s, "vector")
        prog.wait("tensor", s["svec"], xnv)


def transpose_block(ctx, srcs, dsts, ps_ap, width=128):
    prog, s = ctx["prog"], ctx["sems"]
    fns = []
    for i, src in enumerate(srcs):
        def f_t(e, src=src, d=ps_ap[:, i*width:(i+1)*width], idn=ctx["ident"]):
            return e.transpose(d, src, idn[:])
        fns.append(f_t)
    mmv = tensor_group(prog, s, fns)
    prog.wait("vector", s["smm"], mmv)
    n = len(dsts)
    stride = None
    if n > 1 and all(d.tensor is dsts[0].tensor for d in dsts):
        offs = [d.offset for d in dsts]
        st0 = offs[1] - offs[0]
        if st0 > 0 and all(offs[i+1] - offs[i] == st0 for i in range(n-1)):
            stride = st0
    if stride is not None and all(d.partition_size() == 128 for d in dsts):
        import concourse.mybir as _mb
        base = dsts[0]
        t = base.tensor
        dst3 = bass.AP(tensor=t, offset=base.offset,
                       ap=[[t.shape[1], 128], [stride, n], [1, width]])
        src3 = ps_ap[:, 0:n*width].rearrange("p (n w) -> p n w", w=width)
        def f_cp(e, d=dst3, ss=src3):
            e.tensor_copy(d, ss)
        prog.add("vector", f_cp)
    else:
        for i, dst in enumerate(dsts):
            def f_cp(e, d=dst, ss=ps_ap[:, i*width:(i+1)*width]):
                e.tensor_copy(d, ss)
            prog.add("vector", f_cp)
    return pub(prog, s, "vector")


def emit_D(ctx):
    prog, s = ctx["prog"], ctx["sems"]
    nc, din, dint = ctx["nc"], ctx["din"], ctx["dint"]
    x_sb, g0, g1, g2, g3, g4, wbuf, sm, smi = (ctx[k] for k in
        ("x_sb", "g0", "g1", "g2", "g3", "g4", "wbuf", "sm", "smi"))
    ps0, ps1 = ctx["ps0"], ctx["ps1"]
    stW, stG = ctx["stW"], ctx["stG"]
    H = 16
    NC = NT

    xnfm = lambda dc: (g0 if dc < 8 else g1)[:, (dc % 8)*1024:(dc % 8)*1024+1024]
    S_all = lambda h: g3[:, h*128:(h+1)*128]
    S_recv = lambda h: g3[:, 2048 + h*128:2048 + (h+1)*128]
    q_fm = g4[:, 0:1024]; k_fm = g4[:, 1024:2048]
    v_tm = g4[:, 2048:3072]; k_tm = g4[:, 3072:4096]
    Dt = sm[:, 0:128]; qdrep = sm[:, 128:256]
    ld_col = lambda h: sm[:, 256+h:256+h+1]
    kdec = sm[:, 280:281]; cdec = sm[:, 282:283]; cdpow = sm[:, 284:285]
    col4 = sm[:, 300:304]
    kdc_c = sm[:, 304:305]; flag_c = sm[:, 306:307]
    diffTm = sm[:, 320:448]; maskT = sm[:, 448:576]; qdr_c = sm[:, 576:704]
    og_all = g2[:, 4096:5120]
    attnT = g2[:, 5120:5248]; qsc = g2[:, 5248:5376]
    ktil = g2[:, 5376:5504]
    t_fm = g2[:, 5504:6528]
    gate_h = g2[:, 6528:7552]
    mscol = sm[:, 704:712]; rstdc = sm[:, 712:720]
    mu_c = sm[:, 720:728]; sg_c = sm[:, 728:736]
    nw_h = sm[:, 736:864]
    onorm_fm = lambda hi: g2[:, hi*1024:(hi+1)*1024]

    cl = stW.group("sync", [
        (sm[:, 256:272], din["d_ld_rep"]),
        (kdc_c, din["d_kdc"]),
        (flag_c, din["d_flag"]),
        (diffTm, din["d_diffTm"]),
        (maskT, din["d_maskT"]),
        (qdr_c, din["d_qdr"]),
        (smi[:, 0:1], din["d_srow"]),
    ])
    prog.wait("vector", stW.ctr, cl)
    prog.wait("scalar", stW.ctr, cl)
    prog.wait("gpsimd", stW.ctr, cl)

    prog.wait("vector", s["smm"], s["smm"].v)
    prog.wait("vector", s["ssc"], s["ssc"].v)
    for t in range(NT):
        xt = x_sb[:, t*D:(t+1)*D]
        lnv = ln_tile(ctx, xt, g4[:, 0:D], g4[:, D:2*D], col4,
                      save_musig=(mu_c[:, t:t+1], rstdc[:, t:t+1]))
        prog.wait("tensor", s["svec"], lnv)
        for half in range(2):
            srcs = [g4[:, kc*128:(kc+1)*128] for kc in range(half*8, half*8+8)]
            dsts = [xnfm(kc)[:, t*128:(t+1)*128] for kc in range(half*8, half*8+8)]
            cv = transpose_block(ctx, srcs, dsts, ps1[:, 0:1024])
            prog.wait("tensor", s["svec"], cv)
    prog.wait("vector", s["ssc"], s["ssc"].v)
    def f_sig(e, r=rstdc[:, 0:NT], o=sg_c[:, 0:NT]):
        e.reciprocal(o, r)
        e.drain()
        e.memset(x_sb[:], 0.0)
    prog.add("vector", f_sig)
    xn_done = pub(prog, s, "vector")
    stW.consumed(s["svec"], xn_done)

    gv = stW.group("sync", [(wbuf[:, 0:2048],
                             din["d_Wg1"].rearrange("(dc p) f -> p dc f", p=128))])
    prog.wait("tensor", stW.ctr, gv)
    prog.wait("tensor", s["svec"], xn_done)
    fns = []
    for tg in range(2):
        for dc in range(NDC):
            def f_mm(e, d=ps0[:, tg*512:(tg+1)*512], l=wbuf[:, dc*128:(dc+1)*128],
                     r=xnfm(dc)[:, tg*512:(tg+1)*512], st_=(dc == 0), sp=(dc == NDC-1)):
                return e.matmul(d, l, r, start=st_, stop=sp)
            fns.append(f_mm)
    mmv = tensor_group(prog, s, fns)
    stW.consumed(s["smm"], mmv)
    prog.wait("vector", s["smm"], mmv)
    prog.add("vector", lambda e: e.tensor_copy(t_fm, ps0[:, 0:1024]))
    tfm_done = pub(prog, s, "vector")
    prog.wait("tensor", s["svec"], tfm_done)

    for h in range(H):
        prog.wait("sync", s["smm"], s["smm"].v)
        wv = stW.group("sync", [(wbuf[:, 0:2048],
            din["d_Wk"].rearrange("(dc p) f -> p dc f", p=128)[:, :, h*128:(h+1)*128])])
        prog.wait("tensor", stW.ctr, wv)
        if h > 0:
            prog.wait("scalar", stG.ctr, stG.ctr.v)
        for tg in range(2):
            fns = []
            for dc in range(NDC):
                def f_mm(e, d=ps0[:, tg*512:(tg+1)*512], l=wbuf[:, dc*128:(dc+1)*128],
                         r=xnfm(dc)[:, tg*512:(tg+1)*512], st_=(dc == 0), sp=(dc == NDC-1)):
                    return e.matmul(d, l, r, start=st_, stop=sp)
                fns.append(f_mm)
            mmv = tensor_group(prog, s, fns)
            prog.wait("scalar", s["smm"], mmv)
            def f_silu(e, d=k_fm[:, tg*512:(tg+1)*512], ss=ps0[:, tg*512:(tg+1)*512]):
                e.activation(d, ss, AF.Silu)
            prog.add("scalar", f_silu)
        kfm_done = pub(prog, s, "scalar")
        stW.consumed(s["ssc"], kfm_done)

        wv = stW.group("sync", [(wbuf[:, 0:2048],
            din["d_Wv"].rearrange("(dc p) f -> p dc f", p=128)[:, :, h*128:(h+1)*128])])
        prog.wait("tensor", stW.ctr, wv)
        prog.wait("tensor", s["ssc"], kfm_done)
        for tg in range(2):
            fns = []
            for dc in range(NDC):
                def f_mm(e, d=ps0[:, tg*512:(tg+1)*512], l=wbuf[:, dc*128:(dc+1)*128],
                         r=xnfm(dc)[:, tg*512:(tg+1)*512], st_=(dc == 0), sp=(dc == NDC-1)):
                    return e.matmul(d, l, r, start=st_, stop=sp)
                fns.append(f_mm)
            mmv = tensor_group(prog, s, fns)
            prog.wait("scalar", s["smm"], mmv)
            def f_silu(e, d=q_fm[:, tg*512:(tg+1)*512], ss=ps0[:, tg*512:(tg+1)*512]):
                e.activation(d, ss, AF.Silu)
            prog.add("scalar", f_silu)
        vfm_done = pub(prog, s, "scalar")
        stW.consumed(s["ssc"], vfm_done)
        prog.wait("tensor", s["ssc"], vfm_done)
        cv = transpose_block(ctx, [q_fm[:, c*128:(c+1)*128] for c in range(8)],
                             [v_tm[:, c*128:(c+1)*128] for c in range(8)], ps1[:, 0:1024])
        cv2 = transpose_block(ctx, [k_fm[:, c*128:(c+1)*128] for c in range(8)],
                              [k_tm[:, c*128:(c+1)*128] for c in range(8)], ps1[:, 1024:2048])
        prog.wait("gpsimd", s["svec"], cv2)
        stG.group("gpsimd", [
            (dint["kvpark"][(2*h)*128:(2*h+1)*128, :], k_fm),
            (dint["kvpark"][(2*h+1)*128:(2*h+2)*128, :], q_fm),
        ])
        def f_kd(e, ldc=ld_col(h), o=kdec, kc=kdc_c):
            e.tensor_scalar(o, kc, ldc, None, OP.mult, OP.bypass)
        prog.add("vector", f_kd)
        vv = pub(prog, s, "vector")
        prog.wait("scalar", s["svec"], vv)
        def f_kd2(e, o=kdec, ldc=ld_col(h), cd=cdec):
            e.activation(o, o, AF.Exp)
            e.activation(cd, ldc, AF.Exp, scale=128.0)
        prog.add("scalar", f_kd2)
        scv = pub(prog, s, "scalar")
        prog.wait("vector", s["ssc"], scv)
        prog.add("vector", lambda e, Sh=S_all(h): (e.memset(Sh, 0.0), e.drain()))
        for c in range(NC):
            def f_kt(e, kt=k_tm[:, c*128:(c+1)*128], kd=kdec, o=ktil):
                e.tensor_scalar(o, kt, kd, None, OP.mult, OP.bypass)
            prog.add("vector", f_kt)
            kv_ = pub(prog, s, "vector")
            prog.wait("tensor", s["svec"], kv_)
            bank = ps1[:, (c % 2)*512:(c % 2)*512+128]
            mmv = tensor_group(prog, s, [
                lambda e, d=bank, l=ktil, r=v_tm[:, c*128:(c+1)*128]:
                e.matmul(d, l, r, start=True, stop=True)])
            prog.wait("vector", s["smm"], mmv)
            def f_su(e, Sh=S_all(h), cd=cdec, ps=bank):
                e.scalar_tensor_tensor(Sh, Sh, cd, ps, OP.mult, OP.add)
            prog.add("vector", f_su)
            sv_ = pub(prog, s, "vector")
            prog.wait("tensor", s["svec"], sv_)

    p1_done = pub(prog, s, "vector")
    prog.wait("gpsimd", s["svec"], p1_done)
    sdump = stG.group("gpsimd", [(dint["s_my"], g3[:, 0:2048])])
    prog.wait("gpsimd", stG.ctr, sdump)
    csem = ctx["new_coll_sem"]()
    def f_coll(e, h=csem.h):
        e.collective_compute("AllGather", OP.bypass, replica_groups=PAIR_GROUPS,
                             ins=[dint["s_my"]], outs=[dint["s_ag"]]).then_inc(h, 1)
    prog.add("gpsimd", f_coll)
    csem.bump()
    prog.wait("gpsimd", csem)
    srv = stG.group("gpsimd", pairs=[], indirect=[
        (g3[:, 2048:4096], None, dint["s_ag"],
         IndirectOffsetOnAxis(smi[:, 0:1], 0), OP.bypass)])
    prog.wait("tensor", stG.ctr, srv)
    prog.wait("vector", stG.ctr, srv)

    prog.wait("sync", stG.ctr, stG.ctr.v)
    for grp in range(4):
        for hi in range(4):
            h = grp*4 + hi
            prog.wait("sync", s["smm"], s["smm"].v)
            kv_l = stW.group("sync", [
                (k_fm, dint["kvpark"][(2*h)*128:(2*h+1)*128, :]),
                (q_fm, dint["kvpark"][(2*h+1)*128:(2*h+2)*128, :]),
                (nw_h, din["d_normw_rep"][:, h*128:(h+1)*128]),
                (wbuf[:, 0:128], din["d_Wg2"][:, h*128:(h+1)*128]),
            ])
            prog.wait("tensor", stW.ctr, kv_l)
            if h == 0:
                prog.wait("tensor", s["svec"], tfm_done)
            fns = []
            for tt in range(NT):
                def f_mm(e, d=ps0[:, (tt % 8)*128:(tt % 8)*128+128],
                         l=t_fm[:, tt*128:(tt+1)*128], r=wbuf[:, 0:128]):
                    return e.matmul(d, l, r, start=True, stop=True)
                fns.append(f_mm)
            mmg = tensor_group(prog, s, fns)
            prog.wait("scalar", s["smm"], mmg)
            def f_sg2(e, d=gate_h, ss=ps0[:, 0:1024]):
                e.activation(d, ss, AF.Sigmoid)
            prog.add("scalar", f_sg2)
            gsv = pub(prog, s, "scalar")
            prog.wait("tensor", s["ssc"], gsv)
            prog.wait("tensor", stW.ctr, kv_l)
            cv = transpose_block(ctx, [q_fm[:, c*128:(c+1)*128] for c in range(8)],
                                 [v_tm[:, c*128:(c+1)*128] for c in range(8)], ps1[:, 0:1024])
            cv2 = transpose_block(ctx, [k_fm[:, c*128:(c+1)*128] for c in range(8)],
                                  [k_tm[:, c*128:(c+1)*128] for c in range(8)], ps1[:, 1024:2048])
            stW.consumed(s["svec"], cv2)
            qw = stW.group("sync", [(wbuf[:, 0:2048],
                din["d_Wq"].rearrange("(dc p) f -> p dc f", p=128)[:, :, h*128:(h+1)*128])])
            prog.wait("tensor", stW.ctr, qw)
            prog.wait("tensor", s["svec"], cv)
            prog.wait("scalar", s["svec"], cv)
            for tg in range(2):
                fns = []
                for dc in range(NDC):
                    def f_mm(e, d=ps0[:, tg*512:(tg+1)*512], l=wbuf[:, dc*128:(dc+1)*128],
                             r=xnfm(dc)[:, tg*512:(tg+1)*512], st_=(dc == 0), sp=(dc == NDC-1)):
                        return e.matmul(d, l, r, start=st_, stop=sp)
                    fns.append(f_mm)
                mmv = tensor_group(prog, s, fns)
                prog.wait("scalar", s["smm"], mmv)
                def f_silu(e, d=q_fm[:, tg*512:(tg+1)*512], ss=ps0[:, tg*512:(tg+1)*512]):
                    e.activation(d, ss, AF.Silu)
                prog.add("scalar", f_silu)
            qfm_done = pub(prog, s, "scalar")
            stW.consumed(s["ssc"], qfm_done)
            prog.wait("vector", s["ssc"], qfm_done)
            def f_dt(e, ldc=ld_col(h), o=Dt, df=diffTm, qr=qdrep, qc=qdr_c, kd=kdec, kc=kdc_c):
                e.tensor_scalar(o, df, ldc, None, OP.mult, OP.bypass)
                e.tensor_scalar(qr, qc, ldc, None, OP.mult, OP.bypass)
                e.tensor_scalar(kd, kc, ldc, None, OP.mult, OP.bypass)
            prog.add("vector", f_dt)
            dv_ = pub(prog, s, "vector")
            prog.wait("scalar", s["svec"], dv_)
            def f_dt2(e, o=Dt, qr=qdrep, kd=kdec, ldc=ld_col(h), cd=cdec):
                e.activation(o, o, AF.Exp)
                e.activation(qr, qr, AF.Exp)
                e.activation(kd, kd, AF.Exp)
                e.activation(cd, ldc, AF.Exp, scale=128.0)
            prog.add("scalar", f_dt2)
            dsv = pub(prog, s, "scalar")
            prog.wait("vector", s["ssc"], dsv)
            def f_dt3(e, o=Dt, mk=maskT, Sh=S_all(h), Sr=S_recv(h), fl=flag_c):
                e.tensor_mul(o, o, mk)
                e.tensor_scalar(Sh, Sr, fl, None, OP.mult, OP.bypass)
                e.drain()
            prog.add("vector", f_dt3)
            dmv = pub(prog, s, "vector")
            prog.wait("tensor", s["svec"], dmv)

            for c in range(NC):
                pS = c % 2
                b_sc = ps0[:, 1024 + pS*512: 1024 + pS*512 + 128]
                b_kv = ps1[:, pS*512: pS*512 + 128]
                b_o = ps1[:, 1024 + pS*512: 1024 + pS*512 + 128]
                mm_sc = tensor_group(prog, s, [
                    lambda e, d=b_sc, l=k_fm[:, c*128:(c+1)*128], r=q_fm[:, c*128:(c+1)*128]:
                    e.matmul(d, l, r, start=True, stop=True)])
                prog.wait("vector", s["smm"], mm_sc)
                def f_at(e, o=attnT, ps=b_sc, dt=Dt, kt=k_tm[:, c*128:(c+1)*128],
                         kd=kdec, kk=ktil, qf=q_fm[:, c*128:(c+1)*128], qd=qdrep, qq=qsc):
                    e.tensor_mul(o, ps, dt)
                    e.tensor_scalar(kk, kt, kd, None, OP.mult, OP.bypass)
                    e.tensor_mul(qq, qf, qd)
                prog.add("vector", f_at)
                at_v = pub(prog, s, "vector")
                prog.wait("tensor", s["svec"], at_v)
                mm_o3 = tensor_group(prog, s, [
                    lambda e, d=b_o, l=attnT, r=v_tm[:, c*128:(c+1)*128]:
                    e.matmul(d, l, r, start=True, stop=False),
                    lambda e, d=b_o, l=qsc, r=S_all(h):
                    e.matmul(d, l, r, start=False, stop=True),
                    lambda e, d=b_kv, l=ktil, r=v_tm[:, c*128:(c+1)*128]:
                    e.matmul(d, l, r, start=True, stop=True),
                ])
                prog.wait("vector", s["smm"], mm_o3)
                def f_su(e, Sh=S_all(h), cd=cdec, ps=b_kv, po=b_o,
                         gs=gate_h[:, c*128:(c+1)*128],
                         og=og_all[:, c*128:(c+1)*128]):
                    e.scalar_tensor_tensor(Sh, Sh, cd, ps, OP.mult, OP.add)
                    e.tensor_mul(og, po, gs)
                prog.add("vector", f_su)
                su_v = pub(prog, s, "vector")
                prog.wait("tensor", s["svec"], su_v)

            prog.wait("scalar", s["svec"], s["svec"].v)
            for c in range(NC):
                def f_sq(e, og=og_all[:, c*128:(c+1)*128], sq=attnT, ms=mscol[:, c:c+1]):
                    e.activation(sq, og, AF.Square, accum_out=ms)
                    e.drain()
                prog.add("scalar", f_sq)
            sqv = pub(prog, s, "scalar")
            prog.wait("vector", s["ssc"], sqv)
            def f_rs(e, ms=mscol[:, 0:8], o=rstdc[:, 0:8]):
                e.tensor_scalar(o, ms, 1.0/128.0, 1e-5, OP.mult, OP.add)
            prog.add("vector", f_rs)
            rsv = pub(prog, s, "vector")
            prog.wait("scalar", s["svec"], rsv)
            def f_rs2(e, o=rstdc[:, 0:8]):
                e.activation(o, o, AF.Ln)
                e.drain()
                e.activation(o, o, AF.Exp, scale=-0.5)
            prog.add("scalar", f_rs2)
            rs2v = pub(prog, s, "scalar")
            prog.wait("vector", s["ssc"], rs2v)
            def f_on(e, og=og_all.rearrange("p (c k) -> p c k", k=128),
                     rs=rstdc[:, 0:8].unsqueeze(2).broadcast_to([128, 8, 128]),
                     nw=nw_h.unsqueeze(1).broadcast_to([128, 8, 128])):
                e.tensor_tensor(og, og, rs, OP.mult)
                e.drain()
                e.tensor_tensor(og, og, nw, OP.mult)
            prog.add("vector", f_on)
            onv = pub(prog, s, "vector")
            prog.wait("tensor", s["svec"], onv)
            cvo = transpose_block(ctx, [og_all[:, c*128:(c+1)*128] for c in range(8)],
                                  [onorm_fm(hi)[:, c*128:(c+1)*128] for c in range(8)],
                                  ps1[:, 0:1024])
            prog.wait("tensor", s["svec"], cvo)

        for j in range(4):
            mmv = None
            for hi in range(4):
                h = grp*4 + hi
                ww = stW.group("sync", [(wbuf[:, hi*512:(hi+1)*512],
                                         din["d_Wo"][h*128:(h+1)*128, j*512:(j+1)*512])])
                prog.wait("tensor", stW.ctr, ww)
                fns = []
                for tt in range(NT):
                    pst = (ps0 if tt < 4 else ps1)[:, (tt % 4)*512:(tt % 4)*512+512]
                    def f_mm(e, d=pst, l=onorm_fm(hi)[:, tt*128:(tt+1)*128],
                             r=wbuf[:, hi*512:(hi+1)*512], st_=(hi == 0), sp=(hi == 3)):
                        return e.matmul(d, l, r, start=st_, stop=sp)
                    fns.append(f_mm)
                mmv = tensor_group(prog, s, fns)
                if hi == 3:
                    stW.consumed(s["smm"], mmv)
            prog.wait("vector", s["smm"], mmv)
            for tt in range(NT):
                pst = (ps0 if tt < 4 else ps1)[:, (tt % 4)*512:(tt % 4)*512+512]
                def f_acc(e, xo=x_sb[:, tt*D + j*512: tt*D + (j+1)*512], ps=pst):
                    e.tensor_add(xo, xo, ps)
                    e.drain()
                prog.add("vector", f_acc)
            av = pub(prog, s, "vector")
            prog.wait("tensor", s["svec"], av)

    for tt in range(NT):
        fns = []
        for dc in range(NDC):
            def f_t(e, src=xnfm(dc)[:, tt*128:(tt+1)*128], d=ps0[:, dc*128:(dc+1)*128],
                    idn=ctx["ident"]):
                return e.transpose(d, src, idn[:])
            fns.append(f_t)
        mmv = tensor_group(prog, s, fns)
        prog.wait("vector", s["smm"], mmv)
        def f_fin(e, ps=ps0[:, 0:2048], xo=x_sb[:, tt*D:(tt+1)*D],
                  sg=sg_c[:, tt:tt+1], mu=mu_c[:, tt:tt+1], tmp=g4[:, 0:2048]):
            e.tensor_scalar(tmp, ps, sg, mu, OP.mult, OP.add)
            e.drain()
            e.tensor_add(xo, xo, tmp)
        prog.add("vector", f_fin)
        fv = pub(prog, s, "vector")
        prog.wait("tensor", s["svec"], fv)



def make_inputs(core, full, sublayers=None):
    if sublayers is None:
        sublayers = SUBLAYERS
    r, h = core // 2, core % 2
    f32 = lambda k: np.ascontiguousarray(np.asarray(full[k], np.float32))
    rep = lambda v: np.ascontiguousarray(np.broadcast_to(
        np.asarray(v, np.float32), (128, len(np.asarray(v).reshape(-1)))))
    X = np.asarray(full["L0_attn_x"], dtype=np.float32)
    m = {
        "x_in": np.ascontiguousarray(X[r, h*TC:(h+1)*TC, :]),
        "ident": np.eye(128, dtype=np.float32),
    }
    if "C" in sublayers or "C2" in sublayers:
        m["c_weight"] = f32("L1_ffn_weight")
        m["c_bias_rep"] = rep(full["L1_ffn_bias"])
    if "A" in sublayers or "A2" in sublayers:
        h = core % 2
        m["a_wq"] = f32("L0_ffn_wq"); m["a_wk"] = f32("L0_ffn_wk")
        m["a_wv"] = f32("L0_ffn_wv"); m["a_wo"] = f32("L0_ffn_wo")
        m["a_w1"] = f32("L0_ffn_w1"); m["a_w3"] = f32("L0_ffn_w3")
        m["a_w2"] = f32("L0_ffn_w2")
        m["a_g1_rep"] = rep(full["L0_ffn_g1"]); m["a_g2_rep"] = rep(full["L0_ffn_g2"])
        m["a_wrt_rep"] = rep(full["L0_ffn_w_router"])
        msk = np.asarray(full["L0_ffn_mask"], np.float32)
        m["a_mask_my"] = np.ascontiguousarray(msk[h*SELC:(h+1)*SELC, :])
        fc = np.asarray(full["L0_ffn_freqs_cis"], np.float32)[h*SELC:(h+1)*SELC]
        cosf, sinf = fc[:, :, 0], fc[:, :, 1]
        tabA = np.concatenate([cosf, sinf], 1).reshape(4, 128, 128)
        tabB = np.concatenate([sinf, cosf], 1).reshape(4, 128, 128)
        m["a_ropeA"] = np.ascontiguousarray(tabA.transpose(1, 0, 2).reshape(128, 512))
        m["a_ropeB"] = np.ascontiguousarray(tabB.transpose(1, 0, 2).reshape(128, 512))
        m["a_iota"] = np.ascontiguousarray(
            np.broadcast_to(np.arange(T, dtype=np.float32)[None, :], (128, T)))
        pp = np.arange(128, dtype=np.float32)
        m["a_idxcol"] = np.ascontiguousarray(
            (np.arange(16, dtype=np.float32)[None, :]*128 + pp[:, None]))
        m["a_rcol"] = np.ascontiguousarray(
            (h*SELC + np.arange(4, dtype=np.float32)[None, :]*128 + pp[:, None]))
        m["a_ones1"] = np.ones((1, 128), np.float32)
        m["a_halfflag"] = np.full((128, 1), float(h), np.float32)
    if "B" in sublayers:
        h0col = (core % 2) * 1024
        m["b_Wq_my"] = np.ascontiguousarray(np.asarray(full["L0_attn_Wq"], np.float32)[:, h0col:h0col+1024])
        g0col = (core % 2) * 256
        m["b_Wk_my"] = np.ascontiguousarray(np.asarray(full["L0_attn_Wk"], np.float32)[:, g0col:g0col+256])
        m["b_Wv_my"] = np.ascontiguousarray(np.asarray(full["L0_attn_Wv"], np.float32)[:, g0col:g0col+256])
        m["b_Wo"] = f32("L0_attn_Wo")
        inv = (1.0 / 10000.0 ** (np.arange(32, dtype=np.float64) * 2.0 / 64.0))
        ang = np.arange(T, dtype=np.float64)[:, None] * inv[None, :]
        cosr = np.cos(ang).astype(np.float32).reshape(16, 128, 32)
        sinr = np.sin(ang).astype(np.float32).reshape(16, 128, 32)
        ra = np.concatenate([cosr, sinr], -1)
        rb = np.concatenate([sinr, cosr], -1)
        m["b_ropeA"] = np.ascontiguousarray(ra.transpose(1, 0, 2).reshape(128, 1024))
        m["b_ropeB"] = np.ascontiguousarray(rb.transpose(1, 0, 2).reshape(128, 1024))
        i_ = np.arange(128)
        m["b_causal"] = np.where(i_[None, :] <= i_[:, None], 0.0, -1e30).astype(np.float32)
    if "D" in sublayers:
        m["d_Wq"] = f32("L1_attn_Wq"); m["d_Wk"] = f32("L1_attn_Wk")
        m["d_Wv"] = f32("L1_attn_Wv"); m["d_Wo"] = f32("L1_attn_Wo")
        m["d_Wg1"] = f32("L1_attn_Wg1"); m["d_Wg2"] = f32("L1_attn_Wg2")
        m["d_normw_rep"] = rep(full["L1_attn_norm_w"])
        m["d_ld_rep"] = rep(full["L1_attn_log_decay"])
        i_ = np.arange(128, dtype=np.float32)
        cmi = i_[None, :] - i_[:, None]
        m["d_diffTm"] = np.where(cmi >= 0, cmi, 0.0).astype(np.float32)
        m["d_maskT"] = (cmi >= 0).astype(np.float32)
        m["d_kdc"] = np.ascontiguousarray((127.0 - i_)[:, None].astype(np.float32))
        m["d_qdr"] = np.ascontiguousarray(
            np.broadcast_to((i_ + 1.0)[None, :], (128, 128)).astype(np.float32))
        m["d_flag"] = np.full((128, 1), float(h), dtype=np.float32)
        m["d_srow"] = np.ascontiguousarray(
            ((1 - h) * 128 + np.arange(128, dtype=np.int32))[:, None])
    return m


def kernel(**inputs):
    nc = build_nc()
    in_maps = [make_inputs(c, inputs, SUBLAYERS) for c in range(8)]
    res = run_bass_kernel_spmd(nc, in_maps, core_ids=list(range(8)))
    out = np.empty((B, T, D), dtype=np.float32)
    for c in range(8):
        r, h = c // 2, c % 2
        out[r, h*TC:(h+1)*TC, :] = res.results[c]["y"]
    return out
